# version 02
# baseline (speedup 1.0000x reference)
import math
import contextlib
import numpy as np

import concourse.bass as bass
import concourse.mybir as mybir
from concourse.bass_utils import run_bass_kernel_spmd
from concourse.alu_op_type import AluOpType
from concourse.ap import AP

F32 = mybir.dt.float32
BF16 = mybir.dt.bfloat16
AF = mybir.ActivationFunctionType
AX = mybir.AxisListType

D = 2048
T = 2048
TQ = 1024
NDB = 16
NH = 16
HF = 5632
NHB = 44
EPS = 1e-5
QEPS = 1e-8
ISQ = 1.0 / math.sqrt(128.0)
KB = 1024

TILES_A = [0, 3, 4, 7, 8, 11, 12, 15]
TILES_B = [1, 2, 5, 6, 9, 10, 13, 14]

NDMASEM = 8
SEM_ENG = {"pe": "tensor", "se": "scalar", "ve": "vector", "ge": "gpsimd"}
for _q in ("a", "b"):
    for _i in range(NDMASEM):
        SEM_ENG[f"ld{_q}{_i}"] = None
ENGS = ("sync", "gpsimd", "tensor", "scalar", "vector")


class Plan:

    def __init__(self, nc, es):
        self.nc = nc
        self.q = {e: [] for e in ENGS}
        self.sems = {}
        self.cnt = {}
        self.seen = {e: {} for e in ENGS}
        self.dseq = {"a": 0, "b": 0}
        for s in SEM_ENG:
            self.sems[s] = es.enter_context(nc.semaphore(s))
            self.cnt[s] = 0

    def ev(self, sem):
        return (sem, self.cnt[sem])

    def op(self, eng, fn, wait=(), inc=None, inc_by=1):
        waits = []
        flat = []
        for w in wait:
            if isinstance(w, list):
                flat.extend(w)
            else:
                flat.append(w)
        for w in flat:
            if w is None:
                continue
            s, v = w
            if v <= 0:
                continue
            if SEM_ENG[s] == eng:
                continue
            if self.seen[eng].get(s, 0) >= v:
                continue
            self.seen[eng][s] = v
            waits.append((s, v))
        evo = None
        if inc is not None:
            self.cnt[inc] += inc_by
            evo = (inc, self.cnt[inc])
        self.q[eng].append((waits, fn, (inc, inc_by) if inc else None))
        return evo

    def dma(self, eng, out, in_, wait=()):
        qn = "b" if eng == "gpsimd" else "a"
        k = self.dseq[qn] % NDMASEM
        self.dseq[qn] += 1
        sem = f"ld{qn}{k}"
        waits = list(wait) + [(sem, self.cnt[sem])]
        return self.op(eng,
                       lambda E, o=out, i=in_: E.dma_start(out=o, in_=i),
                       waits, inc=sem, inc_by=16)

    def dma_fence(self, eng):
        qn = "b" if eng == "gpsimd" else "a"
        return [(f"ld{qn}{k}", self.cnt[f"ld{qn}{k}"]) for k in range(NDMASEM)]

    def barrier(self):
        return [(s, self.cnt[s]) for s in self.cnt]

    def build(self, blk):
        for eng in ENGS:
            qq = self.q[eng]
            sems = self.sems

            def runner(E, qq=qq):
                for waits, fn, inc in qq:
                    for (s, v) in waits:
                        E.wait_ge(sems[s], v)
                    if fn is None:
                        continue
                    ins = fn(E)
                    if inc is not None:
                        s, k = inc
                        ins.then_inc(sems[s], k)
            getattr(blk, eng)(runner)


def build_program(debug=False):
    nc = bass.Bass("TRN2", target_bir_lowering=False)

    xt = nc.dram_tensor("xt", [D, T], F32, kind="ExternalInput").ap()
    wq = nc.dram_tensor("wq", [D, D], F32, kind="ExternalInput").ap()
    wk = nc.dram_tensor("wk", [D, D], F32, kind="ExternalInput").ap()
    wv = nc.dram_tensor("wv", [D, D], F32, kind="ExternalInput").ap()
    wo = nc.dram_tensor("wo", [D, D], F32, kind="ExternalInput").ap()
    w1 = nc.dram_tensor("w1", [D, HF], F32, kind="ExternalInput").ap()
    w2 = nc.dram_tensor("w2", [D, HF], F32, kind="ExternalInput").ap()
    w3 = nc.dram_tensor("w3", [HF, D], F32, kind="ExternalInput").ap()
    tri = nc.dram_tensor("tri", [128, 128], F32, kind="ExternalInput").ap()
    msel = nc.dram_tensor("msel", [1, 8], F32, kind="ExternalInput").ap()
    out = nc.dram_tensor("out", [D, TQ], F32, kind="ExternalOutput").ap()
    x2spill = nc.dram_tensor("x2spill", [D, TQ], F32, kind="Internal").ap()
    w3qs = nc.dram_tensor("w3qs", [HF, D], BF16, kind="Internal").ap()
    dbg = {}
    if debug == "th":
        dbg["d_th"] = nc.dram_tensor("d_th", [1, 8], F32,
                                     kind="ExternalOutput").ap()
    elif debug == "late":
        for name, shape, dt in [
            ("d_xn2", [128, NDB * TQ], BF16),
            ("d_hid", [128, NHB * TQ], BF16), ("d_f", [128, NDB * TQ], BF16),
            ("d_th", [1, 8], F32),
        ]:
            dbg[name] = nc.dram_tensor(name, shape, dt, kind="ExternalOutput").ap()
    elif debug:
        for name, shape, dt in [
            ("d_xn1", [D, T], BF16), ("d_k", [128, 4 * T], BF16),
            ("d_v", [128, 16 * 512], BF16), ("d_q", [128, 4 * TQ], BF16),
            ("d_z", [128, NH * TQ], BF16), ("d_ao", [128, NDB * TQ], BF16),
            ("d_x2", [D, TQ], F32), ("d_xn2", [128, NDB * TQ], BF16),
            ("d_hid", [128, NHB * TQ], BF16), ("d_f", [128, NDB * TQ], BF16),
            ("d_th", [1, 8], F32),
        ]:
            dbg[name] = nc.dram_tensor(name, shape, dt, kind="ExternalOutput").ap()

    es = contextlib.ExitStack()
    with es:
        arena = es.enter_context(
            nc.sbuf_tensor("arena", [128, 206 * KB + 512], mybir.dt.uint8))
        ar = arena[:]

        def buf(off_b, dtype, cols, p0=0, p1=128):
            nb = mybir.dt.size(dtype) * cols
            return ar[p0:p1, off_b:off_b + nb].bitcast(dtype)

        C_ONES = buf(0, F32, 1)
        C_TH = [buf(4 + 4 * i, F32, 1) for i in range(6)]
        C_GAC = [buf(28 + 4 * i, F32, 1) for i in range(3)]
        C_GRD = buf(40, F32, 1)
        C_GRD2 = buf(44, F32, 1)
        C_ONESR = buf(64, F32, 128, 0, 1)
        C_ONESRB = buf(576, BF16, 128, 0, 1)
        C_MSELR = buf(832, F32, 8, 0, 1)
        C_MSELB = buf(864, F32, 8)
        C_ONESB = buf(896, BF16, 1)
        C_EPS = buf(900, F32, 1)
        C_TOT = [buf(904 + 4 * i, F32, 1, 0, 1) for i in range(3)]
        C_THR = [buf(916 + 4 * i, F32, 1, 0, 1) for i in range(3)]
        C_THNR = [buf(928 + 4 * i, F32, 1, 0, 1) for i in range(3)]
        TRI = buf(1536, F32, 128)
        RSUM = buf(2 * KB, F32, 1024, 0, 1)
        RREC = buf(6 * KB, F32, 1024, 0, 1)

        def mkrows(base_b, tmax):
            def rowf(slot, cols):
                return buf(base_b + slot * tmax * 4, F32, cols, 0, 1)
            return rowf

        MS = 10 * KB
        PBo = 16 * KB
        BC = 22 * KB
        WS = 30 * KB
        WS2 = 38 * KB
        B0 = 46 * KB
        GW = 198 * KB
        GDUMP = 202 * KB
        GACALL = 204 * KB

        nc.const_aps.register(F32, EPS, C_EPS[:, 0:1])

        ps = [es.enter_context(nc.psum_tensor(f"ps{i}", [128, 512], F32))
              for i in range(8)]
        P = Plan(nc, es)
        bank_free = {b: None for b in range(8)}

        P.op("vector", lambda E: E.memset(C_ONES[:], 1.0))
        P.op("vector", lambda E: E.memset(C_ONESR[:], 1.0))
        P.op("vector", lambda E: E.memset(C_ONESRB[:], 1.0))
        P.op("vector", lambda E: E.memset(C_ONESB[:], 1.0))
        P.op("vector", lambda E: E.memset(C_EPS[:], EPS))
        for g_ in C_GAC:
            P.op("vector", lambda E, g=g_: E.memset(g[:], 0.0))
        ev_init_v = P.op("vector", lambda E: E.engine_nop(), inc="ve")
        ev_tri = P.dma("sync", TRI[:], tri)
        ev_msel = P.dma("sync", C_MSELR[:], msel)
        em = P.op("tensor", lambda E: E.matmul(ps[7][0:128, 0:8], C_ONESR[:],
                                               C_MSELR[:], start=True, stop=True),
                  wait=[ev_msel, ev_init_v], inc="pe")
        ev_mselb = P.op("vector",
                        lambda E: E.tensor_copy(C_MSELB[:], ps[7][0:128, 0:8]),
                        wait=[em], inc="ve")
        bank_free[7] = ev_mselb
        init_bar = P.barrier()

        def ln_rows(Tn, s1_banks, s2_banks, rowf, wait=()):
            off = 0
            for bk, wdt in s1_banks:
                e = P.op("vector", lambda E, b=bk, o=off, w=wdt:
                         E.tensor_copy(rowf(0, o + w)[:, o:o + w],
                                       ps[b][0:1, 0:w]),
                         wait=wait, inc="ve")
                bank_free[bk] = e
                off += wdt
            off = 0
            for bk, wdt in s2_banks:
                e = P.op("vector", lambda E, b=bk, o=off, w=wdt:
                         E.tensor_copy(rowf(1, o + w)[:, o:o + w],
                                       ps[b][0:1, 0:w]),
                         wait=wait, inc="ve")
                bank_free[bk] = e
                off += wdt
            P.op("vector", lambda E, Tn=Tn: E.tensor_scalar(
                rowf(0, Tn), rowf(0, Tn), 1.0 / D, None, AluOpType.mult))
            P.op("vector", lambda E, Tn=Tn: E.tensor_scalar(
                rowf(1, Tn), rowf(1, Tn), 1.0 / D, None, AluOpType.mult))
            P.op("vector", lambda E, Tn=Tn: E.tensor_tensor(
                rowf(2, Tn), rowf(0, Tn), rowf(0, Tn), AluOpType.mult))
            ev = P.op("vector", lambda E, Tn=Tn: E.tensor_tensor(
                rowf(1, Tn), rowf(1, Tn), rowf(2, Tn), AluOpType.subtract),
                inc="ve")
            P.op("scalar", lambda E, Tn=Tn: E.activation(
                rowf(2, Tn), rowf(1, Tn), AF.Ln, bias=EPS), wait=[ev])
            er = P.op("scalar", lambda E, Tn=Tn: E.activation(
                rowf(2, Tn), rowf(2, Tn), AF.Exp, scale=-0.5), inc="se")
            eb = P.op("vector", lambda E, Tn=Tn: E.tensor_tensor(
                rowf(1, Tn), rowf(0, Tn), rowf(2, Tn), AluOpType.mult),
                wait=[er], inc="ve")
            return er, eb

        def bcast_rows(Tn, ev_a, ev_b, rowf, banks=(0, 1)):
            BCA = buf(BC, BF16, Tn)
            BCB = buf(BC + 4 * KB, BF16, Tn)
            ra = buf(WS2, BF16, Tn, 0, 1)
            rb = buf(WS2 + 2 * Tn, BF16, Tn, 0, 1)
            e1 = P.op("vector", lambda E, Tn=Tn: E.tensor_copy(ra[:], rowf(2, Tn)),
                      wait=[ev_a, ev_b], inc="ve")
            e2 = P.op("vector", lambda E, Tn=Tn: E.tensor_copy(rb[:], rowf(1, Tn)),
                      inc="ve")
            evl = None
            for c0 in range(0, Tn, 512):
                w = min(512, Tn - c0)
                bk = banks[(c0 // 512) % len(banks)]
                ea = P.op("tensor", lambda E, b=bk, c=c0, w=w:
                          E.matmul(ps[b][0:128, 0:w], C_ONESRB[:],
                                   ra[:, c:c + w], start=True, stop=True),
                          wait=[e1, bank_free[bk]], inc="pe")
                ca = P.op("vector", lambda E, b=bk, c=c0, w=w:
                          E.tensor_copy(BCA[:, c:c + w], ps[b][0:128, 0:w]),
                          wait=[ea], inc="ve")
                eb = P.op("tensor", lambda E, b=bk, c=c0, w=w:
                          E.matmul(ps[b][0:128, 0:w], C_ONESRB[:],
                                   rb[:, c:c + w], start=True, stop=True),
                          wait=[e2, ca], inc="pe")
                evl = P.op("vector", lambda E, b=bk, c=c0, w=w:
                           E.tensor_copy(BCB[:, c:c + w], ps[b][0:128, 0:w]),
                           wait=[eb], inc="ve")
                bank_free[bk] = evl
            return BCA, BCB, evl

        GACC = buf(GACALL, F32, 528)
        gw_par = [0]
        gw_war = [None, None]
        gamma_jobs = []
        for widx, w_ in ((0, w1), (1, w2)):
            for db in range(NDB):
                for hp in range(11):
                    gamma_jobs.append((widx, w_[db * 128:(db + 1) * 128,
                                                hp * 512:(hp + 1) * 512]))
        for hb in range(NHB):
            for c in range(4):
                gamma_jobs.append((2, w3[hb * 128:(hb + 1) * 128,
                                         c * 512:(c + 1) * 512]))
        gamma_pos = [0]
        GDMP = buf(GDUMP, BF16, 512)

        def pump_gamma(n):
            for _ in range(n):
                i = gamma_pos[0]
                if i >= len(gamma_jobs):
                    return
                widx, src = gamma_jobs[i]
                gamma_pos[0] += 1
                par = gw_par[0] % 2
                gw_par[0] += 1
                gt = buf(GW + par * 2 * KB, F32, 512)
                ed = P.dma("sync", gt[:], src, wait=[gw_war[par]])
                if i % 2 == 0:
                    er = P.op("vector", lambda E, g=gt, i=i: E.tensor_reduce(
                        GACC[:, i:i + 1], g[:], AX.X, AluOpType.add,
                        apply_absolute_value=True), wait=[ed], inc="ve")
                else:
                    er = P.op("scalar", lambda E, g=gt, i=i: E.activation(
                        GDMP[:], g[:], AF.Abs, accum_out=GACC[:, i:i + 1]),
                        wait=[ed], inc="se")
                gw_war[par] = er

        XN1 = buf(B0, BF16, NDB * T)
        XB = [buf(B0 + 64 * KB + i * 4 * KB, BF16, T) for i in range(2)]
        XSQ = buf(MS, BF16, T)
        sq_done = [None, None]
        for db in range(NDB):
            par = db % 2
            ed = P.dma("gpsimd", XB[par][:], xt[db * 128:(db + 1) * 128, :],
                       wait=[sq_done[par]] + (init_bar if db < 2 else []))
            esq = P.op("scalar", lambda E, p=par: E.activation(
                XSQ[:], XB[p][:], AF.Square),
                wait=[ed, sq_done[1 - par]], inc="se")
            last = None
            for c in range(4):
                P.op("tensor", lambda E, p=par, c=c, db=db:
                     E.matmul(ps[c][0:1, 0:512], C_ONESB[:],
                              XB[p][:, c * 512:(c + 1) * 512],
                              start=(db == 0), stop=(db == NDB - 1)),
                     wait=[ed] + (init_bar if db == 0 else []))
                last = P.op("tensor", lambda E, c=c, db=db:
                            E.matmul(ps[4 + c][0:1, 0:512], C_ONESB[:],
                                     XSQ[:, c * 512:(c + 1) * 512],
                                     start=(db == 0), stop=(db == NDB - 1)),
                            wait=[esq], inc="pe" if c == 3 else None)
            sq_done[par] = last
            pump_gamma(2)
        stats_done = P.ev("pe")
        rowsA = mkrows(B0 + 88 * KB, T)
        ea1, eb1 = ln_rows(T, [(c, 512) for c in range(4)],
                           [(4 + c, 512) for c in range(4)], rowsA,
                           wait=[stats_done])
        BCA1, BCB1, ebc1 = bcast_rows(T, ea1, eb1, rowsA)
        XF = [buf(B0 + 72 * KB + i * 8 * KB, F32, T) for i in range(2)]
        T1A = buf(WS, F32, T)
        xf_war = [None, None]
        t1_war = [None]
        xn1_done = None
        for db in range(NDB):
            par = db % 2
            ed = P.dma("sync", XF[par][:], xt[db * 128:(db + 1) * 128, :],
                       wait=[xf_war[par]])
            e1 = P.op("vector", lambda E, p=par: E.tensor_tensor(
                T1A[:], XF[p][:], BCA1[:], AluOpType.mult),
                wait=[ed, ebc1], inc="ve")
            xf_war[par] = e1
            xn1_done = P.op("vector", lambda E, db=db: E.tensor_tensor(
                XN1[:, db * T:(db + 1) * T], T1A[:], BCB1[:],
                AluOpType.subtract), inc="ve")
            pump_gamma(2)
        if debug is True:
            P.dma("sync", dbg["d_xn1"].rearrange("(db p) t -> p db t", p=128),
                  XN1.rearrange("p (db t) -> p db t", db=NDB),
                  wait=[xn1_done])

        KG = buf(B0 + 64 * KB, BF16, 4 * T)
        VG = buf(B0 + 80 * KB, BF16, 16 * 512)
        QG = buf(B0 + 96 * KB, BF16, 4 * TQ)
        ZB = buf(B0 + 104 * KB, BF16, NH * TQ)
        WPQ = [buf(B0 + 136 * KB + i * 4 * KB, BF16, NDB * 128)
               for i in range(4)]
        WPV = [buf(B0 + 136 * KB + i * 8 * KB, BF16, NDB * 256)
               for i in range(2)]
        PT = [buf(PBo + i * 2 * KB, BF16, TQ) for i in range(2)]
        RBB = buf(MS, BF16, TQ)
        RRQ = buf(MS + 4 * KB, BF16, TQ, 0, 1)
        qwar = {0: None, 1: None, 2: None, 3: None}

        def wpanel(dst, wsrc, c0, width, quarters, extra=()):
            src = AP(wsrc.tensor, c0, [[D, 128], [128 * D, NDB], [1, width]])
            waits = [qwar[q] for q in quarters] + list(extra)
            e = P.dma("gpsimd", dst.rearrange("p (db c) -> p db c", db=NDB),
                      src, wait=waits)
            return e

        for g in range(4):
            for ob in range(4):
                par = ob % 2
                ek = wpanel(WPQ[par], wk, (4 * g + ob) * 128, 128, (par,))
                for tc in range(4):
                    bk = tc % 2
                    last = None
                    for db in range(NDB):
                        last = P.op("tensor", lambda E, p=par, b=bk, db=db, tc=tc:
                                    E.matmul(ps[b][0:128, 0:512],
                                             WPQ[p][:, db * 128:(db + 1) * 128],
                                             XN1[:, db * T + tc * 512:
                                                 db * T + tc * 512 + 512],
                                             start=(db == 0), stop=(db == NDB - 1)),
                                    wait=[ek, xn1_done, bank_free[bk]],
                                    inc="pe" if db == NDB - 1 else None)
                    dst = KG[:, ob * T + tc * 512: ob * T + tc * 512 + 512]
                    if tc % 2 == 0:
                        ev = P.op("vector", lambda E, d=dst, b=bk:
                                  E.tensor_copy(d, ps[b][0:128, 0:512]),
                                  wait=[last], inc="ve")
                    else:
                        ev = P.op("scalar", lambda E, d=dst, b=bk:
                                  E.activation(d, ps[b][0:128, 0:512], AF.Copy),
                                  wait=[last], inc="se")
                    bank_free[bk] = ev
                    qwar[par] = ev
            for oc in range(2):
                par = oc % 2
                evw = wpanel(WPV[par], wv, g * 512 + oc * 256, 256,
                             (2 * par, 2 * par + 1))
                for tb in range(16):
                    bk = 2 + tb % 2
                    last = None
                    for db in range(NDB):
                        last = P.op("tensor", lambda E, p=par, b=bk, db=db, tb=tb:
                                    E.matmul(ps[b][0:128, 0:256],
                                             XN1[:, db * T + tb * 128:
                                                 db * T + tb * 128 + 128],
                                             WPV[p][:, db * 256:(db + 1) * 256],
                                             start=(db == 0), stop=(db == NDB - 1)),
                                    wait=[evw, bank_free[bk]],
                                    inc="pe" if db == NDB - 1 else None)
                    dst = VG[:, tb * 512 + oc * 256: tb * 512 + oc * 256 + 256]
                    if tb % 2 == 0:
                        ev = P.op("vector", lambda E, d=dst, b=bk:
                                  E.tensor_copy(d, ps[b][0:128, 0:256]),
                                  wait=[last], inc="ve")
                    else:
                        ev = P.op("scalar", lambda E, d=dst, b=bk:
                                  E.activation(d, ps[b][0:128, 0:256], AF.Copy),
                                  wait=[last], inc="se")
                    bank_free[bk] = ev
                    qwar[2 * par] = ev
                    qwar[2 * par + 1] = ev
            for ob in range(4):
                par = 2 + ob % 2
                eq = wpanel(WPQ[par], wq, (4 * g + ob) * 128, 128, (par,))
                for tc in range(2):
                    bk = 4 + tc
                    last = None
                    for db in range(NDB):
                        last = P.op("tensor", lambda E, p=par, b=bk, db=db, tc=tc:
                                    E.matmul(ps[b][0:128, 0:512],
                                             WPQ[p][:, db * 128:(db + 1) * 128],
                                             XN1[:, db * T + tc * 512:
                                                 db * T + tc * 512 + 512],
                                             start=(db == 0), stop=(db == NDB - 1)),
                                    wait=[eq, bank_free[bk]],
                                    inc="pe" if db == NDB - 1 else None)
                    dst = QG[:, ob * TQ + tc * 512: ob * TQ + tc * 512 + 512]
                    if tc == 0:
                        ev = P.op("vector", lambda E, d=dst, b=bk:
                                  E.tensor_copy(d, ps[b][0:128, 0:512]),
                                  wait=[last], inc="ve")
                    else:
                        ev = P.op("scalar", lambda E, d=dst, b=bk:
                                  E.activation(d, ps[b][0:128, 0:512], AF.Copy),
                                  wait=[last], inc="se")
                    bank_free[bk] = ev
                    qwar[par] = ev
            pump_gamma(8)
            projd = [P.ev("ve"), P.ev("se")]
            if debug is True and g == 0:
                P.dma("sync", dbg["d_k"], KG[:, 0:4 * T], wait=projd)
                P.dma("sync", dbg["d_v"], VG[:, 0:16 * 512], wait=projd)
                P.dma("sync", dbg["d_q"], QG[:, 0:4 * TQ], wait=projd)
                projd = projd + P.dma_fence("sync")
            for h in range(4):
                H = 4 * g + h
                last_z = {0: None, 1: None}
                last_sum = {2: None, 3: None}
                for p in range(16):
                    s = p if p < 8 else p - 8
                    col0 = 128 * s
                    pieces = ([(col0, 512), (512, 1024)] if col0 < 512
                              else [(col0, 1024)])
                    par = p % 2
                    evp = []
                    for pi, (c0, c1) in enumerate(pieces):
                        sb = 4 + 2 * par + pi
                        wd = c1 - c0
                        emm = P.op("tensor", lambda E, b=sb, h=h, p=p, c0=c0, wd=wd:
                                   E.matmul(ps[b][0:128, 0:wd],
                                            KG[:, h * T + p * 128:
                                               h * T + p * 128 + 128],
                                            QG[:, h * TQ + c0: h * TQ + c0 + wd],
                                            start=True, stop=True),
                                   wait=[bank_free[sb]] + projd, inc="pe")
                        ee = P.op("scalar", lambda E, b=sb, pr=par, c0=c0, wd=wd:
                                  E.activation(PT[pr][:, c0:c0 + wd],
                                               ps[b][0:128, 0:wd],
                                               AF.Exp, scale=ISQ),
                                  wait=[emm], inc="se")
                        bank_free[sb] = ee
                        evp.append(ee)
                    if p < 8:
                        emk = P.op("vector", lambda E, pr=par, c0=col0:
                                   E.tensor_tensor(PT[pr][:, c0:c0 + 128],
                                                   PT[pr][:, c0:c0 + 128],
                                                   TRI[:], AluOpType.mult),
                                   wait=[evp[0], ev_tri], inc="ve")
                    else:
                        emk = P.op("vector", lambda E, pr=par, c0=col0, k=p - 8:
                                   E.tensor_scalar(PT[pr][:, c0:c0 + 128],
                                                   PT[pr][:, c0:c0 + 128],
                                                   C_MSELB[:, k:k + 1], None,
                                                   AluOpType.mult),
                                   wait=[evp[0], ev_mselb], inc="ve")
                    for pi, (c0, c1) in enumerate(pieces):
                        wd = c1 - c0
                        zb_ = c0 // 512
                        zo = c0 - 512 * zb_
                        wz = [evp[pi], emk]
                        if p == 0:
                            wz += [bank_free[zb_], bank_free[2 + zb_]]
                        ez = P.op("tensor",
                                  lambda E, b=zb_, zo=zo, wd=wd, p=p, pr=par,
                                  c0=c0, h=h:
                                  E.matmul(ps[b][0:128, zo:zo + wd],
                                           VG[:, p * 512 + h * 128:
                                              p * 512 + h * 128 + 128],
                                           PT[pr][:, c0:c0 + wd],
                                           start=(p == 0), stop=(p == 15)),
                                  wait=wz, inc="pe")
                        last_z[zb_] = ez
                        es_ = P.op("tensor",
                                   lambda E, b=2 + zb_, zo=zo, wd=wd, p=p,
                                   pr=par, c0=c0:
                                   E.matmul(ps[b][0:1, zo:zo + wd], C_ONESB[:],
                                            PT[pr][:, c0:c0 + wd],
                                            start=(p == 0), stop=(p == 15)),
                                   wait=wz, inc="pe")
                        last_sum[2 + zb_] = es_
                er1 = P.op("vector", lambda E: E.tensor_copy(
                    RSUM[:, 0:512], ps[2][0:1, 0:512]),
                    wait=[last_sum[2]], inc="ve")
                er2 = P.op("vector", lambda E: E.tensor_copy(
                    RSUM[:, 512:1024], ps[3][0:1, 0:512]),
                    wait=[last_sum[3]], inc="ve")
                bank_free[2] = er2
                bank_free[3] = er2
                P.op("vector", lambda E: E.reciprocal(RREC[:], RSUM[:]))
                erb = P.op("vector", lambda E: E.tensor_copy(
                    RRQ[:], RREC[:]), inc="ve")
                for c in range(2):
                    sb = 4 + c
                    ebm = P.op("tensor", lambda E, b=sb, c=c:
                               E.matmul(ps[b][0:128, 0:512], C_ONESRB[:],
                                        RRQ[:, c * 512:(c + 1) * 512],
                                        start=True, stop=True),
                               wait=[erb, bank_free[sb]], inc="pe")
                    ecp = P.op("vector", lambda E, b=sb, c=c:
                               E.tensor_copy(RBB[:, c * 512:(c + 1) * 512],
                                             ps[b][0:128, 0:512]),
                               wait=[ebm], inc="ve")
                    bank_free[sb] = ecp
                    ezv = P.op("vector", lambda E, b=c, c=c, H=H:
                               E.tensor_tensor(
                                   ZB[:, H * TQ + c * 512: H * TQ + c * 512 + 512],
                                   ps[b][0:128, 0:512],
                                   RBB[:, c * 512:(c + 1) * 512],
                                   AluOpType.mult),
                               wait=[last_z[c], ecp], inc="ve")
                    bank_free[c] = ezv
                pump_gamma(4)
        zb_done = P.ev("ve")
        if debug is True:
            P.dma("sync", dbg["d_z"], ZB[:, 0:NH * TQ], wait=[zb_done])

        AO = buf(B0, BF16, NDB * TQ)
        X2 = buf(B0 + 32 * KB, F32, NDB * TQ)
        WOP = [buf(B0 + 136 * KB + i * 4 * KB, BF16, NDB * 128)
               for i in range(2)]
        AOSQ = buf(MS, BF16, TQ)
        last_st2 = None
        for ob in range(NDB):
            par = ob % 2
            ew = wpanel(WOP[par], wo, ob * 128, 128, (par,))
            for tc in range(2):
                bk = tc
                last = None
                for db in range(NDB):
                    last = P.op("tensor", lambda E, p=par, b=bk, db=db, tc=tc:
                                E.matmul(ps[b][0:128, 0:512],
                                         WOP[p][:, db * 128:(db + 1) * 128],
                                         ZB[:, db * TQ + tc * 512:
                                            db * TQ + tc * 512 + 512],
                                         start=(db == 0), stop=(db == NDB - 1)),
                                wait=[ew, zb_done, bank_free[bk]],
                                inc="pe" if db == NDB - 1 else None)
                dst = AO[:, ob * TQ + tc * 512: ob * TQ + tc * 512 + 512]
                eev = P.op("scalar", lambda E, d=dst, b=bk: E.activation(
                    d, ps[b][0:128, 0:512], AF.Copy), wait=[last], inc="se")
                bank_free[bk] = eev
                qwar[par] = eev
                esq = P.op("scalar", lambda E, d=dst, tc=tc: E.activation(
                    AOSQ[:, tc * 512:(tc + 1) * 512], d, AF.Square), inc="se")
                P.op("tensor", lambda E, d=dst, b=4 + tc, ob=ob:
                     E.matmul(ps[b][0:1, 0:512], C_ONESB[:], d,
                              start=(ob == 0), stop=(ob == NDB - 1)),
                     wait=[eev] + ([bank_free[4 + tc]] if ob == 0 else []))
                last_st2 = P.op("tensor", lambda E, b=6 + tc, tc=tc, ob=ob:
                                E.matmul(ps[b][0:1, 0:512], C_ONESB[:],
                                         AOSQ[:, tc * 512:(tc + 1) * 512],
                                         start=(ob == 0), stop=(ob == NDB - 1)),
                                wait=[esq] + ([bank_free[6 + tc]] if ob == 0 else []),
                                inc="pe")
            pump_gamma(2)
        if debug is True:
            edbg = P.dma("sync", dbg["d_ao"], AO[:, 0:NDB * TQ],
                         wait=[P.ev("se")])
        rowsC = mkrows(B0 + 96 * KB, TQ)
        ea2, eb2 = ln_rows(TQ, [(4, 512), (5, 512)], [(6, 512), (7, 512)],
                           rowsC, wait=[last_st2])
        BCA2, BCB2, ebc2 = bcast_rows(TQ, ea2, eb2, rowsC)
        XMY = [buf(WS + i * 4 * KB, F32, TQ) for i in range(2)]
        T1C = buf(PBo, F32, TQ)
        T2C = buf(WS2 + 4 * KB, F32, TQ)
        X2B = buf(MS + 2 * KB, BF16, TQ)
        X2SQ = buf(MS + 4 * KB, BF16, TQ)
        xmy_war = [None, None]
        last_st3 = None
        prev_shadow = [None]
        for ob in range(NDB):
            par = ob % 2
            ed = P.dma("sync", XMY[par][:], xt[ob * 128:(ob + 1) * 128, 0:TQ],
                       wait=[xmy_war[par]])
            e1 = P.op("vector", lambda E, ob=ob: E.tensor_tensor(
                T1C[:], AO[:, ob * TQ:(ob + 1) * TQ], BCA2[:], AluOpType.mult),
                wait=[ebc2], inc="ve")
            e2 = P.op("vector", lambda E: E.tensor_tensor(
                T2C[:], T1C[:], BCB2[:], AluOpType.subtract), inc="ve")
            e3 = P.op("vector", lambda E, ob=ob, p=par: E.tensor_tensor(
                X2[:, ob * TQ:(ob + 1) * TQ], T2C[:], XMY[p][:],
                AluOpType.add), wait=[ed], inc="ve")
            xmy_war[par] = e3
            e4 = P.op("vector", lambda E, ob=ob: E.tensor_copy(
                X2B[:], X2[:, ob * TQ:(ob + 1) * TQ]),
                wait=[prev_shadow[0]], inc="ve")
            e5 = P.op("scalar", lambda E: E.activation(X2SQ[:], X2B[:],
                                                       AF.Square),
                      wait=[e4], inc="se")
            for tc in range(2):
                P.op("tensor", lambda E, b=4 + tc, tc=tc, ob=ob:
                     E.matmul(ps[b][0:1, 0:512], C_ONESB[:],
                              X2B[:, tc * 512:(tc + 1) * 512],
                              start=(ob == 0), stop=(ob == NDB - 1)),
                     wait=[e4] + ([bank_free[4 + tc]] if ob == 0 else []))
                last_st3 = P.op("tensor", lambda E, b=6 + tc, tc=tc, ob=ob:
                                E.matmul(ps[b][0:1, 0:512], C_ONESB[:],
                                         X2SQ[:, tc * 512:(tc + 1) * 512],
                                         start=(ob == 0), stop=(ob == NDB - 1)),
                                wait=[e5] + ([bank_free[6 + tc]] if ob == 0 else []),
                                inc="pe")
            prev_shadow[0] = last_st3
            P.dma("sync", x2spill[ob * 128:(ob + 1) * 128, :],
                  X2[:, ob * TQ:(ob + 1) * TQ], wait=[e3])
            pump_gamma(2)
        x2_done = P.ev("ve")
        x2spill_done = P.dma_fence("sync")
        if debug is True:
            P.dma("sync", dbg["d_x2"].rearrange("(db p) t -> p db t", p=128),
                  X2.rearrange("p (db t) -> p db t", db=NDB), wait=[x2_done])
        ea3, eb3 = ln_rows(TQ, [(4, 512), (5, 512)], [(6, 512), (7, 512)],
                           rowsC, wait=[last_st3])
        BCA3, BCB3, ebc3 = bcast_rows(TQ, ea3, eb3, rowsC)
        XN2 = buf(B0, BF16, NDB * TQ)
        dbg_bar = P.dma_fence("sync") if debug is True else []
        xn2_done = None
        for ob in range(NDB):
            e1 = P.op("vector", lambda E, ob=ob: E.tensor_tensor(
                T1C[:], X2[:, ob * TQ:(ob + 1) * TQ], BCA3[:], AluOpType.mult),
                wait=[ebc3] + dbg_bar, inc="ve")
            xn2_done = P.op("vector", lambda E, ob=ob: E.tensor_tensor(
                XN2[:, ob * TQ:(ob + 1) * TQ], T1C[:], BCB3[:],
                AluOpType.subtract), inc="ve")
        if debug in (True, "late"):
            P.dma("sync", dbg["d_xn2"], XN2[:, 0:NDB * TQ], wait=[xn2_done])

        while gamma_pos[0] < len(gamma_jobs):
            pump_gamma(8)
        gamma_done = [P.ev("ve"), P.ev("se")]

        spans = {0: [], 1: [], 2: []}
        for j, (widx, _s) in enumerate(gamma_jobs):
            spans[widx].append(j)
        for i in range(3):
            cols = spans[i]
            c0, c1 = min(cols), max(cols) + 1
            assert cols == list(range(c0, c1))
            n = c1 - c0
            eg = P.op("tensor", lambda E, c0=c0, n=n: E.matmul(
                ps[7][0:1, 0:n], C_ONES[:], GACC[:, c0:c0 + n],
                start=True, stop=True),
                wait=gamma_done + [bank_free[7]], inc="pe")
            ecr = P.op("vector", lambda E, i=i, n=n: E.tensor_reduce(
                C_TOT[i][:], ps[7][0:1, 0:n], AX.X, AluOpType.add),
                wait=[eg], inc="ve")
            bank_free[7] = ecr
            Nw = D * HF
            eth = P.op("gpsimd", lambda E, i=i, N=Nw: E.tensor_scalar(
                C_THR[i][:], C_TOT[i][:], 0.5 / N, 0.5 * QEPS,
                AluOpType.mult, AluOpType.add), wait=[ecr], inc="ge")
            ethn = P.op("gpsimd", lambda E, i=i, N=Nw: E.tensor_scalar(
                C_THNR[i][:], C_TOT[i][:], -0.5 / N, -0.5 * QEPS,
                AluOpType.mult, AluOpType.add), inc="ge")
            eb1 = P.op("tensor", lambda E, i=i: E.matmul(
                ps[7][0:128, 0:1], C_ONESR[:], C_THR[i][:], start=True,
                stop=True), wait=[eth, bank_free[7]], inc="pe")
            ec1 = P.op("vector", lambda E, i=i: E.tensor_copy(
                C_TH[2 * i][:], ps[7][0:128, 0:1]), wait=[eb1], inc="ve")
            bank_free[7] = ec1
            eb2 = P.op("tensor", lambda E, i=i: E.matmul(
                ps[7][0:128, 0:1], C_ONESR[:], C_THNR[i][:], start=True,
                stop=True), wait=[ethn, ec1], inc="pe")
            ec2 = P.op("vector", lambda E, i=i: E.tensor_copy(
                C_TH[2 * i + 1][:], ps[7][0:128, 0:1]), wait=[eb2], inc="ve")
            bank_free[7] = ec2
        th_done = P.op("vector", lambda E: E.engine_nop(), inc="ve")
        if debug:
            for i in range(3):
                P.dma("sync", dbg["d_th"][0:1, i:i + 1], C_TH[2 * i][0:1, 0:1],
                      wait=[th_done])
                P.dma("sync", dbg["d_th"][0:1, 3 + i:4 + i], C_TOT[i][:],
                      wait=[th_done])

        HID = buf(B0 + 32 * KB, BF16, NHB * TQ)
        W1Q = buf(B0 + 120 * KB, BF16, NDB * 512)
        W2Q = buf(B0 + 136 * KB, BF16, NDB * 512)
        WF = [buf(WS + i * 2 * KB, F32, 512) for i in range(4)]
        SG = buf(MS, F32, 512)
        TT1 = buf(MS + 2 * KB, BF16, 512)
        TT2 = buf(MS + 3 * KB, BF16, 512)
        WF3 = [buf(WS2 + i * 2 * KB, F32, 512) for i in range(2)]
        T31 = buf(MS + 4 * KB, BF16, 512)
        T32 = buf(MS + 5 * KB, BF16, 512)
        W3T = buf(PBo, BF16, 512)
        w3_jobs = [(hb, c) for hb in range(NHB) for c in range(4)]
        w3_pos = [0]
        wf3_war = [None, None]
        w3t_war = [None]
        t32_war = [None]

        def pump_w3(n):
            for _ in range(n):
                i = w3_pos[0]
                if i >= len(w3_jobs):
                    return
                hb, c = w3_jobs[i]
                w3_pos[0] += 1
                par = i % 2
                ed = P.dma("sync", WF3[par][:],
                           w3[hb * 128:(hb + 1) * 128, c * 512:(c + 1) * 512],
                           wait=[wf3_war[par], th_done])
                eq1 = P.op("vector", lambda E, p=par: E.tensor_scalar(
                    T31[:], WF3[p][:], C_TH[4][:, 0:1], None, AluOpType.is_ge),
                    wait=[ed, th_done], inc="ve")
                eq2 = P.op("gpsimd", lambda E, p=par: E.tensor_scalar(
                    T32[:], WF3[p][:], C_TH[5][:, 0:1], None, AluOpType.is_le),
                    wait=[ed, th_done, t32_war[0]], inc="ge")
                wf3_war[par] = eq2
                eq3 = P.op("vector", lambda E: E.tensor_tensor(
                    W3T[:], T31[:], T32[:], AluOpType.subtract),
                    wait=[eq1, eq2, w3t_war[0]], inc="ve")
                t32_war[0] = eq3
                ewr = P.dma("sync", w3qs[hb * 128:(hb + 1) * 128,
                                         c * 512:(c + 1) * 512], W3T[:],
                            wait=[eq3])
                w3t_war[0] = ewr

        wf_war = [None] * 4
        tt2_war = [None]
        sg_war = [None]
        prev_hid = [None]
        hid_done = None
        for hp in range(11):
            qevs = []
            for db in range(NDB):
                for wi, (wsrc, WQb, thp, thn) in enumerate(
                        [(w1, W1Q, C_TH[0], C_TH[1]),
                         (w2, W2Q, C_TH[2], C_TH[3])]):
                    par = wi + 2 * (db % 2)
                    ed = P.dma("sync", WF[par][:],
                               wsrc[db * 128:(db + 1) * 128,
                                    hp * 512:(hp + 1) * 512],
                               wait=[wf_war[par], None if hp or db else th_done])
                    e1 = P.op("vector", lambda E, p=par, t=thp: E.tensor_scalar(
                        TT1[:], WF[p][:], t[:, 0:1], None, AluOpType.is_ge),
                        wait=[ed, th_done, prev_hid[0]], inc="ve")
                    e2 = P.op("gpsimd", lambda E, p=par, t=thn: E.tensor_scalar(
                        TT2[:], WF[p][:], t[:, 0:1], None, AluOpType.is_le),
                        wait=[ed, th_done, tt2_war[0]], inc="ge")
                    wf_war[par] = e2
                    e3 = P.op("vector", lambda E, W=WQb, db=db: E.tensor_tensor(
                        W[:, db * 512:(db + 1) * 512], TT1[:], TT2[:],
                        AluOpType.subtract), wait=[e2], inc="ve")
                    tt2_war[0] = e3
                    qevs.append(e3)
                pump_w3(1)
            qdone = qevs[-1]
            for hb in range(4):
                hglob = hp * 4 + hb
                for tc in range(2):
                    b1 = 2 * tc
                    b2 = b1 + 1
                    l1 = l2 = None
                    for db in range(NDB):
                        l1 = P.op("tensor", lambda E, b=b1, db=db, hb=hb, tc=tc:
                                  E.matmul(ps[b][0:128, 0:512],
                                           W1Q[:, db * 512 + hb * 128:
                                               db * 512 + hb * 128 + 128],
                                           XN2[:, db * TQ + tc * 512:
                                               db * TQ + tc * 512 + 512],
                                           start=(db == 0), stop=(db == NDB - 1)),
                                  wait=[qdone, xn2_done, bank_free[b1]],
                                  inc="pe" if db == NDB - 1 else None)
                        l2 = P.op("tensor", lambda E, b=b2, db=db, hb=hb, tc=tc:
                                  E.matmul(ps[b][0:128, 0:512],
                                           W2Q[:, db * 512 + hb * 128:
                                               db * 512 + hb * 128 + 128],
                                           XN2[:, db * TQ + tc * 512:
                                               db * TQ + tc * 512 + 512],
                                           start=(db == 0), stop=(db == NDB - 1)),
                                  wait=[bank_free[b2]],
                                  inc="pe" if db == NDB - 1 else None)
                    eg1 = P.op("scalar", lambda E, b=b1: E.activation(
                        SG[:], ps[b][0:128, 0:512], AF.Sigmoid),
                        wait=[l1, sg_war[0]], inc="se")
                    bank_free[b1] = eg1
                    eh = P.op("vector", lambda E, b=b2, hg=hglob, tc=tc:
                              E.tensor_tensor(
                                  HID[:, hg * TQ + tc * 512:
                                      hg * TQ + tc * 512 + 512],
                                  SG[:], ps[b][0:128, 0:512], AluOpType.mult),
                              wait=[eg1, l2], inc="ve")
                    bank_free[b2] = eh
                    sg_war[0] = eh
                    hid_done = eh
            prev_hid[0] = hid_done
        while w3_pos[0] < len(w3_jobs):
            pump_w3(4)
        w3q_written = P.dma_fence("sync")
        if debug in (True, "late"):
            P.dma("sync", dbg["d_hid"], HID[:, 0:NHB * TQ], wait=[hid_done])

        FB = buf(B0, BF16, NDB * TQ)
        W3P = [buf(B0 + 120 * KB + i * 11 * KB, BF16, NHB * 128)
               for i in range(2)]
        FSQ = buf(MS, BF16, TQ)
        X2S = [buf(B0 + 142 * KB + i * 4 * KB, F32, TQ) for i in range(2)]
        SCR = [buf(B0 + 150 * KB + i * 4 * KB, F32, TQ) for i in range(2)]
        w3p_war = [None, None]
        last_st4 = None
        for ob in range(NDB):
            par = ob % 2
            for qtr in range(4):
                src = AP(w3qs.tensor, ob * 128 + qtr * 11 * 128 * D,
                        [[D, 128], [128 * D, 11], [1, 128]])
                dstp = W3P[par][:, qtr * 11 * 128:(qtr + 1) * 11 * 128]
                ewp = P.dma("sync", dstp.rearrange("p (hb c) -> p hb c", hb=11),
                            src, wait=[w3p_war[par], hid_done, w3q_written])
            ew = ewp
            for tc in range(2):
                bk = tc
                last = None
                for hb in range(NHB):
                    last = P.op("tensor", lambda E, p=par, b=bk, hb=hb, tc=tc:
                                E.matmul(ps[b][0:128, 0:512],
                                         W3P[p][:, hb * 128:(hb + 1) * 128],
                                         HID[:, hb * TQ + tc * 512:
                                             hb * TQ + tc * 512 + 512],
                                         start=(hb == 0), stop=(hb == NHB - 1)),
                                wait=[ew, hid_done, bank_free[bk]],
                                inc="pe" if hb == NHB - 1 else None)
                dst = FB[:, ob * TQ + tc * 512: ob * TQ + tc * 512 + 512]
                eev = P.op("scalar", lambda E, d=dst, b=bk: E.activation(
                    d, ps[b][0:128, 0:512], AF.Copy), wait=[last], inc="se")
                bank_free[bk] = eev
                w3p_war[par] = eev
                esq = P.op("scalar", lambda E, d=dst, tc=tc: E.activation(
                    FSQ[:, tc * 512:(tc + 1) * 512], d, AF.Square), inc="se")
                P.op("tensor", lambda E, d=dst, b=4 + tc, ob=ob:
                     E.matmul(ps[b][0:1, 0:512], C_ONESB[:], d,
                              start=(ob == 0), stop=(ob == NDB - 1)),
                     wait=[eev] + ([bank_free[4 + tc]] if ob == 0 else []))
                last_st4 = P.op("tensor", lambda E, b=6 + tc, tc=tc, ob=ob:
                                E.matmul(ps[b][0:1, 0:512], C_ONESB[:],
                                         FSQ[:, tc * 512:(tc + 1) * 512],
                                         start=(ob == 0), stop=(ob == NDB - 1)),
                                wait=[esq] + ([bank_free[6 + tc]] if ob == 0 else []),
                                inc="pe")
        if debug in (True, "late"):
            P.dma("sync", dbg["d_f"], FB[:, 0:NDB * TQ], wait=[P.ev("se")])
        rowsE = mkrows(B0 + 32 * KB, TQ)
        ea4, eb4 = ln_rows(TQ, [(4, 512), (5, 512)], [(6, 512), (7, 512)],
                           rowsE, wait=[last_st4])
        BCA4, BCB4, ebc4 = bcast_rows(TQ, ea4, eb4, rowsE)
        x2s_war = [None, None]
        scr_war = [None, None]
        out_ev = None
        for ob in range(NDB):
            par = ob % 2
            ed = P.dma("sync", X2S[par][:],
                       x2spill[ob * 128:(ob + 1) * 128, :],
                       wait=[x2s_war[par], hid_done, x2spill_done])
            e1 = P.op("vector", lambda E, ob=ob: E.tensor_tensor(
                T1C[:], FB[:, ob * TQ:(ob + 1) * TQ], BCA4[:], AluOpType.mult),
                wait=[ebc4], inc="ve")
            e2 = P.op("vector", lambda E, p=par: E.tensor_tensor(
                SCR[p][:], T1C[:], BCB4[:], AluOpType.subtract),
                wait=[scr_war[par]], inc="ve")
            e3 = P.op("vector", lambda E, p=par: E.tensor_tensor(
                SCR[p][:], SCR[p][:], X2S[p][:], AluOpType.add),
                wait=[ed], inc="ve")
            x2s_war[par] = e3
            out_ev = P.dma("sync", out[ob * 128:(ob + 1) * 128, :],
                           SCR[par][:], wait=[e3])
            scr_war[par] = out_ev
        P.op("sync", None, wait=P.dma_fence("sync") + P.dma_fence("gpsimd"))

        blk = es.enter_context(nc.Block())
        P.build(blk)
    return nc


_NC_CACHE = {}


def get_nc(debug=False):
    if debug not in _NC_CACHE:
        _NC_CACHE[debug] = build_program(debug)
    return _NC_CACHE[debug]


def host_inputs(L0_attn_x, L0_attn_Wq, L0_attn_Wk, L0_attn_Wv, L0_attn_Wo,
                L0_ffn_w1, L0_ffn_w2, L0_ffn_w3):
    x = np.asarray(L0_attn_x)
    wqT = np.ascontiguousarray(np.asarray(L0_attn_Wq).T)
    wkT = np.ascontiguousarray(np.asarray(L0_attn_Wk).T)
    wvT = np.ascontiguousarray(np.asarray(L0_attn_Wv).T)
    woT = np.ascontiguousarray(np.asarray(L0_attn_Wo).T)
    w1T = np.ascontiguousarray(np.asarray(L0_ffn_w1).T)
    w2T = np.ascontiguousarray(np.asarray(L0_ffn_w2).T)
    w3T = np.ascontiguousarray(np.asarray(L0_ffn_w3).T)
    tri = np.triu(np.ones((128, 128), np.float32))
    msel_h = []
    for mine, peer in ((TILES_A, TILES_B), (TILES_B, TILES_A)):
        m = np.zeros((1, 8), np.float32)
        for k in range(8):
            s_real = sum(1 for i in mine if i < peer[k])
            m[0, k] = 0.0 if s_real > k else 1.0
        msel_h.append(m)
    perms = []
    for mine, peer in ((TILES_A, TILES_B), (TILES_B, TILES_A)):
        cols = []
        for i in mine + peer:
            cols.extend(range(i * 128, (i + 1) * 128))
        perms.append(np.array(cols))
    in_maps = []
    for core in range(8):
        b = core // 2
        half = core % 2
        xtp = np.ascontiguousarray(x[b].T[:, perms[half]])
        in_maps.append({
            "xt": xtp, "wq": wqT, "wk": wkT, "wv": wvT, "wo": woT,
            "w1": w1T, "w2": w2T, "w3": w3T, "tri": tri,
            "msel": msel_h[half],
        })
    return in_maps


def kernel(L0_attn_x, L0_attn_Wq, L0_attn_Wk, L0_attn_Wv, L0_attn_Wo,
           L0_ffn_w1, L0_ffn_w2, L0_ffn_w3):
    in_maps = host_inputs(L0_attn_x, L0_attn_Wq, L0_attn_Wk, L0_attn_Wv,
                          L0_attn_Wo, L0_ffn_w1, L0_ffn_w2, L0_ffn_w3)
    nc = get_nc(False)
    res = run_bass_kernel_spmd(nc, in_maps, core_ids=list(range(8)))
    y = np.empty((4, T, D), np.float32)
    for b in range(4):
        yT = np.empty((D, T), np.float32)
        for half, tiles in ((0, TILES_A), (1, TILES_B)):
            o = np.asarray(res.results[2 * b + half]["out"])
            for a, i in enumerate(tiles):
                yT[:, i * 128:(i + 1) * 128] = o[:, a * 128:(a + 1) * 128]
        y[b] = yT.T
    return y


# version 03
# speedup vs baseline: 1.5957x; 1.5957x over previous
import math
import contextlib
import numpy as np

import concourse.bass as bass
import concourse.mybir as mybir
from concourse.bass_utils import run_bass_kernel_spmd
from concourse.alu_op_type import AluOpType
from concourse.ap import AP

F32 = mybir.dt.float32
BF16 = mybir.dt.bfloat16
AF = mybir.ActivationFunctionType
AX = mybir.AxisListType

D = 2048
T = 2048
TQ = 1024
NDB = 16
NH = 16
HF = 5632
NHB = 44
EPS = 1e-5
QEPS = 1e-8
ISQ = 1.0 / math.sqrt(128.0)
KB = 1024

TILES_A = [0, 3, 4, 7, 8, 11, 12, 15]
TILES_B = [1, 2, 5, 6, 9, 10, 13, 14]

NDMASEM = 8
SEM_ENG = {"pe": "tensor", "se": "scalar", "ve": "vector", "ge": "gpsimd"}
for _q in ("a", "b"):
    for _i in range(NDMASEM):
        SEM_ENG[f"ld{_q}{_i}"] = None
ENGS = ("sync", "gpsimd", "tensor", "scalar", "vector")


class Plan:

    def __init__(self, nc, es):
        self.nc = nc
        self.q = {e: [] for e in ENGS}
        self.sems = {}
        self.cnt = {}
        self.seen = {e: {} for e in ENGS}
        self.dseq = {"a": 0, "b": 0}
        for s in SEM_ENG:
            self.sems[s] = es.enter_context(nc.semaphore(s))
            self.cnt[s] = 0

    def ev(self, sem):
        return (sem, self.cnt[sem])

    def op(self, eng, fn, wait=(), inc=None, inc_by=1):
        waits = []
        flat = []
        for w in wait:
            if isinstance(w, list):
                flat.extend(w)
            else:
                flat.append(w)
        for w in flat:
            if w is None:
                continue
            s, v = w
            if v <= 0:
                continue
            if SEM_ENG[s] == eng:
                continue
            if self.seen[eng].get(s, 0) >= v:
                continue
            self.seen[eng][s] = v
            waits.append((s, v))
        evo = None
        if inc is not None:
            self.cnt[inc] += inc_by
            evo = (inc, self.cnt[inc])
        self.q[eng].append((waits, fn, (inc, inc_by) if inc else None))
        return evo

    def dma(self, eng, out, in_, wait=()):
        qn = "b" if eng == "gpsimd" else "a"
        k = self.dseq[qn] % NDMASEM
        self.dseq[qn] += 1
        sem = f"ld{qn}{k}"
        waits = list(wait) + [(sem, self.cnt[sem])]
        return self.op(eng,
                       lambda E, o=out, i=in_: E.dma_start(out=o, in_=i),
                       waits, inc=sem, inc_by=16)

    def dma_fence(self, eng):
        qn = "b" if eng == "gpsimd" else "a"
        return [(f"ld{qn}{k}", self.cnt[f"ld{qn}{k}"]) for k in range(NDMASEM)]

    def barrier(self):
        return [(s, self.cnt[s]) for s in self.cnt]

    def build(self, blk):
        for eng in ENGS:
            qq = self.q[eng]
            sems = self.sems

            def runner(E, qq=qq):
                for waits, fn, inc in qq:
                    for (s, v) in waits:
                        E.wait_ge(sems[s], v)
                    if fn is None:
                        continue
                    ins = fn(E)
                    if inc is not None:
                        s, k = inc
                        ins.then_inc(sems[s], k)
            getattr(blk, eng)(runner)


def build_program(debug=False):
    nc = bass.Bass("TRN2", target_bir_lowering=False)

    xt = nc.dram_tensor("xt", [D, T], F32, kind="ExternalInput").ap()
    wq = nc.dram_tensor("wq", [D, D], F32, kind="ExternalInput").ap()
    wk = nc.dram_tensor("wk", [D, D], F32, kind="ExternalInput").ap()
    wv = nc.dram_tensor("wv", [D, D], F32, kind="ExternalInput").ap()
    wo = nc.dram_tensor("wo", [D, D], F32, kind="ExternalInput").ap()
    w1 = nc.dram_tensor("w1", [D, HF], F32, kind="ExternalInput").ap()
    w2 = nc.dram_tensor("w2", [D, HF], F32, kind="ExternalInput").ap()
    w3 = nc.dram_tensor("w3", [HF, D], F32, kind="ExternalInput").ap()
    tri = nc.dram_tensor("tri", [128, 128], F32, kind="ExternalInput").ap()
    msel = nc.dram_tensor("msel", [1, 8], F32, kind="ExternalInput").ap()
    out = nc.dram_tensor("out", [D, TQ], F32, kind="ExternalOutput").ap()
    x2spill = nc.dram_tensor("x2spill", [D, TQ], F32, kind="Internal").ap()
    w3qs = nc.dram_tensor("w3qs", [HF, D], BF16, kind="Internal").ap()
    dbg = {}
    if debug == "th":
        dbg["d_th"] = nc.dram_tensor("d_th", [1, 8], F32,
                                     kind="ExternalOutput").ap()
    elif debug == "late":
        for name, shape, dt in [
            ("d_xn2", [128, NDB * TQ], BF16),
            ("d_hid", [128, NHB * TQ], BF16), ("d_f", [128, NDB * TQ], BF16),
            ("d_th", [1, 8], F32),
        ]:
            dbg[name] = nc.dram_tensor(name, shape, dt, kind="ExternalOutput").ap()
    elif debug:
        for name, shape, dt in [
            ("d_xn1", [D, T], BF16), ("d_k", [128, 4 * T], BF16),
            ("d_v", [128, 16 * 512], BF16), ("d_q", [128, 4 * TQ], BF16),
            ("d_z", [128, NH * TQ], BF16), ("d_ao", [128, NDB * TQ], BF16),
            ("d_x2", [D, TQ], F32), ("d_xn2", [128, NDB * TQ], BF16),
            ("d_hid", [128, NHB * TQ], BF16), ("d_f", [128, NDB * TQ], BF16),
            ("d_th", [1, 8], F32),
        ]:
            dbg[name] = nc.dram_tensor(name, shape, dt, kind="ExternalOutput").ap()

    es = contextlib.ExitStack()
    with es:
        arena = es.enter_context(
            nc.sbuf_tensor("arena", [128, 206 * KB + 512], mybir.dt.uint8))
        ar = arena[:]

        def buf(off_b, dtype, cols, p0=0, p1=128):
            nb = mybir.dt.size(dtype) * cols
            return ar[p0:p1, off_b:off_b + nb].bitcast(dtype)

        C_ONES = buf(0, F32, 1)
        C_TH = [buf(4 + 4 * i, F32, 1) for i in range(6)]
        C_GAC = [buf(28 + 4 * i, F32, 1) for i in range(3)]
        C_GRD = buf(40, F32, 1)
        C_GRD2 = buf(44, F32, 1)
        C_ONESR = buf(64, F32, 128, 0, 1)
        C_ONESRB = buf(576, BF16, 128, 0, 1)
        C_MSELR = buf(832, F32, 8, 0, 1)
        C_MSELB = buf(864, F32, 8)
        C_ONESB = buf(896, BF16, 1)
        C_EPS = buf(900, F32, 1)
        C_TOT = [buf(904 + 4 * i, F32, 1, 0, 1) for i in range(3)]
        C_THR = [buf(916 + 4 * i, F32, 1, 0, 1) for i in range(3)]
        C_THNR = [buf(928 + 4 * i, F32, 1, 0, 1) for i in range(3)]
        TRI = buf(1536, F32, 128)
        RSUM = buf(2 * KB, F32, 1024, 0, 1)
        RREC = buf(6 * KB, F32, 1024, 0, 1)

        def mkrows(base_b, tmax):
            def rowf(slot, cols):
                return buf(base_b + slot * tmax * 4, F32, cols, 0, 1)
            return rowf

        MS = 10 * KB
        PBo = 16 * KB
        BC = 22 * KB
        WS = 30 * KB
        WS2 = 38 * KB
        B0 = 46 * KB
        GW = 198 * KB
        GDUMP = 202 * KB
        GACALL = 204 * KB

        nc.const_aps.register(F32, EPS, C_EPS[:, 0:1])

        ps = [es.enter_context(nc.psum_tensor(f"ps{i}", [128, 512], F32))
              for i in range(8)]
        P = Plan(nc, es)
        bank_free = {b: None for b in range(8)}

        P.op("vector", lambda E: E.memset(C_ONES[:], 1.0))
        P.op("vector", lambda E: E.memset(C_ONESR[:], 1.0))
        P.op("vector", lambda E: E.memset(C_ONESRB[:], 1.0))
        P.op("vector", lambda E: E.memset(C_ONESB[:], 1.0))
        P.op("vector", lambda E: E.memset(C_EPS[:], EPS))
        for g_ in C_GAC:
            P.op("vector", lambda E, g=g_: E.memset(g[:], 0.0))
        ev_init_v = P.op("vector", lambda E: E.engine_nop(), inc="ve")
        ev_tri = P.dma("sync", TRI[:], tri)
        ev_msel = P.dma("sync", C_MSELR[:], msel)
        em = P.op("tensor", lambda E: E.matmul(ps[7][0:128, 0:8], C_ONESR[:],
                                               C_MSELR[:], start=True, stop=True),
                  wait=[ev_msel, ev_init_v], inc="pe")
        ev_mselb = P.op("vector",
                        lambda E: E.tensor_copy(C_MSELB[:], ps[7][0:128, 0:8]),
                        wait=[em], inc="ve")
        bank_free[7] = ev_mselb
        init_bar = P.barrier()

        def ln_rows(Tn, s1_banks, s2_banks, rowf, wait=()):
            off = 0
            for bk, wdt in s1_banks:
                e = P.op("vector", lambda E, b=bk, o=off, w=wdt:
                         E.tensor_copy(rowf(0, o + w)[:, o:o + w],
                                       ps[b][0:1, 0:w]),
                         wait=wait, inc="ve")
                bank_free[bk] = e
                off += wdt
            off = 0
            for bk, wdt in s2_banks:
                e = P.op("vector", lambda E, b=bk, o=off, w=wdt:
                         E.tensor_copy(rowf(1, o + w)[:, o:o + w],
                                       ps[b][0:1, 0:w]),
                         wait=wait, inc="ve")
                bank_free[bk] = e
                off += wdt
            P.op("vector", lambda E, Tn=Tn: E.tensor_scalar(
                rowf(0, Tn), rowf(0, Tn), 1.0 / D, None, AluOpType.mult))
            P.op("vector", lambda E, Tn=Tn: E.tensor_scalar(
                rowf(1, Tn), rowf(1, Tn), 1.0 / D, None, AluOpType.mult))
            P.op("vector", lambda E, Tn=Tn: E.tensor_tensor(
                rowf(2, Tn), rowf(0, Tn), rowf(0, Tn), AluOpType.mult))
            ev = P.op("vector", lambda E, Tn=Tn: E.tensor_tensor(
                rowf(1, Tn), rowf(1, Tn), rowf(2, Tn), AluOpType.subtract),
                inc="ve")
            P.op("scalar", lambda E, Tn=Tn: E.activation(
                rowf(2, Tn), rowf(1, Tn), AF.Ln, bias=EPS), wait=[ev])
            er = P.op("scalar", lambda E, Tn=Tn: E.activation(
                rowf(2, Tn), rowf(2, Tn), AF.Exp, scale=-0.5), inc="se")
            eb = P.op("vector", lambda E, Tn=Tn: E.tensor_tensor(
                rowf(1, Tn), rowf(0, Tn), rowf(2, Tn), AluOpType.mult),
                wait=[er], inc="ve")
            return er, eb

        def bcast_rows(Tn, ev_a, ev_b, rowf, banks=(0, 1)):
            BCA = buf(BC, BF16, Tn)
            BCB = buf(BC + 4 * KB, BF16, Tn)
            ra = buf(WS2, BF16, Tn, 0, 1)
            rb = buf(WS2 + 2 * Tn, BF16, Tn, 0, 1)
            e1 = P.op("vector", lambda E, Tn=Tn: E.tensor_copy(ra[:], rowf(2, Tn)),
                      wait=[ev_a, ev_b], inc="ve")
            e2 = P.op("vector", lambda E, Tn=Tn: E.tensor_copy(rb[:], rowf(1, Tn)),
                      inc="ve")
            evl = None
            for c0 in range(0, Tn, 512):
                w = min(512, Tn - c0)
                bk = banks[(c0 // 512) % len(banks)]
                ea = P.op("tensor", lambda E, b=bk, c=c0, w=w:
                          E.matmul(ps[b][0:128, 0:w], C_ONESRB[:],
                                   ra[:, c:c + w], start=True, stop=True),
                          wait=[e1, bank_free[bk]], inc="pe")
                ca = P.op("vector", lambda E, b=bk, c=c0, w=w:
                          E.tensor_copy(BCA[:, c:c + w], ps[b][0:128, 0:w]),
                          wait=[ea], inc="ve")
                eb = P.op("tensor", lambda E, b=bk, c=c0, w=w:
                          E.matmul(ps[b][0:128, 0:w], C_ONESRB[:],
                                   rb[:, c:c + w], start=True, stop=True),
                          wait=[e2, ca], inc="pe")
                evl = P.op("vector", lambda E, b=bk, c=c0, w=w:
                           E.tensor_copy(BCB[:, c:c + w], ps[b][0:128, 0:w]),
                           wait=[eb], inc="ve")
                bank_free[bk] = evl
            return BCA, BCB, evl

        GACC = buf(GACALL, F32, 528)
        gw_par = [0]
        gw_war = [None, None]
        gamma_jobs = []
        for widx, w_ in ((0, w1), (1, w2)):
            for db in range(NDB):
                for hp in range(11):
                    gamma_jobs.append((widx, w_[db * 128:(db + 1) * 128,
                                                hp * 512:(hp + 1) * 512]))
        for hb in range(NHB):
            for c in range(4):
                gamma_jobs.append((2, w3[hb * 128:(hb + 1) * 128,
                                         c * 512:(c + 1) * 512]))
        gamma_pos = [0]
        GDMP = buf(GDUMP, BF16, 512)

        def pump_gamma(n):
            for _ in range(n):
                i = gamma_pos[0]
                if i >= len(gamma_jobs):
                    return
                widx, src = gamma_jobs[i]
                gamma_pos[0] += 1
                par = gw_par[0] % 2
                gw_par[0] += 1
                gt = buf(GW + par * 2 * KB, F32, 512)
                ed = P.dma("sync", gt[:], src, wait=[gw_war[par]])
                if i % 2 == 0:
                    er = P.op("vector", lambda E, g=gt, i=i: E.tensor_reduce(
                        GACC[:, i:i + 1], g[:], AX.X, AluOpType.add,
                        apply_absolute_value=True), wait=[ed], inc="ve")
                else:
                    er = P.op("scalar", lambda E, g=gt, i=i: E.activation(
                        GDMP[:], g[:], AF.Abs, accum_out=GACC[:, i:i + 1]),
                        wait=[ed], inc="se")
                gw_war[par] = er

        XN1 = buf(B0, BF16, NDB * T)
        XB = [buf(B0 + 64 * KB + i * 4 * KB, BF16, T) for i in range(2)]
        XSQ = buf(MS, BF16, T)
        sq_done = [None, None]
        for db in range(NDB):
            par = db % 2
            ed = P.dma("gpsimd", XB[par][:], xt[db * 128:(db + 1) * 128, :],
                       wait=[sq_done[par]] + (init_bar if db < 2 else []))
            esq = P.op("scalar", lambda E, p=par: E.activation(
                XSQ[:], XB[p][:], AF.Square),
                wait=[ed, sq_done[1 - par]], inc="se")
            last = None
            for c in range(4):
                P.op("tensor", lambda E, p=par, c=c, db=db:
                     E.matmul(ps[c][0:1, 0:512], C_ONESB[:],
                              XB[p][:, c * 512:(c + 1) * 512],
                              start=(db == 0), stop=(db == NDB - 1)),
                     wait=[ed] + (init_bar if db == 0 else []))
                last = P.op("tensor", lambda E, c=c, db=db:
                            E.matmul(ps[4 + c][0:1, 0:512], C_ONESB[:],
                                     XSQ[:, c * 512:(c + 1) * 512],
                                     start=(db == 0), stop=(db == NDB - 1)),
                            wait=[esq], inc="pe" if c == 3 else None)
            sq_done[par] = last
            pump_gamma(6)
        stats_done = P.ev("pe")
        rowsA = mkrows(B0 + 88 * KB, T)
        ea1, eb1 = ln_rows(T, [(c, 512) for c in range(4)],
                           [(4 + c, 512) for c in range(4)], rowsA,
                           wait=[stats_done])
        BCA1, BCB1, ebc1 = bcast_rows(T, ea1, eb1, rowsA)
        XF = [buf(B0 + 72 * KB + i * 8 * KB, F32, T) for i in range(2)]
        T1A = buf(WS, F32, T)
        xf_war = [None, None]
        t1_war = [None]
        xn1_done = None
        for db in range(NDB):
            par = db % 2
            ed = P.dma("sync", XF[par][:], xt[db * 128:(db + 1) * 128, :],
                       wait=[xf_war[par]])
            e1 = P.op("vector", lambda E, p=par: E.tensor_tensor(
                T1A[:], XF[p][:], BCA1[:], AluOpType.mult),
                wait=[ed, ebc1], inc="ve")
            xf_war[par] = e1
            xn1_done = P.op("vector", lambda E, db=db: E.tensor_tensor(
                XN1[:, db * T:(db + 1) * T], T1A[:], BCB1[:],
                AluOpType.subtract), inc="ve")
            pump_gamma(6)
        if debug is True:
            P.dma("sync", dbg["d_xn1"].rearrange("(db p) t -> p db t", p=128),
                  XN1.rearrange("p (db t) -> p db t", db=NDB),
                  wait=[xn1_done])

        KG = buf(B0 + 64 * KB, BF16, 4 * T)
        VG = buf(B0 + 80 * KB, BF16, 16 * 512)
        QG = buf(B0 + 96 * KB, BF16, 4 * TQ)
        ZB = buf(B0 + 104 * KB, BF16, NH * TQ)
        WPQ = [buf(B0 + 136 * KB + i * 4 * KB, BF16, NDB * 128)
               for i in range(4)]
        WPV = [buf(B0 + 136 * KB + i * 8 * KB, BF16, NDB * 256)
               for i in range(2)]
        PT = [buf(PBo + i * 2 * KB, BF16, TQ) for i in range(2)]
        RBB = buf(MS, BF16, TQ)
        RRQ = buf(MS + 4 * KB, BF16, TQ, 0, 1)
        qwar = {0: None, 1: None, 2: None, 3: None}

        def wpanel(dst, wsrc, c0, width, quarters, extra=()):
            src = AP(wsrc.tensor, c0, [[D, 128], [128 * D, NDB], [1, width]])
            waits = [qwar[q] for q in quarters] + list(extra)
            e = P.dma("gpsimd", dst.rearrange("p (db c) -> p db c", db=NDB),
                      src, wait=waits)
            return e

        for g in range(4):
            for ob in range(4):
                par = ob % 2
                ek = wpanel(WPQ[par], wk, (4 * g + ob) * 128, 128, (par,))
                for tc in range(4):
                    bk = tc % 2
                    last = None
                    for db in range(NDB):
                        last = P.op("tensor", lambda E, p=par, b=bk, db=db, tc=tc:
                                    E.matmul(ps[b][0:128, 0:512],
                                             WPQ[p][:, db * 128:(db + 1) * 128],
                                             XN1[:, db * T + tc * 512:
                                                 db * T + tc * 512 + 512],
                                             start=(db == 0), stop=(db == NDB - 1)),
                                    wait=[ek, xn1_done, bank_free[bk]],
                                    inc="pe" if db == NDB - 1 else None)
                    dst = KG[:, ob * T + tc * 512: ob * T + tc * 512 + 512]
                    if tc % 2 == 0:
                        ev = P.op("vector", lambda E, d=dst, b=bk:
                                  E.tensor_copy(d, ps[b][0:128, 0:512]),
                                  wait=[last], inc="ve")
                    else:
                        ev = P.op("scalar", lambda E, d=dst, b=bk:
                                  E.activation(d, ps[b][0:128, 0:512], AF.Copy),
                                  wait=[last], inc="se")
                    bank_free[bk] = ev
                    qwar[par] = ev
            for oc in range(2):
                par = oc % 2
                evw = wpanel(WPV[par], wv, g * 512 + oc * 256, 256,
                             (2 * par, 2 * par + 1))
                for tb in range(16):
                    bk = 2 + tb % 2
                    last = None
                    for db in range(NDB):
                        last = P.op("tensor", lambda E, p=par, b=bk, db=db, tb=tb:
                                    E.matmul(ps[b][0:128, 0:256],
                                             XN1[:, db * T + tb * 128:
                                                 db * T + tb * 128 + 128],
                                             WPV[p][:, db * 256:(db + 1) * 256],
                                             start=(db == 0), stop=(db == NDB - 1)),
                                    wait=[evw, bank_free[bk]],
                                    inc="pe" if db == NDB - 1 else None)
                    dst = VG[:, tb * 512 + oc * 256: tb * 512 + oc * 256 + 256]
                    if tb % 2 == 0:
                        ev = P.op("vector", lambda E, d=dst, b=bk:
                                  E.tensor_copy(d, ps[b][0:128, 0:256]),
                                  wait=[last], inc="ve")
                    else:
                        ev = P.op("scalar", lambda E, d=dst, b=bk:
                                  E.activation(d, ps[b][0:128, 0:256], AF.Copy),
                                  wait=[last], inc="se")
                    bank_free[bk] = ev
                    qwar[2 * par] = ev
                    qwar[2 * par + 1] = ev
            for ob in range(4):
                par = 2 + ob % 2
                eq = wpanel(WPQ[par], wq, (4 * g + ob) * 128, 128, (par,))
                for tc in range(2):
                    bk = 4 + tc
                    last = None
                    for db in range(NDB):
                        last = P.op("tensor", lambda E, p=par, b=bk, db=db, tc=tc:
                                    E.matmul(ps[b][0:128, 0:512],
                                             WPQ[p][:, db * 128:(db + 1) * 128],
                                             XN1[:, db * T + tc * 512:
                                                 db * T + tc * 512 + 512],
                                             start=(db == 0), stop=(db == NDB - 1)),
                                    wait=[eq, bank_free[bk]],
                                    inc="pe" if db == NDB - 1 else None)
                    dst = QG[:, ob * TQ + tc * 512: ob * TQ + tc * 512 + 512]
                    if tc == 0:
                        ev = P.op("vector", lambda E, d=dst, b=bk:
                                  E.tensor_copy(d, ps[b][0:128, 0:512]),
                                  wait=[last], inc="ve")
                    else:
                        ev = P.op("scalar", lambda E, d=dst, b=bk:
                                  E.activation(d, ps[b][0:128, 0:512], AF.Copy),
                                  wait=[last], inc="se")
                    bank_free[bk] = ev
                    qwar[par] = ev
            pump_gamma(8)
            projd = [P.ev("ve"), P.ev("se")]
            if debug is True and g == 0:
                P.dma("sync", dbg["d_k"], KG[:, 0:4 * T], wait=projd)
                P.dma("sync", dbg["d_v"], VG[:, 0:16 * 512], wait=projd)
                P.dma("sync", dbg["d_q"], QG[:, 0:4 * TQ], wait=projd)
                projd = projd + P.dma_fence("sync")
            for h in range(4):
                H = 4 * g + h
                last_z = {0: None, 1: None}
                last_sum = {2: None, 3: None}
                for p in range(16):
                    s = p if p < 8 else p - 8
                    col0 = 128 * s
                    pieces = ([(col0, 512), (512, 1024)] if col0 < 512
                              else [(col0, 1024)])
                    par = p % 2
                    evp = []
                    for pi, (c0, c1) in enumerate(pieces):
                        sb = 4 + 2 * par + pi
                        wd = c1 - c0
                        emm = P.op("tensor", lambda E, b=sb, h=h, p=p, c0=c0, wd=wd:
                                   E.matmul(ps[b][0:128, 0:wd],
                                            KG[:, h * T + p * 128:
                                               h * T + p * 128 + 128],
                                            QG[:, h * TQ + c0: h * TQ + c0 + wd],
                                            start=True, stop=True),
                                   wait=[bank_free[sb]] + projd, inc="pe")
                        ee = P.op("scalar", lambda E, b=sb, pr=par, c0=c0, wd=wd:
                                  E.activation(PT[pr][:, c0:c0 + wd],
                                               ps[b][0:128, 0:wd],
                                               AF.Exp, scale=ISQ),
                                  wait=[emm], inc="se")
                        bank_free[sb] = ee
                        evp.append(ee)
                    if p < 8:
                        emk = P.op("vector", lambda E, pr=par, c0=col0:
                                   E.tensor_tensor(PT[pr][:, c0:c0 + 128],
                                                   PT[pr][:, c0:c0 + 128],
                                                   TRI[:], AluOpType.mult),
                                   wait=[evp[0], ev_tri], inc="ve")
                    else:
                        emk = P.op("vector", lambda E, pr=par, c0=col0, k=p - 8:
                                   E.tensor_scalar(PT[pr][:, c0:c0 + 128],
                                                   PT[pr][:, c0:c0 + 128],
                                                   C_MSELB[:, k:k + 1], None,
                                                   AluOpType.mult),
                                   wait=[evp[0], ev_mselb], inc="ve")
                    for pi, (c0, c1) in enumerate(pieces):
                        wd = c1 - c0
                        zb_ = c0 // 512
                        zo = c0 - 512 * zb_
                        wz = [evp[pi], emk]
                        if p == 0:
                            wz += [bank_free[zb_], bank_free[2 + zb_]]
                        ez = P.op("tensor",
                                  lambda E, b=zb_, zo=zo, wd=wd, p=p, pr=par,
                                  c0=c0, h=h:
                                  E.matmul(ps[b][0:128, zo:zo + wd],
                                           VG[:, p * 512 + h * 128:
                                              p * 512 + h * 128 + 128],
                                           PT[pr][:, c0:c0 + wd],
                                           start=(p == 0), stop=(p == 15)),
                                  wait=wz, inc="pe")
                        last_z[zb_] = ez
                        es_ = P.op("tensor",
                                   lambda E, b=2 + zb_, zo=zo, wd=wd, p=p,
                                   pr=par, c0=c0:
                                   E.matmul(ps[b][0:1, zo:zo + wd], C_ONESB[:],
                                            PT[pr][:, c0:c0 + wd],
                                            start=(p == 0), stop=(p == 15)),
                                   wait=wz, inc="pe")
                        last_sum[2 + zb_] = es_
                er1 = P.op("vector", lambda E: E.tensor_copy(
                    RSUM[:, 0:512], ps[2][0:1, 0:512]),
                    wait=[last_sum[2]], inc="ve")
                er2 = P.op("vector", lambda E: E.tensor_copy(
                    RSUM[:, 512:1024], ps[3][0:1, 0:512]),
                    wait=[last_sum[3]], inc="ve")
                bank_free[2] = er2
                bank_free[3] = er2
                P.op("vector", lambda E: E.reciprocal(RREC[:], RSUM[:]))
                erb = P.op("vector", lambda E: E.tensor_copy(
                    RRQ[:], RREC[:]), inc="ve")
                for c in range(2):
                    sb = 4 + c
                    ebm = P.op("tensor", lambda E, b=sb, c=c:
                               E.matmul(ps[b][0:128, 0:512], C_ONESRB[:],
                                        RRQ[:, c * 512:(c + 1) * 512],
                                        start=True, stop=True),
                               wait=[erb, bank_free[sb]], inc="pe")
                    ecp = P.op("vector", lambda E, b=sb, c=c:
                               E.tensor_copy(RBB[:, c * 512:(c + 1) * 512],
                                             ps[b][0:128, 0:512]),
                               wait=[ebm], inc="ve")
                    bank_free[sb] = ecp
                    ezv = P.op("vector", lambda E, b=c, c=c, H=H:
                               E.tensor_tensor(
                                   ZB[:, H * TQ + c * 512: H * TQ + c * 512 + 512],
                                   ps[b][0:128, 0:512],
                                   RBB[:, c * 512:(c + 1) * 512],
                                   AluOpType.mult),
                               wait=[last_z[c], ecp], inc="ve")
                    bank_free[c] = ezv
                pump_gamma(12)
        zb_done = P.ev("ve")
        if debug is True:
            P.dma("sync", dbg["d_z"], ZB[:, 0:NH * TQ], wait=[zb_done])

        AO = buf(B0, BF16, NDB * TQ)
        X2 = buf(B0 + 32 * KB, F32, NDB * TQ)
        WOP = [buf(B0 + 136 * KB + i * 4 * KB, BF16, NDB * 128)
               for i in range(2)]
        AOSQ = buf(MS, BF16, TQ)
        last_st2 = None
        for ob in range(NDB):
            par = ob % 2
            ew = wpanel(WOP[par], wo, ob * 128, 128, (par,))
            for tc in range(2):
                bk = tc
                last = None
                for db in range(NDB):
                    last = P.op("tensor", lambda E, p=par, b=bk, db=db, tc=tc:
                                E.matmul(ps[b][0:128, 0:512],
                                         WOP[p][:, db * 128:(db + 1) * 128],
                                         ZB[:, db * TQ + tc * 512:
                                            db * TQ + tc * 512 + 512],
                                         start=(db == 0), stop=(db == NDB - 1)),
                                wait=[ew, zb_done, bank_free[bk]],
                                inc="pe" if db == NDB - 1 else None)
                dst = AO[:, ob * TQ + tc * 512: ob * TQ + tc * 512 + 512]
                eev = P.op("scalar", lambda E, d=dst, b=bk: E.activation(
                    d, ps[b][0:128, 0:512], AF.Copy), wait=[last], inc="se")
                bank_free[bk] = eev
                qwar[par] = eev
                esq = P.op("scalar", lambda E, d=dst, tc=tc: E.activation(
                    AOSQ[:, tc * 512:(tc + 1) * 512], d, AF.Square), inc="se")
                P.op("tensor", lambda E, d=dst, b=4 + tc, ob=ob:
                     E.matmul(ps[b][0:1, 0:512], C_ONESB[:], d,
                              start=(ob == 0), stop=(ob == NDB - 1)),
                     wait=[eev] + ([bank_free[4 + tc]] if ob == 0 else []))
                last_st2 = P.op("tensor", lambda E, b=6 + tc, tc=tc, ob=ob:
                                E.matmul(ps[b][0:1, 0:512], C_ONESB[:],
                                         AOSQ[:, tc * 512:(tc + 1) * 512],
                                         start=(ob == 0), stop=(ob == NDB - 1)),
                                wait=[esq] + ([bank_free[6 + tc]] if ob == 0 else []),
                                inc="pe")
            pump_gamma(2)
        if debug is True:
            edbg = P.dma("sync", dbg["d_ao"], AO[:, 0:NDB * TQ],
                         wait=[P.ev("se")])
        rowsC = mkrows(B0 + 96 * KB, TQ)
        ea2, eb2 = ln_rows(TQ, [(4, 512), (5, 512)], [(6, 512), (7, 512)],
                           rowsC, wait=[last_st2])
        BCA2, BCB2, ebc2 = bcast_rows(TQ, ea2, eb2, rowsC)
        XMY = [buf(WS + i * 4 * KB, F32, TQ) for i in range(2)]
        T1C = buf(PBo, F32, TQ)
        T2C = buf(WS2 + 4 * KB, F32, TQ)
        X2B = buf(MS + 2 * KB, BF16, TQ)
        X2SQ = buf(MS + 4 * KB, BF16, TQ)
        xmy_war = [None, None]
        last_st3 = None
        prev_shadow = [None]
        for ob in range(NDB):
            par = ob % 2
            ed = P.dma("sync", XMY[par][:], xt[ob * 128:(ob + 1) * 128, 0:TQ],
                       wait=[xmy_war[par]])
            e1 = P.op("vector", lambda E, ob=ob: E.tensor_tensor(
                T1C[:], AO[:, ob * TQ:(ob + 1) * TQ], BCA2[:], AluOpType.mult),
                wait=[ebc2], inc="ve")
            e2 = P.op("vector", lambda E: E.tensor_tensor(
                T2C[:], T1C[:], BCB2[:], AluOpType.subtract), inc="ve")
            e3 = P.op("vector", lambda E, ob=ob, p=par: E.tensor_tensor(
                X2[:, ob * TQ:(ob + 1) * TQ], T2C[:], XMY[p][:],
                AluOpType.add), wait=[ed], inc="ve")
            xmy_war[par] = e3
            e4 = P.op("vector", lambda E, ob=ob: E.tensor_copy(
                X2B[:], X2[:, ob * TQ:(ob + 1) * TQ]),
                wait=[prev_shadow[0]], inc="ve")
            e5 = P.op("scalar", lambda E: E.activation(X2SQ[:], X2B[:],
                                                       AF.Square),
                      wait=[e4], inc="se")
            for tc in range(2):
                P.op("tensor", lambda E, b=4 + tc, tc=tc, ob=ob:
                     E.matmul(ps[b][0:1, 0:512], C_ONESB[:],
                              X2B[:, tc * 512:(tc + 1) * 512],
                              start=(ob == 0), stop=(ob == NDB - 1)),
                     wait=[e4] + ([bank_free[4 + tc]] if ob == 0 else []))
                last_st3 = P.op("tensor", lambda E, b=6 + tc, tc=tc, ob=ob:
                                E.matmul(ps[b][0:1, 0:512], C_ONESB[:],
                                         X2SQ[:, tc * 512:(tc + 1) * 512],
                                         start=(ob == 0), stop=(ob == NDB - 1)),
                                wait=[e5] + ([bank_free[6 + tc]] if ob == 0 else []),
                                inc="pe")
            prev_shadow[0] = last_st3
            P.dma("sync", x2spill[ob * 128:(ob + 1) * 128, :],
                  X2[:, ob * TQ:(ob + 1) * TQ], wait=[e3])
            pump_gamma(5)
        x2_done = P.ev("ve")
        x2spill_done = P.dma_fence("sync")
        if debug is True:
            P.dma("sync", dbg["d_x2"].rearrange("(db p) t -> p db t", p=128),
                  X2.rearrange("p (db t) -> p db t", db=NDB), wait=[x2_done])
        ea3, eb3 = ln_rows(TQ, [(4, 512), (5, 512)], [(6, 512), (7, 512)],
                           rowsC, wait=[last_st3])
        BCA3, BCB3, ebc3 = bcast_rows(TQ, ea3, eb3, rowsC)
        XN2 = buf(B0, BF16, NDB * TQ)
        dbg_bar = P.dma_fence("sync") if debug is True else []
        xn2_done = None
        for ob in range(NDB):
            e1 = P.op("vector", lambda E, ob=ob: E.tensor_tensor(
                T1C[:], X2[:, ob * TQ:(ob + 1) * TQ], BCA3[:], AluOpType.mult),
                wait=[ebc3] + dbg_bar, inc="ve")
            xn2_done = P.op("vector", lambda E, ob=ob: E.tensor_tensor(
                XN2[:, ob * TQ:(ob + 1) * TQ], T1C[:], BCB3[:],
                AluOpType.subtract), inc="ve")
        if debug in (True, "late"):
            P.dma("sync", dbg["d_xn2"], XN2[:, 0:NDB * TQ], wait=[xn2_done])

        while gamma_pos[0] < len(gamma_jobs):
            pump_gamma(8)
        gamma_done = [P.ev("ve"), P.ev("se")]

        spans = {0: [], 1: [], 2: []}
        for j, (widx, _s) in enumerate(gamma_jobs):
            spans[widx].append(j)
        for i in range(3):
            cols = spans[i]
            c0, c1 = min(cols), max(cols) + 1
            assert cols == list(range(c0, c1))
            n = c1 - c0
            eg = P.op("tensor", lambda E, c0=c0, n=n: E.matmul(
                ps[7][0:1, 0:n], C_ONES[:], GACC[:, c0:c0 + n],
                start=True, stop=True),
                wait=gamma_done + [bank_free[7]], inc="pe")
            ecr = P.op("vector", lambda E, i=i, n=n: E.tensor_reduce(
                C_TOT[i][:], ps[7][0:1, 0:n], AX.X, AluOpType.add),
                wait=[eg], inc="ve")
            bank_free[7] = ecr
            Nw = D * HF
            eth = P.op("gpsimd", lambda E, i=i, N=Nw: E.tensor_scalar(
                C_THR[i][:], C_TOT[i][:], 0.5 / N, 0.5 * QEPS,
                AluOpType.mult, AluOpType.add), wait=[ecr], inc="ge")
            ethn = P.op("gpsimd", lambda E, i=i, N=Nw: E.tensor_scalar(
                C_THNR[i][:], C_TOT[i][:], -0.5 / N, -0.5 * QEPS,
                AluOpType.mult, AluOpType.add), inc="ge")
            eb1 = P.op("tensor", lambda E, i=i: E.matmul(
                ps[7][0:128, 0:1], C_ONESR[:], C_THR[i][:], start=True,
                stop=True), wait=[eth, bank_free[7]], inc="pe")
            ec1 = P.op("vector", lambda E, i=i: E.tensor_copy(
                C_TH[2 * i][:], ps[7][0:128, 0:1]), wait=[eb1], inc="ve")
            bank_free[7] = ec1
            eb2 = P.op("tensor", lambda E, i=i: E.matmul(
                ps[7][0:128, 0:1], C_ONESR[:], C_THNR[i][:], start=True,
                stop=True), wait=[ethn, ec1], inc="pe")
            ec2 = P.op("vector", lambda E, i=i: E.tensor_copy(
                C_TH[2 * i + 1][:], ps[7][0:128, 0:1]), wait=[eb2], inc="ve")
            bank_free[7] = ec2
        th_done = P.op("vector", lambda E: E.engine_nop(), inc="ve")
        if debug:
            for i in range(3):
                P.dma("sync", dbg["d_th"][0:1, i:i + 1], C_TH[2 * i][0:1, 0:1],
                      wait=[th_done])
                P.dma("sync", dbg["d_th"][0:1, 3 + i:4 + i], C_TOT[i][:],
                      wait=[th_done])

        HID = buf(B0 + 32 * KB, BF16, NHB * TQ)
        W1Q = [buf(B0 + 120 * KB + i * 8 * KB, BF16, NDB * 256)
               for i in range(2)]
        W2Q = [buf(B0 + 136 * KB + i * 8 * KB, BF16, NDB * 256)
               for i in range(2)]
        WF = [buf(WS + i * KB, F32, 256) for i in range(4)]
        SG = buf(MS, F32, 512)
        T2P = buf(MS + 2 * KB, BF16, 256)
        WF3 = [buf(WS2 + i * 2 * KB, F32, 512) for i in range(2)]
        T3P = buf(MS + 3 * KB, BF16, 512)
        W3T = buf(PBo, BF16, 512)
        w3_jobs = [(hb, c) for hb in range(NHB) for c in range(4)]
        w3_pos = [0]
        wf3_war = [None, None]
        w3t_war = [None]

        def pump_w3(n):
            for _ in range(n):
                i = w3_pos[0]
                if i >= len(w3_jobs):
                    return
                hb, c = w3_jobs[i]
                w3_pos[0] += 1
                par = i % 2
                ed = P.dma("sync", WF3[par][:],
                           w3[hb * 128:(hb + 1) * 128, c * 512:(c + 1) * 512],
                           wait=[wf3_war[par], th_done])
                P.op("vector", lambda E, p=par: E.tensor_scalar(
                    T3P[:], WF3[p][:], C_TH[5][:, 0:1], -1.0,
                    AluOpType.is_gt, AluOpType.add),
                    wait=[ed, th_done], inc="ve")
                eq2 = P.op("vector", lambda E, p=par: E.scalar_tensor_tensor(
                    W3T[:], WF3[p][:], C_TH[4][:, 0:1], T3P[:],
                    AluOpType.is_ge, AluOpType.add),
                    wait=[w3t_war[0]], inc="ve")
                wf3_war[par] = eq2
                ewr = P.dma("sync", w3qs[hb * 128:(hb + 1) * 128,
                                         c * 512:(c + 1) * 512], W3T[:],
                            wait=[eq2])
                w3t_war[0] = ewr

        wf_war = [None] * 4
        sg_war = [None]
        hid_done = None
        NPAN = 22
        qdone = {}

        def quant_panel(hp):
            par = hp % 2
            last = None
            for db in range(NDB):
                for wi, (wsrc, WQb, thp, thn) in enumerate(
                        [(w1, W1Q[par], C_TH[0], C_TH[1]),
                         (w2, W2Q[par], C_TH[2], C_TH[3])]):
                    wpar = wi + 2 * (db % 2)
                    ed = P.dma("sync", WF[wpar][:],
                               wsrc[db * 128:(db + 1) * 128,
                                    hp * 256:(hp + 1) * 256],
                               wait=[wf_war[wpar]])
                    P.op("vector", lambda E, p=wpar, t=thn: E.tensor_scalar(
                        T2P[:], WF[p][:], t[:, 0:1], -1.0,
                        AluOpType.is_gt, AluOpType.add),
                        wait=[ed, th_done], inc="ve")
                    last = P.op("vector",
                                lambda E, p=wpar, t=thp, W=WQb, db=db:
                                E.scalar_tensor_tensor(
                                    W[:, db * 256:(db + 1) * 256], WF[p][:],
                                    t[:, 0:1], T2P[:],
                                    AluOpType.is_ge, AluOpType.add),
                                inc="ve")
                    wf_war[wpar] = last
                if db % 8 == 0:
                    pump_w3(1)
            qdone[hp] = last

        quant_panel(0)
        for hp in range(NPAN):
            if hp + 1 < NPAN:
                quant_panel(hp + 1)
            par = hp % 2
            for hb in range(2):
                hglob = hp * 2 + hb
                for tc in range(2):
                    b1 = 2 * tc
                    b2 = b1 + 1
                    l1 = l2 = None
                    for db in range(NDB):
                        l1 = P.op("tensor",
                                  lambda E, b=b1, db=db, hb=hb, tc=tc, par=par:
                                  E.matmul(ps[b][0:128, 0:512],
                                           W1Q[par][:, db * 256 + hb * 128:
                                               db * 256 + hb * 128 + 128],
                                           XN2[:, db * TQ + tc * 512:
                                               db * TQ + tc * 512 + 512],
                                           start=(db == 0), stop=(db == NDB - 1)),
                                  wait=[qdone[hp], xn2_done, bank_free[b1]],
                                  inc="pe" if db == NDB - 1 else None)
                        l2 = P.op("tensor",
                                  lambda E, b=b2, db=db, hb=hb, tc=tc, par=par:
                                  E.matmul(ps[b][0:128, 0:512],
                                           W2Q[par][:, db * 256 + hb * 128:
                                               db * 256 + hb * 128 + 128],
                                           XN2[:, db * TQ + tc * 512:
                                               db * TQ + tc * 512 + 512],
                                           start=(db == 0), stop=(db == NDB - 1)),
                                  wait=[bank_free[b2]],
                                  inc="pe" if db == NDB - 1 else None)
                    eg1 = P.op("scalar", lambda E, b=b1: E.activation(
                        SG[:], ps[b][0:128, 0:512], AF.Sigmoid),
                        wait=[l1, sg_war[0]], inc="se")
                    bank_free[b1] = eg1
                    eh = P.op("vector", lambda E, b=b2, hg=hglob, tc=tc:
                              E.tensor_tensor(
                                  HID[:, hg * TQ + tc * 512:
                                      hg * TQ + tc * 512 + 512],
                                  SG[:], ps[b][0:128, 0:512], AluOpType.mult),
                              wait=[eg1, l2], inc="ve")
                    bank_free[b2] = eh
                    sg_war[0] = eh
                    hid_done = eh
        while w3_pos[0] < len(w3_jobs):
            pump_w3(4)
        w3q_written = P.dma_fence("sync")
        if debug in (True, "late"):
            P.dma("sync", dbg["d_hid"], HID[:, 0:NHB * TQ], wait=[hid_done])

        FB = buf(B0, BF16, NDB * TQ)
        W3P = [buf(B0 + 120 * KB + i * 11 * KB, BF16, NHB * 128)
               for i in range(2)]
        FSQ = buf(MS, BF16, TQ)
        X2S = [buf(B0 + 142 * KB + i * 4 * KB, F32, TQ) for i in range(2)]
        SCR = [buf(B0 + 150 * KB + i * 4 * KB, F32, TQ) for i in range(2)]
        w3p_war = [None, None]
        last_st4 = None
        for ob in range(NDB):
            par = ob % 2
            for qtr in range(4):
                src = AP(w3qs.tensor, ob * 128 + qtr * 11 * 128 * D,
                        [[D, 128], [128 * D, 11], [1, 128]])
                dstp = W3P[par][:, qtr * 11 * 128:(qtr + 1) * 11 * 128]
                ewp = P.dma("sync", dstp.rearrange("p (hb c) -> p hb c", hb=11),
                            src, wait=[w3p_war[par], hid_done, w3q_written])
            ew = ewp
            for tc in range(2):
                bk = tc
                last = None
                for hb in range(NHB):
                    last = P.op("tensor", lambda E, p=par, b=bk, hb=hb, tc=tc:
                                E.matmul(ps[b][0:128, 0:512],
                                         W3P[p][:, hb * 128:(hb + 1) * 128],
                                         HID[:, hb * TQ + tc * 512:
                                             hb * TQ + tc * 512 + 512],
                                         start=(hb == 0), stop=(hb == NHB - 1)),
                                wait=[ew, hid_done, bank_free[bk]],
                                inc="pe" if hb == NHB - 1 else None)
                dst = FB[:, ob * TQ + tc * 512: ob * TQ + tc * 512 + 512]
                eev = P.op("scalar", lambda E, d=dst, b=bk: E.activation(
                    d, ps[b][0:128, 0:512], AF.Copy), wait=[last], inc="se")
                bank_free[bk] = eev
                w3p_war[par] = eev
                esq = P.op("scalar", lambda E, d=dst, tc=tc: E.activation(
                    FSQ[:, tc * 512:(tc + 1) * 512], d, AF.Square), inc="se")
                P.op("tensor", lambda E, d=dst, b=4 + tc, ob=ob:
                     E.matmul(ps[b][0:1, 0:512], C_ONESB[:], d,
                              start=(ob == 0), stop=(ob == NDB - 1)),
                     wait=[eev] + ([bank_free[4 + tc]] if ob == 0 else []))
                last_st4 = P.op("tensor", lambda E, b=6 + tc, tc=tc, ob=ob:
                                E.matmul(ps[b][0:1, 0:512], C_ONESB[:],
                                         FSQ[:, tc * 512:(tc + 1) * 512],
                                         start=(ob == 0), stop=(ob == NDB - 1)),
                                wait=[esq] + ([bank_free[6 + tc]] if ob == 0 else []),
                                inc="pe")
        if debug in (True, "late"):
            P.dma("sync", dbg["d_f"], FB[:, 0:NDB * TQ], wait=[P.ev("se")])
        rowsE = mkrows(B0 + 32 * KB, TQ)
        ea4, eb4 = ln_rows(TQ, [(4, 512), (5, 512)], [(6, 512), (7, 512)],
                           rowsE, wait=[last_st4])
        BCA4, BCB4, ebc4 = bcast_rows(TQ, ea4, eb4, rowsE)
        x2s_war = [None, None]
        scr_war = [None, None]
        out_ev = None
        for ob in range(NDB):
            par = ob % 2
            ed = P.dma("sync", X2S[par][:],
                       x2spill[ob * 128:(ob + 1) * 128, :],
                       wait=[x2s_war[par], hid_done, x2spill_done])
            e1 = P.op("vector", lambda E, ob=ob: E.tensor_tensor(
                T1C[:], FB[:, ob * TQ:(ob + 1) * TQ], BCA4[:], AluOpType.mult),
                wait=[ebc4], inc="ve")
            e2 = P.op("vector", lambda E, p=par: E.tensor_tensor(
                SCR[p][:], T1C[:], BCB4[:], AluOpType.subtract),
                wait=[scr_war[par]], inc="ve")
            e3 = P.op("vector", lambda E, p=par: E.tensor_tensor(
                SCR[p][:], SCR[p][:], X2S[p][:], AluOpType.add),
                wait=[ed], inc="ve")
            x2s_war[par] = e3
            out_ev = P.dma("sync", out[ob * 128:(ob + 1) * 128, :],
                           SCR[par][:], wait=[e3])
            scr_war[par] = out_ev
        P.op("sync", None, wait=P.dma_fence("sync") + P.dma_fence("gpsimd"))

        blk = es.enter_context(nc.Block())
        P.build(blk)
    return nc


_NC_CACHE = {}


def get_nc(debug=False):
    if debug not in _NC_CACHE:
        _NC_CACHE[debug] = build_program(debug)
    return _NC_CACHE[debug]


def host_inputs(L0_attn_x, L0_attn_Wq, L0_attn_Wk, L0_attn_Wv, L0_attn_Wo,
                L0_ffn_w1, L0_ffn_w2, L0_ffn_w3):
    x = np.asarray(L0_attn_x)
    wqT = np.ascontiguousarray(np.asarray(L0_attn_Wq).T)
    wkT = np.ascontiguousarray(np.asarray(L0_attn_Wk).T)
    wvT = np.ascontiguousarray(np.asarray(L0_attn_Wv).T)
    woT = np.ascontiguousarray(np.asarray(L0_attn_Wo).T)
    w1T = np.ascontiguousarray(np.asarray(L0_ffn_w1).T)
    w2T = np.ascontiguousarray(np.asarray(L0_ffn_w2).T)
    w3T = np.ascontiguousarray(np.asarray(L0_ffn_w3).T)
    tri = np.triu(np.ones((128, 128), np.float32))
    msel_h = []
    for mine, peer in ((TILES_A, TILES_B), (TILES_B, TILES_A)):
        m = np.zeros((1, 8), np.float32)
        for k in range(8):
            s_real = sum(1 for i in mine if i < peer[k])
            m[0, k] = 0.0 if s_real > k else 1.0
        msel_h.append(m)
    perms = []
    for mine, peer in ((TILES_A, TILES_B), (TILES_B, TILES_A)):
        cols = []
        for i in mine + peer:
            cols.extend(range(i * 128, (i + 1) * 128))
        perms.append(np.array(cols))
    in_maps = []
    for core in range(8):
        b = core // 2
        half = core % 2
        xtp = np.ascontiguousarray(x[b].T[:, perms[half]])
        in_maps.append({
            "xt": xtp, "wq": wqT, "wk": wkT, "wv": wvT, "wo": woT,
            "w1": w1T, "w2": w2T, "w3": w3T, "tri": tri,
            "msel": msel_h[half],
        })
    return in_maps


def kernel(L0_attn_x, L0_attn_Wq, L0_attn_Wk, L0_attn_Wv, L0_attn_Wo,
           L0_ffn_w1, L0_ffn_w2, L0_ffn_w3):
    in_maps = host_inputs(L0_attn_x, L0_attn_Wq, L0_attn_Wk, L0_attn_Wv,
                          L0_attn_Wo, L0_ffn_w1, L0_ffn_w2, L0_ffn_w3)
    nc = get_nc(False)
    res = run_bass_kernel_spmd(nc, in_maps, core_ids=list(range(8)))
    y = np.empty((4, T, D), np.float32)
    for b in range(4):
        yT = np.empty((D, T), np.float32)
        for half, tiles in ((0, TILES_A), (1, TILES_B)):
            o = np.asarray(res.results[2 * b + half]["out"])
            for a, i in enumerate(tiles):
                yT[:, i * 128:(i + 1) * 128] = o[:, a * 128:(a + 1) * 128]
        y[b] = yT.T
    return y


# version 05
# speedup vs baseline: 1.9623x; 1.2298x over previous
import math
import contextlib
import numpy as np

import concourse.bass as bass
import concourse.mybir as mybir
from concourse.bass_utils import run_bass_kernel_spmd
from concourse.alu_op_type import AluOpType
from concourse.ap import AP

F32 = mybir.dt.float32
BF16 = mybir.dt.bfloat16
AF = mybir.ActivationFunctionType
AX = mybir.AxisListType

D = 2048
T = 2048
TQ = 1024
NDB = 16
NH = 16
HF = 5632
NHB = 44
EPS = 1e-5
QEPS = 1e-8
ISQ = 1.0 / math.sqrt(128.0)
KB = 1024

TILES_A = [0, 3, 4, 7, 8, 11, 12, 15]
TILES_B = [1, 2, 5, 6, 9, 10, 13, 14]

NDMASEM = 8
SEM_ENG = {"pe": "tensor", "se": "scalar", "ve": "vector", "ge": "gpsimd"}
for _q in ("a", "b"):
    for _i in range(NDMASEM):
        SEM_ENG[f"ld{_q}{_i}"] = None
ENGS = ("sync", "gpsimd", "tensor", "scalar", "vector")


class Plan:

    def __init__(self, nc, es):
        self.nc = nc
        self.q = {e: [] for e in ENGS}
        self.sems = {}
        self.cnt = {}
        self.seen = {e: {} for e in ENGS}
        self.dseq = {"a": 0, "b": 0}
        for s in SEM_ENG:
            self.sems[s] = es.enter_context(nc.semaphore(s))
            self.cnt[s] = 0

    def ev(self, sem):
        return (sem, self.cnt[sem])

    def op(self, eng, fn, wait=(), inc=None, inc_by=1):
        waits = []
        flat = []
        for w in wait:
            if isinstance(w, list):
                flat.extend(w)
            else:
                flat.append(w)
        for w in flat:
            if w is None:
                continue
            s, v = w
            if v <= 0:
                continue
            if SEM_ENG[s] == eng:
                continue
            if self.seen[eng].get(s, 0) >= v:
                continue
            self.seen[eng][s] = v
            waits.append((s, v))
        evo = None
        if inc is not None:
            self.cnt[inc] += inc_by
            evo = (inc, self.cnt[inc])
        self.q[eng].append((waits, fn, (inc, inc_by) if inc else None))
        return evo

    def dma(self, eng, out, in_, wait=()):
        qn = "b" if eng == "gpsimd" else "a"
        k = self.dseq[qn] % NDMASEM
        self.dseq[qn] += 1
        sem = f"ld{qn}{k}"
        waits = list(wait) + [(sem, self.cnt[sem])]
        return self.op(eng,
                       lambda E, o=out, i=in_: E.dma_start(out=o, in_=i),
                       waits, inc=sem, inc_by=16)

    def dma_fence(self, eng):
        qn = "b" if eng == "gpsimd" else "a"
        return [(f"ld{qn}{k}", self.cnt[f"ld{qn}{k}"]) for k in range(NDMASEM)]

    def barrier(self):
        return [(s, self.cnt[s]) for s in self.cnt]

    def build(self, blk):
        for eng in ENGS:
            qq = self.q[eng]
            sems = self.sems

            def runner(E, qq=qq):
                for waits, fn, inc in qq:
                    for (s, v) in waits:
                        E.wait_ge(sems[s], v)
                    if fn is None:
                        continue
                    ins = fn(E)
                    if inc is not None:
                        s, k = inc
                        ins.then_inc(sems[s], k)
            getattr(blk, eng)(runner)


def build_program(debug=False):
    nc = bass.Bass("TRN2", target_bir_lowering=False)

    xt = nc.dram_tensor("xt", [D, T], F32, kind="ExternalInput").ap()
    wq = nc.dram_tensor("wq", [D, D], F32, kind="ExternalInput").ap()
    wk = nc.dram_tensor("wk", [D, D], F32, kind="ExternalInput").ap()
    wv = nc.dram_tensor("wv", [D, D], F32, kind="ExternalInput").ap()
    wo = nc.dram_tensor("wo", [D, D], F32, kind="ExternalInput").ap()
    w1 = nc.dram_tensor("w1", [D, HF], F32, kind="ExternalInput").ap()
    w2 = nc.dram_tensor("w2", [D, HF], F32, kind="ExternalInput").ap()
    w3 = nc.dram_tensor("w3", [HF, D], F32, kind="ExternalInput").ap()
    tri = nc.dram_tensor("tri", [128, 128], F32, kind="ExternalInput").ap()
    msel = nc.dram_tensor("msel", [1, 8], F32, kind="ExternalInput").ap()
    out = nc.dram_tensor("out", [D, TQ], F32, kind="ExternalOutput").ap()
    x2spill = nc.dram_tensor("x2spill", [D, TQ], F32, kind="Internal").ap()
    dbg = {}
    if debug == "th":
        dbg["d_th"] = nc.dram_tensor("d_th", [1, 8], F32,
                                     kind="ExternalOutput").ap()
        dbg["d_thv"] = nc.dram_tensor("d_thv", [128, 8], F32,
                                      kind="ExternalOutput").ap()


    elif debug == "late":
        for name, shape, dt in [
            ("d_xn2", [128, NDB * TQ], BF16),
            ("d_hid", [128, NHB * TQ], BF16), ("d_f", [128, NDB * TQ], BF16),
            ("d_th", [1, 8], F32),
        ]:
            dbg[name] = nc.dram_tensor(name, shape, dt, kind="ExternalOutput").ap()
    elif debug:
        for name, shape, dt in [
            ("d_xn1", [D, T], BF16), ("d_k", [128, 4 * T], BF16),
            ("d_v", [128, 16 * 512], BF16), ("d_q", [128, 4 * TQ], BF16),
            ("d_z", [128, NH * TQ], BF16), ("d_ao", [128, NDB * TQ], BF16),
            ("d_x2", [D, TQ], F32), ("d_xn2", [128, NDB * TQ], BF16),
            ("d_hid", [128, NHB * TQ], BF16), ("d_f", [128, NDB * TQ], BF16),
            ("d_th", [1, 8], F32),
        ]:
            dbg[name] = nc.dram_tensor(name, shape, dt, kind="ExternalOutput").ap()

    es = contextlib.ExitStack()
    with es:
        arena = es.enter_context(
            nc.sbuf_tensor("arena", [128, 207 * KB + 256], mybir.dt.uint8))
        ar = arena[:]

        def buf(off_b, dtype, cols, p0=0, p1=128):
            nb = mybir.dt.size(dtype) * cols
            return ar[p0:p1, off_b:off_b + nb].bitcast(dtype)

        C_ONES = buf(0, F32, 1)
        C_TH = [buf(4 + 4 * i, F32, 1) for i in range(6)]
        C_GAC = [buf(28 + 4 * i, F32, 1) for i in range(3)]
        C_GRD = buf(40, F32, 1)
        C_GRD2 = buf(44, F32, 1)
        C_ONESR = buf(64, F32, 128, 0, 1)
        C_ONESRB = buf(576, BF16, 128, 0, 1)
        C_MSELR = buf(832, F32, 8, 0, 1)
        C_MSELB = buf(864, F32, 8)
        C_ONESB = buf(896, BF16, 1)
        C_ONESM = buf(1024, BF16, 128)
        C_EPS = buf(900, F32, 1)
        C_TOT = [buf(904 + 4 * i, F32, 1, 0, 1) for i in range(3)]
        C_EPS4 = buf(940, F32, 1)
        C_THR = [buf(916 + 4 * i, F32, 1, 0, 1) for i in range(3)]
        C_THNR = [buf(928 + 4 * i, F32, 1, 0, 1) for i in range(3)]
        TRI = buf(1536, F32, 128)
        RSUM = buf(2 * KB, F32, 1024, 0, 1)
        RREC = buf(6 * KB, F32, 1024, 0, 1)

        def mkrows(base_b, tmax):
            def rowf(slot, cols):
                return buf(base_b + slot * tmax * 4, F32, cols, 0, 1)
            return rowf

        MS = 10 * KB
        PBo = 16 * KB
        BC = 22 * KB
        WS = 30 * KB
        WS2 = 38 * KB
        B0 = 46 * KB
        GW = 198 * KB
        GDUMP = 202 * KB
        GACALL = 204 * KB

        nc.const_aps.register(F32, EPS, C_EPS[:, 0:1])
        nc.const_aps.register(F32, 4 * EPS, C_EPS4[:, 0:1])

        ps = [es.enter_context(nc.psum_tensor(f"ps{i}", [128, 512], F32))
              for i in range(8)]
        P = Plan(nc, es)
        bank_free = {b: None for b in range(8)}

        P.op("vector", lambda E: E.memset(C_ONES[:], 1.0))
        P.op("vector", lambda E: E.memset(C_ONESR[:], 1.0))
        P.op("vector", lambda E: E.memset(C_ONESRB[:], 1.0))
        P.op("vector", lambda E: E.memset(C_ONESB[:], 1.0))
        P.op("vector", lambda E: E.memset(C_ONESM[:], 1.0))
        P.op("vector", lambda E: E.memset(C_EPS[:], EPS))
        P.op("vector", lambda E: E.memset(C_EPS4[:], 4 * EPS))
        for g_ in C_GAC:
            P.op("vector", lambda E, g=g_: E.memset(g[:], 0.0))
        ev_init_v = P.op("vector", lambda E: E.engine_nop(), inc="ve")
        ev_tri = P.dma("sync", TRI[:], tri)
        ev_msel = P.dma("sync", C_MSELR[:], msel)
        em = P.op("tensor", lambda E: E.matmul(ps[7][0:128, 0:8], C_ONESR[:],
                                               C_MSELR[:], start=True, stop=True),
                  wait=[ev_msel, ev_init_v], inc="pe")
        ev_mselb = P.op("vector",
                        lambda E: E.tensor_copy(C_MSELB[:], ps[7][0:128, 0:8]),
                        wait=[em], inc="ve")
        bank_free[7] = ev_mselb
        init_bar = P.barrier()

        def ln_bcast(Tn, s1_banks, s2_banks, sa, sb_, sc, BCA, BCB,
                     wait=(), eps_val=EPS):
            for bk, o, w in s1_banks:
                e = P.op("vector", lambda E, b=bk, o=o, w=w: E.tensor_scalar(
                    sa[:, o:o + w], ps[b][0:128, 0:w], 1.0 / D, None,
                    AluOpType.mult), wait=wait, inc="ve")
                bank_free[bk] = e
            for bk, o, w in s2_banks:
                e = P.op("vector", lambda E, b=bk, o=o, w=w: E.tensor_scalar(
                    sb_[:, o:o + w], ps[b][0:128, 0:w], 1.0 / D, None,
                    AluOpType.mult), wait=wait, inc="ve")
                bank_free[bk] = e
            P.op("vector", lambda E, Tn=Tn: E.tensor_tensor(
                sc[:, 0:Tn], sa[:, 0:Tn], sa[:, 0:Tn], AluOpType.mult))
            ev = P.op("vector", lambda E, Tn=Tn: E.tensor_tensor(
                sb_[:, 0:Tn], sb_[:, 0:Tn], sc[:, 0:Tn], AluOpType.subtract),
                inc="ve")
            P.op("scalar", lambda E, Tn=Tn, e=eps_val: E.activation(
                sc[:, 0:Tn], sb_[:, 0:Tn], AF.Ln, bias=e), wait=[ev])
            er = P.op("scalar", lambda E, Tn=Tn: E.activation(
                sc[:, 0:Tn], sc[:, 0:Tn], AF.Exp, scale=-0.5), inc="se")
            ea = P.op("vector", lambda E, Tn=Tn: E.tensor_copy(
                BCA[:, 0:Tn], sc[:, 0:Tn]), wait=[er], inc="ve")
            eb = P.op("vector", lambda E, Tn=Tn: E.tensor_tensor(
                BCB[:, 0:Tn], sa[:, 0:Tn], sc[:, 0:Tn], AluOpType.mult),
                inc="ve")
            return eb

        GACC = buf(GACALL, F32, 528)
        gw_par = [0]
        gw_war = [None, None]
        gamma_jobs = []
        for hb in range(NHB):
            for c in range(4):
                gamma_jobs.append((2, w3[hb * 128:(hb + 1) * 128,
                                         c * 512:(c + 1) * 512]))
        for widx, w_ in ((0, w1), (1, w2)):
            for db in range(NDB):
                for hp in range(11):
                    gamma_jobs.append((widx, w_[db * 128:(db + 1) * 128,
                                                hp * 512:(hp + 1) * 512]))
        gamma_pos = [0]
        GDMP = buf(PBo + 4 * KB, BF16, 512)
        SGN1 = buf(GDUMP, BF16, 512)
        SGN2 = buf(GDUMP + 1 * KB, BF16, 512)
        T3P = buf(MS + 3 * KB, BF16, 512)
        W3T = buf(MS + 4 * KB, BF16, 512)

        def pump_gamma(n):
            for _ in range(n):
                i = gamma_pos[0]
                if i >= len(gamma_jobs):
                    return
                widx, src = gamma_jobs[i]
                gamma_pos[0] += 1
                par = gw_par[0] % 2
                gw_par[0] += 1
                gt = buf(GW + par * 2 * KB, F32, 512)
                ed = P.dma("sync", gt[:], src, wait=[gw_war[par]])
                if i % 2 == 0 or gamma_pos[0] > 352:
                    er = P.op("vector", lambda E, g=gt, i=i: E.tensor_reduce(
                        GACC[:, i:i + 1], g[:], AX.X, AluOpType.add,
                        apply_absolute_value=True), wait=[ed], inc="ve")
                else:
                    er = P.op("scalar", lambda E, g=gt, i=i: E.activation(
                        GDMP[:], g[:], AF.Abs, accum_out=GACC[:, i:i + 1]),
                        wait=[ed], inc="se")
                gw_war[par] = er

        XN1 = buf(B0, BF16, NDB * T)
        XB = [buf(B0 + 64 * KB + i * 4 * KB, BF16, T) for i in range(2)]
        XSQ = buf(MS, BF16, T)
        sq_done = [None, None]
        for db in range(NDB):
            par = db % 2
            ed = P.dma("gpsimd", XB[par][:], xt[db * 128:(db + 1) * 128, :],
                       wait=[sq_done[par]] + (init_bar if db < 2 else []))
            esq = P.op("scalar", lambda E, p=par: E.activation(
                XSQ[:], XB[p][:], AF.Square),
                wait=[ed, sq_done[1 - par]], inc="se")
            last = None
            for c in range(4):
                P.op("tensor", lambda E, p=par, c=c, db=db:
                     E.matmul(ps[c][0:128, 0:512], C_ONESM[:],
                              XB[p][:, c * 512:(c + 1) * 512],
                              start=(db == 0), stop=(db == NDB - 1)),
                     wait=[ed] + (init_bar if db == 0 else []))
                last = P.op("tensor", lambda E, c=c, db=db:
                            E.matmul(ps[4 + c][0:128, 0:512], C_ONESM[:],
                                     XSQ[:, c * 512:(c + 1) * 512],
                                     start=(db == 0), stop=(db == NDB - 1)),
                            wait=[esq], inc="pe" if c == 3 else None)
            sq_done[par] = last
            pump_gamma(6)
        stats_done = P.ev("pe")
        BCA1 = buf(BC, BF16, T)
        BCB1 = buf(BC + 4 * KB, BF16, T)
        SCA1 = buf(B0 + 88 * KB, F32, T)
        SCB1 = buf(B0 + 96 * KB, F32, T)
        SCC1 = buf(B0 + 104 * KB, F32, T)
        ebc1 = ln_bcast(T, [(c, c * 512, 512) for c in range(4)],
                        [(4 + c, c * 512, 512) for c in range(4)],
                        SCA1, SCB1, SCC1, BCA1, BCB1, wait=[stats_done])
        XF = [buf(B0 + 72 * KB + i * 8 * KB, F32, T) for i in range(2)]
        T1A = buf(WS, F32, T)
        xf_war = [None, None]
        t1_war = [None]
        xn1_done = None
        for db in range(NDB):
            par = db % 2
            ed = P.dma("sync", XF[par][:], xt[db * 128:(db + 1) * 128, :],
                       wait=[xf_war[par]])
            e1 = P.op("vector", lambda E, p=par: E.tensor_tensor(
                T1A[:], XF[p][:], BCA1[:], AluOpType.mult),
                wait=[ed, ebc1], inc="ve")
            xf_war[par] = e1
            xn1_done = P.op("vector", lambda E, db=db: E.tensor_tensor(
                XN1[:, db * T:(db + 1) * T], T1A[:], BCB1[:],
                AluOpType.subtract), inc="ve")
            pump_gamma(6)
        if debug is True:
            P.dma("sync", dbg["d_xn1"].rearrange("(db p) t -> p db t", p=128),
                  XN1.rearrange("p (db t) -> p db t", db=NDB),
                  wait=[xn1_done])

        def theta_block(i, c0, n, gate):
            eg = P.op("tensor", lambda E, c0=c0, n=n: E.matmul(
                ps[7][0:1, 0:n], C_ONES[:], GACC[:, c0:c0 + n],
                start=True, stop=True),
                wait=list(gate) + [bank_free[7]], inc="pe")
            ecr = P.op("vector", lambda E, i=i, n=n: E.tensor_reduce(
                C_TOT[i][:], ps[7][0:1, 0:n], AX.X, AluOpType.add),
                wait=[eg], inc="ve")
            bank_free[7] = ecr
            Nw = D * HF
            eth = P.op("gpsimd", lambda E, i=i, N=Nw: E.tensor_scalar(
                C_THR[i][:], C_TOT[i][:], 0.5 / N, 0.5 * QEPS,
                AluOpType.mult, AluOpType.add), wait=[ecr], inc="ge")
            ethn = P.op("gpsimd", lambda E, i=i, N=Nw: E.tensor_scalar(
                C_THNR[i][:], C_TOT[i][:], -0.5 / N, -0.5 * QEPS,
                AluOpType.mult, AluOpType.add), inc="ge")
            eb1 = P.op("tensor", lambda E, i=i: E.matmul(
                ps[7][0:128, 0:1], C_ONESR[:], C_THR[i][:], start=True,
                stop=True), wait=[eth, bank_free[7]], inc="pe")
            ec1 = P.op("vector", lambda E, i=i: E.tensor_copy(
                C_TH[2 * i][:], ps[7][0:128, 0:1]), wait=[eb1], inc="ve")
            bank_free[7] = ec1
            eb2 = P.op("tensor", lambda E, i=i: E.matmul(
                ps[7][0:128, 0:1], C_ONESR[:], C_THNR[i][:], start=True,
                stop=True), wait=[ethn, ec1], inc="ve" if False else "pe")
            ec2 = P.op("vector", lambda E, i=i: E.tensor_copy(
                C_TH[2 * i + 1][:], ps[7][0:128, 0:1]), wait=[eb2], inc="ve")
            bank_free[7] = ec2
            return ec2

        assert gamma_pos[0] >= 176, gamma_pos[0]
        th3_done = theta_block(2, 0, 176, [P.ev("ve"), P.ev("se")])

        KG = buf(B0 + 64 * KB, BF16, 4 * T)
        VG = buf(B0 + 80 * KB, BF16, 16 * 512)
        QG = buf(B0 + 96 * KB, BF16, 4 * TQ)
        ZB = buf(B0 + 104 * KB, BF16, NH * TQ)
        WPQ = [buf(B0 + 136 * KB + i * 4 * KB, BF16, NDB * 128)
               for i in range(4)]
        WPV = [buf(B0 + 136 * KB + i * 8 * KB, BF16, NDB * 256)
               for i in range(2)]
        PT = [buf(PBo + i * 2 * KB, BF16, TQ) for i in range(2)]
        SMS = buf(MS, F32, TQ)
        qwar = {0: None, 1: None, 2: None, 3: None}

        def wpanel(dst, wsrc, c0, width, quarters, extra=()):
            src = AP(wsrc.tensor, c0, [[D, 128], [128 * D, NDB], [1, width]])
            waits = [qwar[q] for q in quarters] + list(extra)
            e = P.dma("gpsimd", dst.rearrange("p (db c) -> p db c", db=NDB),
                      src, wait=waits)
            return e

        for g in range(4):
            for ob in range(4):
                par = ob % 2
                ek = wpanel(WPQ[par], wk, (4 * g + ob) * 128, 128, (par,))
                for tc in range(4):
                    bk = tc % 2
                    last = None
                    for db in range(NDB):
                        last = P.op("tensor", lambda E, p=par, b=bk, db=db, tc=tc:
                                    E.matmul(ps[b][0:128, 0:512],
                                             WPQ[p][:, db * 128:(db + 1) * 128],
                                             XN1[:, db * T + tc * 512:
                                                 db * T + tc * 512 + 512],
                                             start=(db == 0), stop=(db == NDB - 1)),
                                    wait=[ek, xn1_done, bank_free[bk]],
                                    inc="pe" if db == NDB - 1 else None)
                    dst = KG[:, ob * T + tc * 512: ob * T + tc * 512 + 512]
                    if tc % 2 == 0:
                        ev = P.op("vector", lambda E, d=dst, b=bk:
                                  E.tensor_copy(d, ps[b][0:128, 0:512]),
                                  wait=[last], inc="ve")
                    else:
                        ev = P.op("scalar", lambda E, d=dst, b=bk:
                                  E.activation(d, ps[b][0:128, 0:512], AF.Copy),
                                  wait=[last], inc="se")
                    bank_free[bk] = ev
                    qwar[par] = ev
            for oc in range(2):
                par = oc % 2
                evw = wpanel(WPV[par], wv, g * 512 + oc * 256, 256,
                             (2 * par, 2 * par + 1))
                for tb in range(16):
                    bk = 2 + tb % 2
                    last = None
                    for db in range(NDB):
                        last = P.op("tensor", lambda E, p=par, b=bk, db=db, tb=tb:
                                    E.matmul(ps[b][0:128, 0:256],
                                             XN1[:, db * T + tb * 128:
                                                 db * T + tb * 128 + 128],
                                             WPV[p][:, db * 256:(db + 1) * 256],
                                             start=(db == 0), stop=(db == NDB - 1)),
                                    wait=[evw, bank_free[bk]],
                                    inc="pe" if db == NDB - 1 else None)
                    dst = VG[:, tb * 512 + oc * 256: tb * 512 + oc * 256 + 256]
                    if tb % 2 == 0:
                        ev = P.op("vector", lambda E, d=dst, b=bk:
                                  E.tensor_copy(d, ps[b][0:128, 0:256]),
                                  wait=[last], inc="ve")
                    else:
                        ev = P.op("scalar", lambda E, d=dst, b=bk:
                                  E.activation(d, ps[b][0:128, 0:256], AF.Copy),
                                  wait=[last], inc="se")
                    bank_free[bk] = ev
                    qwar[2 * par] = ev
                    qwar[2 * par + 1] = ev
            for ob in range(4):
                par = 2 + ob % 2
                eq = wpanel(WPQ[par], wq, (4 * g + ob) * 128, 128, (par,))
                for tc in range(2):
                    bk = 4 + tc
                    last = None
                    for db in range(NDB):
                        last = P.op("tensor", lambda E, p=par, b=bk, db=db, tc=tc:
                                    E.matmul(ps[b][0:128, 0:512],
                                             WPQ[p][:, db * 128:(db + 1) * 128],
                                             XN1[:, db * T + tc * 512:
                                                 db * T + tc * 512 + 512],
                                             start=(db == 0), stop=(db == NDB - 1)),
                                    wait=[eq, bank_free[bk]],
                                    inc="pe" if db == NDB - 1 else None)
                    dst = QG[:, ob * TQ + tc * 512: ob * TQ + tc * 512 + 512]
                    if tc == 0:
                        ev = P.op("vector", lambda E, d=dst, b=bk:
                                  E.tensor_copy(d, ps[b][0:128, 0:512]),
                                  wait=[last], inc="ve")
                    else:
                        ev = P.op("scalar", lambda E, d=dst, b=bk:
                                  E.activation(d, ps[b][0:128, 0:512], AF.Copy),
                                  wait=[last], inc="se")
                    bank_free[bk] = ev
                    qwar[par] = ev
            pump_gamma(16)
            projd = [P.ev("ve"), P.ev("se")]
            if debug is True and g == 0:
                P.dma("sync", dbg["d_k"], KG[:, 0:4 * T], wait=projd)
                P.dma("sync", dbg["d_v"], VG[:, 0:16 * 512], wait=projd)
                P.dma("sync", dbg["d_q"], QG[:, 0:4 * TQ], wait=projd)
                projd = projd + P.dma_fence("sync")
            for h in range(4):
                H = 4 * g + h
                last_z = {0: None, 1: None}
                last_sum = {2: None, 3: None}
                for p in range(16):
                    s = p if p < 8 else p - 8
                    col0 = 128 * s
                    pieces = ([(col0, 512), (512, 1024)] if col0 < 512
                              else [(col0, 1024)])
                    par = p % 2
                    evp = []
                    for pi, (c0, c1) in enumerate(pieces):
                        sb = 4 + 2 * par + pi
                        wd = c1 - c0
                        emm = P.op("tensor", lambda E, b=sb, h=h, p=p, c0=c0, wd=wd:
                                   E.matmul(ps[b][0:128, 0:wd],
                                            KG[:, h * T + p * 128:
                                               h * T + p * 128 + 128],
                                            QG[:, h * TQ + c0: h * TQ + c0 + wd],
                                            start=True, stop=True),
                                   wait=[bank_free[sb]] + projd, inc="pe")
                        ee = P.op("scalar", lambda E, b=sb, pr=par, c0=c0, wd=wd:
                                  E.activation(PT[pr][:, c0:c0 + wd],
                                               ps[b][0:128, 0:wd],
                                               AF.Exp, scale=ISQ),
                                  wait=[emm], inc="se")
                        bank_free[sb] = ee
                        evp.append(ee)
                    if p < 8:
                        emk = P.op("vector", lambda E, pr=par, c0=col0:
                                   E.tensor_tensor(PT[pr][:, c0:c0 + 128],
                                                   PT[pr][:, c0:c0 + 128],
                                                   TRI[:], AluOpType.mult),
                                   wait=[evp[0], ev_tri], inc="ve")
                    else:
                        emk = P.op("vector", lambda E, pr=par, c0=col0, k=p - 8:
                                   E.tensor_scalar(PT[pr][:, c0:c0 + 128],
                                                   PT[pr][:, c0:c0 + 128],
                                                   C_MSELB[:, k:k + 1], None,
                                                   AluOpType.mult),
                                   wait=[evp[0], ev_mselb], inc="ve")
                    for pi, (c0, c1) in enumerate(pieces):
                        wd = c1 - c0
                        zb_ = c0 // 512
                        zo = c0 - 512 * zb_
                        wz = [evp[pi], emk]
                        if p == 0:
                            wz += [bank_free[zb_], bank_free[2 + zb_]]
                        ez = P.op("tensor",
                                  lambda E, b=zb_, zo=zo, wd=wd, p=p, pr=par,
                                  c0=c0, h=h:
                                  E.matmul(ps[b][0:128, zo:zo + wd],
                                           VG[:, p * 512 + h * 128:
                                              p * 512 + h * 128 + 128],
                                           PT[pr][:, c0:c0 + wd],
                                           start=(p == 0), stop=(p == 15)),
                                  wait=wz, inc="pe")
                        last_z[zb_] = ez
                        es_ = P.op("tensor",
                                   lambda E, b=2 + zb_, zo=zo, wd=wd, p=p,
                                   pr=par, c0=c0:
                                   E.matmul(ps[b][0:128, zo:zo + wd],
                                            C_ONESM[:],
                                            PT[pr][:, c0:c0 + wd],
                                            start=(p == 0), stop=(p == 15)),
                                   wait=wz, inc="pe")
                        last_sum[2 + zb_] = es_
                e1s = P.op("vector", lambda E: E.tensor_copy(
                    SMS[:, 0:512], ps[2][0:128, 0:512]),
                    wait=[last_sum[2], last_z[0], last_z[1]], inc="ve")
                e2s = P.op("vector", lambda E: E.tensor_copy(
                    SMS[:, 512:1024], ps[3][0:128, 0:512]),
                    wait=[last_sum[3]], inc="ve")
                bank_free[2] = e2s
                bank_free[3] = e2s
                err = P.op("vector", lambda E: E.reciprocal(SMS[:], SMS[:]),
                           inc="ve")
                for c in range(2):
                    ezv = P.op("vector", lambda E, b=c, c=c, H=H:
                               E.tensor_tensor(
                                   ZB[:, H * TQ + c * 512: H * TQ + c * 512 + 512],
                                   ps[b][0:128, 0:512],
                                   SMS[:, c * 512:(c + 1) * 512],
                                   AluOpType.mult),
                               wait=[last_z[c], err], inc="ve")
                    bank_free[c] = ezv
                pump_gamma(18)
        zb_done = P.ev("ve")
        while gamma_pos[0] < len(gamma_jobs):
            pump_gamma(8)
        gamma_done = [P.ev("ve"), P.ev("se")]
        theta_block(0, 176, 176, gamma_done)
        e_th2 = theta_block(1, 352, 176, gamma_done)
        th_done = e_th2
        if debug is True:
            P.dma("sync", dbg["d_z"], ZB[:, 0:NH * TQ], wait=[zb_done])

        AO = buf(B0, BF16, NDB * TQ)
        X2 = buf(B0 + 32 * KB, F32, NDB * TQ)
        WOP = [buf(B0 + 136 * KB + i * 4 * KB, BF16, NDB * 128)
               for i in range(2)]
        AOSQ = buf(MS, BF16, TQ)
        last_st2 = None
        for ob in range(NDB):
            par = ob % 2
            ew = wpanel(WOP[par], wo, ob * 128, 128, (par,))
            for tc in range(2):
                bk = tc
                last = None
                for db in range(NDB):
                    last = P.op("tensor", lambda E, p=par, b=bk, db=db, tc=tc:
                                E.matmul(ps[b][0:128, 0:512],
                                         WOP[p][:, db * 128:(db + 1) * 128],
                                         ZB[:, db * TQ + tc * 512:
                                            db * TQ + tc * 512 + 512],
                                         start=(db == 0), stop=(db == NDB - 1)),
                                wait=[ew, zb_done, bank_free[bk]],
                                inc="pe" if db == NDB - 1 else None)
                dst = AO[:, ob * TQ + tc * 512: ob * TQ + tc * 512 + 512]
                eev = P.op("vector", lambda E, d=dst, b=bk: E.tensor_copy(
                    d, ps[b][0:128, 0:512]), wait=[last], inc="ve")
                bank_free[bk] = eev
                qwar[par] = eev
                esq = P.op("scalar", lambda E, d=dst, tc=tc: E.activation(
                    AOSQ[:, tc * 512:(tc + 1) * 512], d, AF.Square),
                    wait=[eev], inc="se")
                P.op("tensor", lambda E, d=dst, b=4 + tc, ob=ob:
                     E.matmul(ps[b][0:128, 0:512], C_ONESM[:], d,
                              start=(ob == 0), stop=(ob == NDB - 1)),
                     wait=[eev] + ([bank_free[4 + tc]] if ob == 0 else []))
                last_st2 = P.op("tensor", lambda E, b=6 + tc, tc=tc, ob=ob:
                                E.matmul(ps[b][0:128, 0:512], C_ONESM[:],
                                         AOSQ[:, tc * 512:(tc + 1) * 512],
                                         start=(ob == 0), stop=(ob == NDB - 1)),
                                wait=[esq] + ([bank_free[6 + tc]] if ob == 0 else []),
                                inc="pe")
        if debug is True:
            edbg = P.dma("sync", dbg["d_ao"], AO[:, 0:NDB * TQ],
                         wait=[P.ev("se")])
        BCA2 = buf(BC, BF16, TQ)
        BCB2 = buf(BC + 4 * KB, BF16, TQ)
        SCAC = buf(WS2, F32, TQ)
        SCBC = buf(WS2 + 4 * KB, F32, TQ)
        SCCC = buf(MS, F32, TQ)
        ebc2 = ln_bcast(TQ, [(4, 0, 512), (5, 512, 512)],
                        [(6, 0, 512), (7, 512, 512)],
                        SCAC, SCBC, SCCC, BCA2, BCB2, wait=[last_st2])
        XMY = [buf(WS + i * 4 * KB, F32, TQ) for i in range(2)]
        T1C = buf(PBo, F32, TQ)
        T2C = buf(WS2 + 4 * KB, F32, TQ)
        X2B = buf(MS + 2 * KB, BF16, TQ)
        X2SQ = buf(MS + 4 * KB, BF16, TQ)
        xmy_war = [None, None]
        last_st3 = None
        prev_shadow = [None]
        for ob in range(NDB):
            par = ob % 2
            ed = P.dma("sync", XMY[par][:], xt[ob * 128:(ob + 1) * 128, 0:TQ],
                       wait=[xmy_war[par]])
            e1 = P.op("vector", lambda E, ob=ob: E.tensor_tensor(
                T1C[:], AO[:, ob * TQ:(ob + 1) * TQ], BCA2[:], AluOpType.mult),
                wait=[ebc2], inc="ve")
            e2 = P.op("vector", lambda E: E.tensor_tensor(
                T2C[:], T1C[:], BCB2[:], AluOpType.subtract), inc="ve")
            e3 = P.op("vector", lambda E, ob=ob, p=par: E.tensor_tensor(
                X2[:, ob * TQ:(ob + 1) * TQ], T2C[:], XMY[p][:],
                AluOpType.add), wait=[ed], inc="ve")
            xmy_war[par] = e3
            e4 = P.op("vector", lambda E, ob=ob: E.tensor_copy(
                X2B[:], X2[:, ob * TQ:(ob + 1) * TQ]),
                wait=[prev_shadow[0]], inc="ve")
            e5 = P.op("scalar", lambda E: E.activation(X2SQ[:], X2B[:],
                                                       AF.Square),
                      wait=[e4], inc="se")
            for tc in range(2):
                P.op("tensor", lambda E, b=4 + tc, tc=tc, ob=ob:
                     E.matmul(ps[b][0:128, 0:512], C_ONESM[:],
                              X2B[:, tc * 512:(tc + 1) * 512],
                              start=(ob == 0), stop=(ob == NDB - 1)),
                     wait=[e4] + ([bank_free[4 + tc]] if ob == 0 else []))
                last_st3 = P.op("tensor", lambda E, b=6 + tc, tc=tc, ob=ob:
                                E.matmul(ps[b][0:128, 0:512], C_ONESM[:],
                                         X2SQ[:, tc * 512:(tc + 1) * 512],
                                         start=(ob == 0), stop=(ob == NDB - 1)),
                                wait=[e5] + ([bank_free[6 + tc]] if ob == 0 else []),
                                inc="pe")
            prev_shadow[0] = last_st3
            P.dma("sync", x2spill[ob * 128:(ob + 1) * 128, :],
                  X2[:, ob * TQ:(ob + 1) * TQ], wait=[e3])
        x2_done = P.ev("ve")
        x2spill_done = P.dma_fence("sync")
        if debug is True:
            P.dma("sync", dbg["d_x2"].rearrange("(db p) t -> p db t", p=128),
                  X2.rearrange("p (db t) -> p db t", db=NDB), wait=[x2_done])
        BCA3 = buf(BC, BF16, TQ)
        BCB3 = buf(BC + 4 * KB, BF16, TQ)
        ebc3 = ln_bcast(TQ, [(4, 0, 512), (5, 512, 512)],
                        [(6, 0, 512), (7, 512, 512)],
                        SCAC, SCBC, SCCC, BCA3, BCB3, wait=[last_st3])
        XN2 = buf(B0, BF16, NDB * TQ)
        dbg_bar = P.dma_fence("sync") if debug is True else []
        xn2_done = None
        for ob in range(NDB):
            e1 = P.op("vector", lambda E, ob=ob: E.tensor_tensor(
                T1C[:], X2[:, ob * TQ:(ob + 1) * TQ], BCA3[:], AluOpType.mult),
                wait=[ebc3] + dbg_bar, inc="ve")
            xn2_done = P.op("vector", lambda E, ob=ob: E.tensor_tensor(
                XN2[:, ob * TQ:(ob + 1) * TQ], T1C[:], BCB3[:],
                AluOpType.subtract), inc="ve")
        if debug in (True, "late"):
            P.dma("sync", dbg["d_xn2"], XN2[:, 0:NDB * TQ], wait=[xn2_done])
        if debug == "th":
            P.dma("sync", dbg["d_thv"][:, 0:6], buf(4, F32, 6),
                  wait=[th_done])

        if debug:
            for i in range(3):
                P.dma("sync", dbg["d_th"][0:1, i:i + 1], C_TH[2 * i][0:1, 0:1],
                      wait=[th_done])
                P.dma("sync", dbg["d_th"][0:1, 3 + i:4 + i], C_TOT[i][:],
                      wait=[th_done])

        HID = buf(B0 + 32 * KB, BF16, NHB * TQ)
        W1Q = [buf(B0 + 120 * KB + i * 8 * KB, BF16, NDB * 256)
               for i in range(2)]
        W2Q = [buf(B0 + 136 * KB + i * 8 * KB, BF16, NDB * 256)
               for i in range(2)]
        WF = [buf(WS + i * KB, F32, 256) for i in range(4)]
        SG = buf(MS, F32, 512)
        T2P = buf(MS + 2 * KB, BF16, 256)
        W3T = buf(PBo, BF16, 512)
        wf_war = [None] * 4
        sg_war = [None]
        sgn_war = [None]
        hid_done = None
        NPAN = 22
        qdone = {}

        def quant_panel(hp):
            par = hp % 2
            last = None
            for db in range(NDB):
                wpar = 2 * (db % 2)
                ed = P.dma("sync", WF[wpar][:],
                           w1[db * 128:(db + 1) * 128,
                              hp * 256:(hp + 1) * 256],
                           wait=[wf_war[wpar]])
                P.op("scalar", lambda E, p=wpar: E.activation(
                    SGN1[:, 0:256], WF[p][:], AF.Sign, bias=C_TH[1][:, 0:1]),
                    wait=[ed, th_done, sgn_war[0]], inc="se")
                es2 = P.op("scalar", lambda E, p=wpar: E.activation(
                    SGN2[:, 0:256], WF[p][:], AF.Sign, bias=C_TH[0][:, 0:1]),
                    inc="se")
                wf_war[wpar] = es2
                e1q = P.op("vector", lambda E, par=par, db=db: E.tensor_tensor(
                    W1Q[par][:, db * 256:(db + 1) * 256], SGN1[:, 0:256],
                    SGN2[:, 0:256], AluOpType.add), wait=[es2], inc="ve")
                sgn_war[0] = e1q
                wpar2 = 1 + 2 * (db % 2)
                ed2 = P.dma("sync", WF[wpar2][:],
                            w2[db * 128:(db + 1) * 128,
                               hp * 256:(hp + 1) * 256],
                            wait=[wf_war[wpar2]])
                P.op("vector", lambda E, p=wpar2: E.tensor_scalar(
                    T2P[:], WF[p][:], C_TH[3][:, 0:1], -1.0,
                    AluOpType.is_gt, AluOpType.add),
                    wait=[ed2, th_done], inc="ve")
                last = P.op("vector", lambda E, p=wpar2, par=par, db=db:
                            E.scalar_tensor_tensor(
                                W2Q[par][:, db * 256:(db + 1) * 256],
                                WF[p][:], C_TH[2][:, 0:1], T2P[:],
                                AluOpType.is_ge, AluOpType.add),
                            inc="ve")
                wf_war[wpar2] = last
            qdone[hp] = [last, P.ev("se")]

        quant_panel(0)
        for hp in range(NPAN):
            if hp + 1 < NPAN:
                quant_panel(hp + 1)
            par = hp % 2
            for hb in range(2):
                hglob = hp * 2 + hb
                for tc in range(2):
                    b1 = 2 * tc
                    b2 = b1 + 1
                    l1 = l2 = None
                    for db in range(NDB):
                        l1 = P.op("tensor",
                                  lambda E, b=b1, db=db, hb=hb, tc=tc, par=par:
                                  E.matmul(ps[b][0:128, 0:512],
                                           W1Q[par][:, db * 256 + hb * 128:
                                               db * 256 + hb * 128 + 128],
                                           XN2[:, db * TQ + tc * 512:
                                               db * TQ + tc * 512 + 512],
                                           start=(db == 0), stop=(db == NDB - 1)),
                                  wait=[qdone[hp], xn2_done, bank_free[b1]],
                                  inc="pe" if db == NDB - 1 else None)
                        l2 = P.op("tensor",
                                  lambda E, b=b2, db=db, hb=hb, tc=tc, par=par:
                                  E.matmul(ps[b][0:128, 0:512],
                                           W2Q[par][:, db * 256 + hb * 128:
                                               db * 256 + hb * 128 + 128],
                                           XN2[:, db * TQ + tc * 512:
                                               db * TQ + tc * 512 + 512],
                                           start=(db == 0), stop=(db == NDB - 1)),
                                  wait=[bank_free[b2]],
                                  inc="pe" if db == NDB - 1 else None)
                    eg1 = P.op("scalar", lambda E, b=b1: E.activation(
                        SG[:], ps[b][0:128, 0:512], AF.Sigmoid, scale=0.5),
                        wait=[l1, sg_war[0]], inc="se")
                    bank_free[b1] = eg1
                    eh = P.op("vector", lambda E, b=b2, hg=hglob, tc=tc:
                              E.tensor_tensor(
                                  HID[:, hg * TQ + tc * 512:
                                      hg * TQ + tc * 512 + 512],
                                  SG[:], ps[b][0:128, 0:512], AluOpType.mult),
                              wait=[eg1, l2], inc="ve")
                    bank_free[b2] = eh
                    sg_war[0] = eh
                    hid_done = eh
        if debug in (True, "late"):
            P.dma("sync", dbg["d_hid"], HID[:, 0:NHB * TQ], wait=[hid_done])

        FB = buf(B0, BF16, NDB * TQ)
        W3F = buf(B0 + 120 * KB, F32, NHB * 128)
        W3Q = buf(B0 + 142 * KB, BF16, NHB * 128)
        T3E = buf(MS + 2 * KB, BF16, 1408)
        FSQ = buf(MS, BF16, TQ)
        X2S = [buf(WS + i * 4 * KB, F32, TQ) for i in range(2)]
        SCR = [buf(WS2 + i * 4 * KB, F32, TQ) for i in range(2)]
        QTR = 11 * 128
        w3f_war = [None] * 4
        w3q_war = [None]
        last_st4 = None
        q3e = {}
        for ob in range(NDB):
            edq = []
            for qtr in range(4):
                srcq = AP(w3.tensor, ob * 128 + qtr * 11 * 128 * D,
                          [[D, 128], [128 * D, 11], [1, 128]])
                dstq = W3F[:, qtr * QTR:(qtr + 1) * QTR]
                edq.append(P.dma(
                    "sync", dstq.rearrange("p (hb c) -> p hb c", hb=11), srcq,
                    wait=[w3f_war[qtr], hid_done]))
            for qtr in range(4):
                P.op("vector", lambda E, q=qtr: E.tensor_scalar(
                    T3E[:], W3F[:, q * QTR:(q + 1) * QTR], C_TH[5][:, 0:1],
                    -1.0, AluOpType.is_gt, AluOpType.add),
                    wait=[edq[qtr], th3_done, w3q_war[0]], inc="ve")
                ev = P.op("vector", lambda E, q=qtr: E.scalar_tensor_tensor(
                    W3Q[:, q * QTR:(q + 1) * QTR],
                    W3F[:, q * QTR:(q + 1) * QTR], C_TH[4][:, 0:1], T3E[:],
                    AluOpType.is_ge, AluOpType.add), inc="ve")
                q3e[(ob, qtr)] = ev
                w3f_war[qtr] = ev
            for tc in range(2):
                bk = tc
                last = None
                for hb in range(NHB):
                    last = P.op("tensor", lambda E, b=bk, hb=hb, tc=tc, ob=ob:
                                E.matmul(ps[b][0:128, 0:512],
                                         W3Q[:, hb * 128:(hb + 1) * 128],
                                         HID[:, hb * TQ + tc * 512:
                                             hb * TQ + tc * 512 + 512],
                                         start=(hb == 0), stop=(hb == NHB - 1)),
                                wait=[q3e[(ob, min(hb // 11, 3))], hid_done,
                                      bank_free[bk]],
                                inc="pe" if hb == NHB - 1 else None)
                dst = FB[:, ob * TQ + tc * 512: ob * TQ + tc * 512 + 512]
                eev = P.op("scalar", lambda E, d=dst, b=bk: E.activation(
                    d, ps[b][0:128, 0:512], AF.Copy), wait=[last], inc="se")
                bank_free[bk] = eev
                if tc == 1:
                    w3q_war[0] = eev
                esq = P.op("scalar", lambda E, d=dst, tc=tc: E.activation(
                    FSQ[:, tc * 512:(tc + 1) * 512], d, AF.Square),
                    wait=[eev], inc="se")
                P.op("tensor", lambda E, d=dst, b=4 + tc, ob=ob:
                     E.matmul(ps[b][0:128, 0:512], C_ONESM[:], d,
                              start=(ob == 0), stop=(ob == NDB - 1)),
                     wait=[eev] + ([bank_free[4 + tc]] if ob == 0 else []))
                last_st4 = P.op("tensor", lambda E, b=6 + tc, tc=tc, ob=ob:
                                E.matmul(ps[b][0:128, 0:512], C_ONESM[:],
                                         FSQ[:, tc * 512:(tc + 1) * 512],
                                         start=(ob == 0), stop=(ob == NDB - 1)),
                                wait=[esq] + ([bank_free[6 + tc]] if ob == 0 else []),
                                inc="pe")
        if debug in (True, "late"):
            P.dma("sync", dbg["d_f"], FB[:, 0:NDB * TQ], wait=[P.ev("se")])
        BCA4 = buf(BC, BF16, TQ)
        BCB4 = buf(BC + 4 * KB, BF16, TQ)
        SCCE = buf(MS, F32, TQ)
        ebc4 = ln_bcast(TQ, [(4, 0, 512), (5, 512, 512)],
                        [(6, 0, 512), (7, 512, 512)],
                        SCAC, SCBC, SCCE, BCA4, BCB4, wait=[last_st4])
        x2s_war = [None, None]
        scr_war = [None, None]
        out_ev = None
        for ob in range(NDB):
            par = ob % 2
            ed = P.dma("sync", X2S[par][:],
                       x2spill[ob * 128:(ob + 1) * 128, :],
                       wait=[x2s_war[par], hid_done, x2spill_done])
            e1 = P.op("vector", lambda E, ob=ob: E.tensor_tensor(
                T1C[:], FB[:, ob * TQ:(ob + 1) * TQ], BCA4[:], AluOpType.mult),
                wait=[ebc4], inc="ve")
            e2 = P.op("vector", lambda E, p=par: E.tensor_tensor(
                SCR[p][:], T1C[:], BCB4[:], AluOpType.subtract),
                wait=[scr_war[par]], inc="ve")
            e3 = P.op("vector", lambda E, p=par: E.tensor_tensor(
                SCR[p][:], SCR[p][:], X2S[p][:], AluOpType.add),
                wait=[ed], inc="ve")
            x2s_war[par] = e3
            out_ev = P.dma("sync", out[ob * 128:(ob + 1) * 128, :],
                           SCR[par][:], wait=[e3])
            scr_war[par] = out_ev
        P.op("sync", None, wait=P.dma_fence("sync") + P.dma_fence("gpsimd"))

        blk = es.enter_context(nc.Block())
        P.build(blk)
    return nc


_NC_CACHE = {}


def get_nc(debug=False):
    if debug not in _NC_CACHE:
        _NC_CACHE[debug] = build_program(debug)
    return _NC_CACHE[debug]


def host_inputs(L0_attn_x, L0_attn_Wq, L0_attn_Wk, L0_attn_Wv, L0_attn_Wo,
                L0_ffn_w1, L0_ffn_w2, L0_ffn_w3):
    x = np.asarray(L0_attn_x)
    wqT = np.ascontiguousarray(np.asarray(L0_attn_Wq).T)
    wkT = np.ascontiguousarray(np.asarray(L0_attn_Wk).T)
    wvT = np.ascontiguousarray(np.asarray(L0_attn_Wv).T)
    woT = np.ascontiguousarray(np.asarray(L0_attn_Wo).T)
    w1T = np.ascontiguousarray(np.asarray(L0_ffn_w1).T)
    w2T = np.ascontiguousarray(np.asarray(L0_ffn_w2).T)
    w3T = np.ascontiguousarray(np.asarray(L0_ffn_w3).T)
    tri = np.triu(np.ones((128, 128), np.float32))
    msel_h = []
    for mine, peer in ((TILES_A, TILES_B), (TILES_B, TILES_A)):
        m = np.zeros((1, 8), np.float32)
        for k in range(8):
            s_real = sum(1 for i in mine if i < peer[k])
            m[0, k] = 0.0 if s_real > k else 1.0
        msel_h.append(m)
    perms = []
    for mine, peer in ((TILES_A, TILES_B), (TILES_B, TILES_A)):
        cols = []
        for i in mine + peer:
            cols.extend(range(i * 128, (i + 1) * 128))
        perms.append(np.array(cols))
    in_maps = []
    for core in range(8):
        b = core // 2
        half = core % 2
        xtp = np.ascontiguousarray(x[b].T[:, perms[half]])
        in_maps.append({
            "xt": xtp, "wq": wqT, "wk": wkT, "wv": wvT, "wo": woT,
            "w1": w1T, "w2": w2T, "w3": w3T, "tri": tri,
            "msel": msel_h[half],
        })
    return in_maps


def kernel(L0_attn_x, L0_attn_Wq, L0_attn_Wk, L0_attn_Wv, L0_attn_Wo,
           L0_ffn_w1, L0_ffn_w2, L0_ffn_w3):
    in_maps = host_inputs(L0_attn_x, L0_attn_Wq, L0_attn_Wk, L0_attn_Wv,
                          L0_attn_Wo, L0_ffn_w1, L0_ffn_w2, L0_ffn_w3)
    nc = get_nc(False)
    res = run_bass_kernel_spmd(nc, in_maps, core_ids=list(range(8)))
    y = np.empty((4, T, D), np.float32)
    for b in range(4):
        yT = np.empty((D, T), np.float32)
        for half, tiles in ((0, TILES_A), (1, TILES_B)):
            o = np.asarray(res.results[2 * b + half]["out"])
            for a, i in enumerate(tiles):
                yT[:, i * 128:(i + 1) * 128] = o[:, a * 128:(a + 1) * 128]
        y[b] = yT.T
    return y


# version 06
# speedup vs baseline: 2.0687x; 1.0542x over previous
import math
import contextlib
import numpy as np

import concourse.bass as bass
import concourse.mybir as mybir
from concourse.bass_utils import run_bass_kernel_spmd
from concourse.alu_op_type import AluOpType
from concourse.ap import AP

F32 = mybir.dt.float32
BF16 = mybir.dt.bfloat16
AF = mybir.ActivationFunctionType
AX = mybir.AxisListType

D = 2048
T = 2048
TQ = 1024
NDB = 16
NH = 16
HF = 5632
NHB = 44
EPS = 1e-5
QEPS = 1e-8
ISQ = 1.0 / math.sqrt(128.0)
KB = 1024

TILES_A = [0, 3, 4, 7, 8, 11, 12, 15]
TILES_B = [1, 2, 5, 6, 9, 10, 13, 14]

NDMASEM = 8
SEM_ENG = {"pe": "tensor", "se": "scalar", "ve": "vector", "ge": "gpsimd"}
for _q in ("a", "b"):
    for _i in range(NDMASEM):
        SEM_ENG[f"ld{_q}{_i}"] = None
ENGS = ("sync", "gpsimd", "tensor", "scalar", "vector")


class Plan:

    def __init__(self, nc, es):
        self.nc = nc
        self.q = {e: [] for e in ENGS}
        self.sems = {}
        self.cnt = {}
        self.seen = {e: {} for e in ENGS}
        self.dseq = {"a": 0, "b": 0}
        for s in SEM_ENG:
            self.sems[s] = es.enter_context(nc.semaphore(s))
            self.cnt[s] = 0

    def ev(self, sem):
        return (sem, self.cnt[sem])

    def op(self, eng, fn, wait=(), inc=None, inc_by=1):
        waits = []
        flat = []
        for w in wait:
            if isinstance(w, list):
                flat.extend(w)
            else:
                flat.append(w)
        for w in flat:
            if w is None:
                continue
            s, v = w
            if v <= 0:
                continue
            if SEM_ENG[s] == eng:
                continue
            if self.seen[eng].get(s, 0) >= v:
                continue
            self.seen[eng][s] = v
            waits.append((s, v))
        evo = None
        if inc is not None:
            self.cnt[inc] += inc_by
            evo = (inc, self.cnt[inc])
        self.q[eng].append((waits, fn, (inc, inc_by) if inc else None))
        return evo

    def dma(self, eng, out, in_, wait=()):
        qn = "b" if eng == "gpsimd" else "a"
        k = self.dseq[qn] % NDMASEM
        self.dseq[qn] += 1
        sem = f"ld{qn}{k}"
        waits = list(wait) + [(sem, self.cnt[sem])]
        return self.op(eng,
                       lambda E, o=out, i=in_: E.dma_start(out=o, in_=i),
                       waits, inc=sem, inc_by=16)

    def dma_fence(self, eng):
        qn = "b" if eng == "gpsimd" else "a"
        return [(f"ld{qn}{k}", self.cnt[f"ld{qn}{k}"]) for k in range(NDMASEM)]

    def barrier(self):
        return [(s, self.cnt[s]) for s in self.cnt]

    def build(self, blk):
        for eng in ENGS:
            qq = self.q[eng]
            sems = self.sems

            def runner(E, qq=qq):
                for waits, fn, inc in qq:
                    for (s, v) in waits:
                        E.wait_ge(sems[s], v)
                    if fn is None:
                        continue
                    ins = fn(E)
                    if inc is not None:
                        s, k = inc
                        ins.then_inc(sems[s], k)
            getattr(blk, eng)(runner)


def build_program(debug=False):
    nc = bass.Bass("TRN2", target_bir_lowering=False)

    xt = nc.dram_tensor("xt", [D, T], F32, kind="ExternalInput").ap()
    wq = nc.dram_tensor("wq", [D, D], F32, kind="ExternalInput").ap()
    wk = nc.dram_tensor("wk", [D, D], F32, kind="ExternalInput").ap()
    wv = nc.dram_tensor("wv", [D, D], F32, kind="ExternalInput").ap()
    wo = nc.dram_tensor("wo", [D, D], F32, kind="ExternalInput").ap()
    w1 = nc.dram_tensor("w1", [D, HF], F32, kind="ExternalInput").ap()
    w2 = nc.dram_tensor("w2", [D, HF], F32, kind="ExternalInput").ap()
    w3 = nc.dram_tensor("w3", [HF, D], F32, kind="ExternalInput").ap()
    tri = nc.dram_tensor("tri", [128, 128], F32, kind="ExternalInput").ap()
    msel = nc.dram_tensor("msel", [1, 8], F32, kind="ExternalInput").ap()
    out = nc.dram_tensor("out", [D, TQ], F32, kind="ExternalOutput").ap()
    x2spill = nc.dram_tensor("x2spill", [D, TQ], F32, kind="Internal").ap()
    dbg = {}
    if debug == "th":
        dbg["d_th"] = nc.dram_tensor("d_th", [1, 8], F32,
                                     kind="ExternalOutput").ap()
        dbg["d_thv"] = nc.dram_tensor("d_thv", [128, 8], F32,
                                      kind="ExternalOutput").ap()


    elif debug == "late":
        for name, shape, dt in [
            ("d_xn2", [128, NDB * TQ], BF16),
            ("d_hid", [128, NHB * TQ], BF16), ("d_f", [128, NDB * TQ], BF16),
            ("d_th", [1, 8], F32),
        ]:
            dbg[name] = nc.dram_tensor(name, shape, dt, kind="ExternalOutput").ap()
    elif debug:
        for name, shape, dt in [
            ("d_xn1", [D, T], BF16), ("d_k", [128, 4 * T], BF16),
            ("d_v", [128, 16 * 512], BF16), ("d_q", [128, 4 * TQ], BF16),
            ("d_z", [128, NH * TQ], BF16), ("d_ao", [128, NDB * TQ], BF16),
            ("d_x2", [D, TQ], F32), ("d_xn2", [128, NDB * TQ], BF16),
            ("d_hid", [128, NHB * TQ], BF16), ("d_f", [128, NDB * TQ], BF16),
            ("d_th", [1, 8], F32),
        ]:
            dbg[name] = nc.dram_tensor(name, shape, dt, kind="ExternalOutput").ap()

    es = contextlib.ExitStack()
    with es:
        arena = es.enter_context(
            nc.sbuf_tensor("arena", [128, 207 * KB + 256], mybir.dt.uint8))
        ar = arena[:]

        def buf(off_b, dtype, cols, p0=0, p1=128):
            nb = mybir.dt.size(dtype) * cols
            return ar[p0:p1, off_b:off_b + nb].bitcast(dtype)

        C_ONES = buf(0, F32, 1)
        C_TH = [buf(4 + 4 * i, F32, 1) for i in range(6)]
        C_GAC = [buf(28 + 4 * i, F32, 1) for i in range(3)]
        C_GRD = buf(40, F32, 1)
        C_GRD2 = buf(44, F32, 1)
        C_ONESR = buf(64, F32, 128, 0, 1)
        C_ONESRB = buf(576, BF16, 128, 0, 1)
        C_MSELR = buf(832, F32, 8, 0, 1)
        C_MSELB = buf(864, F32, 8)
        C_ONESB = buf(896, BF16, 1)
        C_ONESM = buf(1024, BF16, 128)
        C_EPS = buf(900, F32, 1)
        C_TOT = [buf(904 + 4 * i, F32, 1, 0, 1) for i in range(3)]
        C_EPS4 = buf(940, F32, 1)
        C_THR = [buf(916 + 4 * i, F32, 1, 0, 1) for i in range(3)]
        C_THNR = [buf(928 + 4 * i, F32, 1, 0, 1) for i in range(3)]
        TRI = buf(1536, F32, 128)
        RSUM = buf(2 * KB, F32, 1024, 0, 1)
        RREC = buf(6 * KB, F32, 1024, 0, 1)

        def mkrows(base_b, tmax):
            def rowf(slot, cols):
                return buf(base_b + slot * tmax * 4, F32, cols, 0, 1)
            return rowf

        MS = 10 * KB
        PBo = 16 * KB
        BC = 22 * KB
        WS = 30 * KB
        WS2 = 38 * KB
        B0 = 46 * KB
        GW = 198 * KB
        GDUMP = 202 * KB
        GACALL = 204 * KB

        nc.const_aps.register(F32, EPS, C_EPS[:, 0:1])
        nc.const_aps.register(F32, 4 * EPS, C_EPS4[:, 0:1])

        ps = [es.enter_context(nc.psum_tensor(f"ps{i}", [128, 512], F32))
              for i in range(8)]
        P = Plan(nc, es)
        bank_free = {b: None for b in range(8)}

        P.op("vector", lambda E: E.memset(C_ONES[:], 1.0))
        P.op("vector", lambda E: E.memset(C_ONESR[:], 1.0))
        P.op("vector", lambda E: E.memset(C_ONESRB[:], 1.0))
        P.op("vector", lambda E: E.memset(C_ONESB[:], 1.0))
        P.op("vector", lambda E: E.memset(C_ONESM[:], 1.0))
        P.op("vector", lambda E: E.memset(C_EPS[:], EPS))
        P.op("vector", lambda E: E.memset(C_EPS4[:], 4 * EPS))
        for g_ in C_GAC:
            P.op("vector", lambda E, g=g_: E.memset(g[:], 0.0))
        ev_init_v = P.op("vector", lambda E: E.engine_nop(), inc="ve")
        ev_tri = P.dma("sync", TRI[:], tri)
        ev_msel = P.dma("sync", C_MSELR[:], msel)
        em = P.op("tensor", lambda E: E.matmul(ps[7][0:128, 0:8], C_ONESR[:],
                                               C_MSELR[:], start=True, stop=True),
                  wait=[ev_msel, ev_init_v], inc="pe")
        ev_mselb = P.op("vector",
                        lambda E: E.tensor_copy(C_MSELB[:], ps[7][0:128, 0:8]),
                        wait=[em], inc="ve")
        bank_free[7] = ev_mselb
        init_bar = P.barrier()

        def ln_bcast(Tn, s1_banks, s2_banks, sa, sb_, sc, BCA, BCB,
                     wait=(), eps_val=EPS):
            for bk, o, w in s1_banks:
                e = P.op("vector", lambda E, b=bk, o=o, w=w: E.tensor_scalar(
                    sa[:, o:o + w], ps[b][0:128, 0:w], 1.0 / D, None,
                    AluOpType.mult), wait=wait, inc="ve")
                bank_free[bk] = e
            for bk, o, w in s2_banks:
                e = P.op("vector", lambda E, b=bk, o=o, w=w: E.tensor_scalar(
                    sb_[:, o:o + w], ps[b][0:128, 0:w], 1.0 / D, None,
                    AluOpType.mult), wait=wait, inc="ve")
                bank_free[bk] = e
            P.op("vector", lambda E, Tn=Tn: E.tensor_tensor(
                sc[:, 0:Tn], sa[:, 0:Tn], sa[:, 0:Tn], AluOpType.mult))
            ev = P.op("vector", lambda E, Tn=Tn: E.tensor_tensor(
                sb_[:, 0:Tn], sb_[:, 0:Tn], sc[:, 0:Tn], AluOpType.subtract),
                inc="ve")
            P.op("scalar", lambda E, Tn=Tn, e=eps_val: E.activation(
                sc[:, 0:Tn], sb_[:, 0:Tn], AF.Ln, bias=e), wait=[ev])
            er = P.op("scalar", lambda E, Tn=Tn: E.activation(
                sc[:, 0:Tn], sc[:, 0:Tn], AF.Exp, scale=-0.5), inc="se")
            ea = P.op("vector", lambda E, Tn=Tn: E.tensor_copy(
                BCA[:, 0:Tn], sc[:, 0:Tn]), wait=[er], inc="ve")
            eb = P.op("vector", lambda E, Tn=Tn: E.tensor_tensor(
                BCB[:, 0:Tn], sa[:, 0:Tn], sc[:, 0:Tn], AluOpType.mult),
                inc="ve")
            return eb

        GACC = buf(GACALL, F32, 528)
        gw_par = [0]
        gw_war = [None, None]
        gamma_jobs = []
        for hb in range(NHB):
            for c in range(4):
                gamma_jobs.append((2, w3[hb * 128:(hb + 1) * 128,
                                         c * 512:(c + 1) * 512]))
        for widx, w_ in ((0, w1), (1, w2)):
            for db in range(NDB):
                for hp in range(11):
                    gamma_jobs.append((widx, w_[db * 128:(db + 1) * 128,
                                                hp * 512:(hp + 1) * 512]))
        gamma_pos = [0]
        GDMP = buf(PBo + 4 * KB, BF16, 512)
        SGN1 = buf(GDUMP, BF16, 512)
        SGN2 = buf(GDUMP + 1 * KB, BF16, 512)
        T3P = buf(MS + 3 * KB, BF16, 512)
        W3T = buf(MS + 4 * KB, BF16, 512)

        def pump_gamma(n):
            for _ in range(n):
                i = gamma_pos[0]
                if i >= len(gamma_jobs):
                    return
                widx, src = gamma_jobs[i]
                gamma_pos[0] += 1
                par = gw_par[0] % 2
                gw_par[0] += 1
                gt = buf(GW + par * 2 * KB, F32, 512)
                ed = P.dma("sync", gt[:], src, wait=[gw_war[par]])
                if i % 2 == 0 or gamma_pos[0] > 352:
                    er = P.op("vector", lambda E, g=gt, i=i: E.tensor_reduce(
                        GACC[:, i:i + 1], g[:], AX.X, AluOpType.add,
                        apply_absolute_value=True), wait=[ed], inc="ve")
                else:
                    er = P.op("scalar", lambda E, g=gt, i=i: E.activation(
                        GDMP[:], g[:], AF.Abs, accum_out=GACC[:, i:i + 1]),
                        wait=[ed], inc="se")
                gw_war[par] = er

        XN1 = buf(B0, BF16, NDB * T)
        XB = [buf(B0 + 64 * KB + i * 4 * KB, BF16, T) for i in range(2)]
        XSQ = buf(MS, BF16, T)
        sq_done = [None, None]
        for db in range(NDB):
            par = db % 2
            ed = P.dma("gpsimd", XB[par][:], xt[db * 128:(db + 1) * 128, :],
                       wait=[sq_done[par]] + (init_bar if db < 2 else []))
            esq = P.op("scalar", lambda E, p=par: E.activation(
                XSQ[:], XB[p][:], AF.Square),
                wait=[ed, sq_done[1 - par]], inc="se")
            last = None
            for c in range(4):
                P.op("tensor", lambda E, p=par, c=c, db=db:
                     E.matmul(ps[c][0:128, 0:512], C_ONESM[:],
                              XB[p][:, c * 512:(c + 1) * 512],
                              start=(db == 0), stop=(db == NDB - 1)),
                     wait=[ed] + (init_bar if db == 0 else []))
                last = P.op("tensor", lambda E, c=c, db=db:
                            E.matmul(ps[4 + c][0:128, 0:512], C_ONESM[:],
                                     XSQ[:, c * 512:(c + 1) * 512],
                                     start=(db == 0), stop=(db == NDB - 1)),
                            wait=[esq], inc="pe" if c == 3 else None)
            sq_done[par] = last
            pump_gamma(6)
        stats_done = P.ev("pe")
        BCA1 = buf(BC, BF16, T)
        BCB1 = buf(BC + 4 * KB, BF16, T)
        SCA1 = buf(B0 + 88 * KB, F32, T)
        SCB1 = buf(B0 + 96 * KB, F32, T)
        SCC1 = buf(B0 + 104 * KB, F32, T)
        ebc1 = ln_bcast(T, [(c, c * 512, 512) for c in range(4)],
                        [(4 + c, c * 512, 512) for c in range(4)],
                        SCA1, SCB1, SCC1, BCA1, BCB1, wait=[stats_done])
        XF = [buf(B0 + 72 * KB + i * 8 * KB, F32, T) for i in range(2)]
        T1A = buf(WS, F32, T)
        xf_war = [None, None]
        t1_war = [None]
        xn1_done = None
        for db in range(NDB):
            par = db % 2
            ed = P.dma("sync", XF[par][:], xt[db * 128:(db + 1) * 128, :],
                       wait=[xf_war[par]])
            e1 = P.op("vector", lambda E, p=par: E.tensor_tensor(
                T1A[:], XF[p][:], BCA1[:], AluOpType.mult),
                wait=[ed, ebc1], inc="ve")
            xf_war[par] = e1
            xn1_done = P.op("vector", lambda E, db=db: E.tensor_tensor(
                XN1[:, db * T:(db + 1) * T], T1A[:], BCB1[:],
                AluOpType.subtract), inc="ve")
            pump_gamma(6)
        if debug is True:
            P.dma("sync", dbg["d_xn1"].rearrange("(db p) t -> p db t", p=128),
                  XN1.rearrange("p (db t) -> p db t", db=NDB),
                  wait=[xn1_done])

        def theta_block(i, c0, n, gate):
            eg = P.op("tensor", lambda E, c0=c0, n=n: E.matmul(
                ps[7][0:1, 0:n], C_ONES[:], GACC[:, c0:c0 + n],
                start=True, stop=True),
                wait=list(gate) + [bank_free[7]], inc="pe")
            ecr = P.op("vector", lambda E, i=i, n=n: E.tensor_reduce(
                C_TOT[i][:], ps[7][0:1, 0:n], AX.X, AluOpType.add),
                wait=[eg], inc="ve")
            bank_free[7] = ecr
            Nw = D * HF
            eth = P.op("gpsimd", lambda E, i=i, N=Nw: E.tensor_scalar(
                C_THR[i][:], C_TOT[i][:], 0.5 / N, 0.5 * QEPS,
                AluOpType.mult, AluOpType.add), wait=[ecr], inc="ge")
            ethn = P.op("gpsimd", lambda E, i=i, N=Nw: E.tensor_scalar(
                C_THNR[i][:], C_TOT[i][:], -0.5 / N, -0.5 * QEPS,
                AluOpType.mult, AluOpType.add), inc="ge")
            eb1 = P.op("tensor", lambda E, i=i: E.matmul(
                ps[7][0:128, 0:1], C_ONESR[:], C_THR[i][:], start=True,
                stop=True), wait=[eth, bank_free[7]], inc="pe")
            ec1 = P.op("vector", lambda E, i=i: E.tensor_copy(
                C_TH[2 * i][:], ps[7][0:128, 0:1]), wait=[eb1], inc="ve")
            bank_free[7] = ec1
            eb2 = P.op("tensor", lambda E, i=i: E.matmul(
                ps[7][0:128, 0:1], C_ONESR[:], C_THNR[i][:], start=True,
                stop=True), wait=[ethn, ec1], inc="ve" if False else "pe")
            ec2 = P.op("vector", lambda E, i=i: E.tensor_copy(
                C_TH[2 * i + 1][:], ps[7][0:128, 0:1]), wait=[eb2], inc="ve")
            bank_free[7] = ec2
            return ec2

        assert gamma_pos[0] >= 176, gamma_pos[0]
        th3_done = theta_block(2, 0, 176, [P.ev("ve"), P.ev("se")])

        KG = buf(B0 + 64 * KB, BF16, 4 * T)
        VG = buf(B0 + 80 * KB, BF16, 16 * 512)
        QG = buf(B0 + 96 * KB, BF16, 4 * TQ)
        ZB = buf(B0 + 104 * KB, BF16, NH * TQ)
        WPQ = [buf(B0 + 136 * KB + i * 4 * KB, BF16, NDB * 128)
               for i in range(4)]
        WPV = [buf(B0 + 136 * KB + i * 8 * KB, BF16, NDB * 256)
               for i in range(2)]
        PT = [buf(PBo + i * 2 * KB, BF16, TQ) for i in range(2)]
        SMS = buf(MS, F32, TQ)
        qwar = {0: None, 1: None, 2: None, 3: None}

        def wpanel(dst, wsrc, c0, width, quarters, extra=()):
            src = AP(wsrc.tensor, c0, [[D, 128], [128 * D, NDB], [1, width]])
            waits = [qwar[q] for q in quarters] + list(extra)
            e = P.dma("gpsimd", dst.rearrange("p (db c) -> p db c", db=NDB),
                      src, wait=waits)
            return e

        for g in range(4):
            for ob in range(4):
                par = ob % 2
                ek = wpanel(WPQ[par], wk, (4 * g + ob) * 128, 128, (par,))
                for tc in range(4):
                    bk = tc % 2
                    last = None
                    for db in range(NDB):
                        last = P.op("tensor", lambda E, p=par, b=bk, db=db, tc=tc:
                                    E.matmul(ps[b][0:128, 0:512],
                                             WPQ[p][:, db * 128:(db + 1) * 128],
                                             XN1[:, db * T + tc * 512:
                                                 db * T + tc * 512 + 512],
                                             start=(db == 0), stop=(db == NDB - 1)),
                                    wait=[ek, xn1_done, bank_free[bk]],
                                    inc="pe" if db == NDB - 1 else None)
                    dst = KG[:, ob * T + tc * 512: ob * T + tc * 512 + 512]
                    if tc % 2 == 0:
                        ev = P.op("vector", lambda E, d=dst, b=bk:
                                  E.tensor_copy(d, ps[b][0:128, 0:512]),
                                  wait=[last], inc="ve")
                    else:
                        ev = P.op("scalar", lambda E, d=dst, b=bk:
                                  E.activation(d, ps[b][0:128, 0:512], AF.Copy),
                                  wait=[last], inc="se")
                    bank_free[bk] = ev
                    qwar[par] = ev
            for oc in range(2):
                par = oc % 2
                evw = wpanel(WPV[par], wv, g * 512 + oc * 256, 256,
                             (2 * par, 2 * par + 1))
                for tb in range(16):
                    bk = 2 + tb % 2
                    last = None
                    for db in range(NDB):
                        last = P.op("tensor", lambda E, p=par, b=bk, db=db, tb=tb:
                                    E.matmul(ps[b][0:128, 0:256],
                                             XN1[:, db * T + tb * 128:
                                                 db * T + tb * 128 + 128],
                                             WPV[p][:, db * 256:(db + 1) * 256],
                                             start=(db == 0), stop=(db == NDB - 1)),
                                    wait=[evw, bank_free[bk]],
                                    inc="pe" if db == NDB - 1 else None)
                    dst = VG[:, tb * 512 + oc * 256: tb * 512 + oc * 256 + 256]
                    if tb % 2 == 0:
                        ev = P.op("vector", lambda E, d=dst, b=bk:
                                  E.tensor_copy(d, ps[b][0:128, 0:256]),
                                  wait=[last], inc="ve")
                    else:
                        ev = P.op("scalar", lambda E, d=dst, b=bk:
                                  E.activation(d, ps[b][0:128, 0:256], AF.Copy),
                                  wait=[last], inc="se")
                    bank_free[bk] = ev
                    qwar[2 * par] = ev
                    qwar[2 * par + 1] = ev
            for ob in range(4):
                par = 2 + ob % 2
                eq = wpanel(WPQ[par], wq, (4 * g + ob) * 128, 128, (par,))
                for tc in range(2):
                    bk = 4 + tc
                    last = None
                    for db in range(NDB):
                        last = P.op("tensor", lambda E, p=par, b=bk, db=db, tc=tc:
                                    E.matmul(ps[b][0:128, 0:512],
                                             WPQ[p][:, db * 128:(db + 1) * 128],
                                             XN1[:, db * T + tc * 512:
                                                 db * T + tc * 512 + 512],
                                             start=(db == 0), stop=(db == NDB - 1)),
                                    wait=[eq, bank_free[bk]],
                                    inc="pe" if db == NDB - 1 else None)
                    dst = QG[:, ob * TQ + tc * 512: ob * TQ + tc * 512 + 512]
                    if tc == 0:
                        ev = P.op("vector", lambda E, d=dst, b=bk:
                                  E.tensor_copy(d, ps[b][0:128, 0:512]),
                                  wait=[last], inc="ve")
                    else:
                        ev = P.op("scalar", lambda E, d=dst, b=bk:
                                  E.activation(d, ps[b][0:128, 0:512], AF.Copy),
                                  wait=[last], inc="se")
                    bank_free[bk] = ev
                    qwar[par] = ev
            pump_gamma(16)
            projd = [P.ev("ve"), P.ev("se")]
            if debug is True and g == 0:
                P.dma("sync", dbg["d_k"], KG[:, 0:4 * T], wait=projd)
                P.dma("sync", dbg["d_v"], VG[:, 0:16 * 512], wait=projd)
                P.dma("sync", dbg["d_q"], QG[:, 0:4 * TQ], wait=projd)
                projd = projd + P.dma_fence("sync")
            for h in range(4):
                H = 4 * g + h
                last_z = {0: None, 1: None}
                last_sum = {2: None, 3: None}
                evp = {}
                emk = {}

                def emit_s(p, h=h):
                    s = p if p < 8 else p - 8
                    col0 = 128 * s
                    pieces = ([(col0, 512), (512, 1024)] if col0 < 512
                              else [(col0, 1024)])
                    par = p % 2
                    evp[p] = []
                    for pi, (c0, c1) in enumerate(pieces):
                        sb = 4 + 2 * par + pi
                        wd = c1 - c0
                        emm = P.op("tensor", lambda E, b=sb, h=h, p=p, c0=c0, wd=wd:
                                   E.matmul(ps[b][0:128, 0:wd],
                                            KG[:, h * T + p * 128:
                                               h * T + p * 128 + 128],
                                            QG[:, h * TQ + c0: h * TQ + c0 + wd],
                                            start=True, stop=True),
                                   wait=[bank_free[sb]] + projd, inc="pe")
                        ee = P.op("scalar", lambda E, b=sb, pr=par, c0=c0, wd=wd:
                                  E.activation(PT[pr][:, c0:c0 + wd],
                                               ps[b][0:128, 0:wd],
                                               AF.Exp, scale=ISQ),
                                  wait=[emm], inc="se")
                        bank_free[sb] = ee
                        evp[p].append(ee)
                    if p < 8:
                        emk[p] = P.op("vector", lambda E, pr=par, c0=col0:
                                      E.tensor_tensor(PT[pr][:, c0:c0 + 128],
                                                      PT[pr][:, c0:c0 + 128],
                                                      TRI[:], AluOpType.mult),
                                      wait=[evp[p][0], ev_tri], inc="ve")
                    else:
                        emk[p] = P.op("vector",
                                      lambda E, pr=par, c0=col0, k=p - 8:
                                      E.tensor_scalar(PT[pr][:, c0:c0 + 128],
                                                      PT[pr][:, c0:c0 + 128],
                                                      C_MSELB[:, k:k + 1], None,
                                                      AluOpType.mult),
                                      wait=[evp[p][0], ev_mselb], inc="ve")

                emit_s(0)
                for p in range(16):
                    if p + 1 < 16:
                        emit_s(p + 1)
                    s = p if p < 8 else p - 8
                    col0 = 128 * s
                    pieces = ([(col0, 512), (512, 1024)] if col0 < 512
                              else [(col0, 1024)])
                    par = p % 2
                    for pi, (c0, c1) in enumerate(pieces):
                        wd = c1 - c0
                        zb_ = c0 // 512
                        zo = c0 - 512 * zb_
                        wz = [evp[p][pi], emk[p]]
                        if p == 0:
                            wz += [bank_free[zb_], bank_free[2 + zb_]]
                        ez = P.op("tensor",
                                  lambda E, b=zb_, zo=zo, wd=wd, p=p, pr=par,
                                  c0=c0, h=h:
                                  E.matmul(ps[b][0:128, zo:zo + wd],
                                           VG[:, p * 512 + h * 128:
                                              p * 512 + h * 128 + 128],
                                           PT[pr][:, c0:c0 + wd],
                                           start=(p == 0), stop=(p == 15)),
                                  wait=wz, inc="pe")
                        last_z[zb_] = ez
                        es_ = P.op("tensor",
                                   lambda E, b=2 + zb_, zo=zo, wd=wd, p=p,
                                   pr=par, c0=c0:
                                   E.matmul(ps[b][0:128, zo:zo + wd],
                                            C_ONESM[:],
                                            PT[pr][:, c0:c0 + wd],
                                            start=(p == 0), stop=(p == 15)),
                                   wait=wz, inc="pe")
                        last_sum[2 + zb_] = es_
                e1s = P.op("vector", lambda E: E.tensor_copy(
                    SMS[:, 0:512], ps[2][0:128, 0:512]),
                    wait=[last_sum[2], last_z[0], last_z[1]], inc="ve")
                e2s = P.op("vector", lambda E: E.tensor_copy(
                    SMS[:, 512:1024], ps[3][0:128, 0:512]),
                    wait=[last_sum[3]], inc="ve")
                bank_free[2] = e2s
                bank_free[3] = e2s
                err = P.op("vector", lambda E: E.reciprocal(SMS[:], SMS[:]),
                           inc="ve")
                for c in range(2):
                    ezv = P.op("vector", lambda E, b=c, c=c, H=H:
                               E.tensor_tensor(
                                   ZB[:, H * TQ + c * 512: H * TQ + c * 512 + 512],
                                   ps[b][0:128, 0:512],
                                   SMS[:, c * 512:(c + 1) * 512],
                                   AluOpType.mult),
                               wait=[last_z[c], err], inc="ve")
                    bank_free[c] = ezv
                pump_gamma(18)
        zb_done = P.ev("ve")
        while gamma_pos[0] < len(gamma_jobs):
            pump_gamma(8)
        gamma_done = [P.ev("ve"), P.ev("se")]
        theta_block(0, 176, 176, gamma_done)
        e_th2 = theta_block(1, 352, 176, gamma_done)
        th_done = e_th2
        if debug is True:
            P.dma("sync", dbg["d_z"], ZB[:, 0:NH * TQ], wait=[zb_done])

        AO = buf(B0, BF16, NDB * TQ)
        X2 = buf(B0 + 32 * KB, F32, NDB * TQ)
        WOP = [buf(B0 + 136 * KB + i * 4 * KB, BF16, NDB * 128)
               for i in range(2)]
        AOSQ = buf(MS, BF16, TQ)
        last_st2 = None
        for ob in range(NDB):
            par = ob % 2
            ew = wpanel(WOP[par], wo, ob * 128, 128, (par,))
            for tc in range(2):
                bk = tc
                last = None
                for db in range(NDB):
                    last = P.op("tensor", lambda E, p=par, b=bk, db=db, tc=tc:
                                E.matmul(ps[b][0:128, 0:512],
                                         WOP[p][:, db * 128:(db + 1) * 128],
                                         ZB[:, db * TQ + tc * 512:
                                            db * TQ + tc * 512 + 512],
                                         start=(db == 0), stop=(db == NDB - 1)),
                                wait=[ew, zb_done, bank_free[bk]],
                                inc="pe" if db == NDB - 1 else None)
                dst = AO[:, ob * TQ + tc * 512: ob * TQ + tc * 512 + 512]
                eev = P.op("vector", lambda E, d=dst, b=bk: E.tensor_copy(
                    d, ps[b][0:128, 0:512]), wait=[last], inc="ve")
                bank_free[bk] = eev
                qwar[par] = eev
                esq = P.op("scalar", lambda E, d=dst, tc=tc: E.activation(
                    AOSQ[:, tc * 512:(tc + 1) * 512], d, AF.Square),
                    wait=[eev], inc="se")
                P.op("tensor", lambda E, d=dst, b=4 + tc, ob=ob:
                     E.matmul(ps[b][0:128, 0:512], C_ONESM[:], d,
                              start=(ob == 0), stop=(ob == NDB - 1)),
                     wait=[eev] + ([bank_free[4 + tc]] if ob == 0 else []))
                last_st2 = P.op("tensor", lambda E, b=6 + tc, tc=tc, ob=ob:
                                E.matmul(ps[b][0:128, 0:512], C_ONESM[:],
                                         AOSQ[:, tc * 512:(tc + 1) * 512],
                                         start=(ob == 0), stop=(ob == NDB - 1)),
                                wait=[esq] + ([bank_free[6 + tc]] if ob == 0 else []),
                                inc="pe")
        if debug is True:
            edbg = P.dma("sync", dbg["d_ao"], AO[:, 0:NDB * TQ],
                         wait=[P.ev("se")])
        BCA2 = buf(BC, BF16, TQ)
        BCB2 = buf(BC + 4 * KB, BF16, TQ)
        SCAC = buf(WS2, F32, TQ)
        SCBC = buf(WS2 + 4 * KB, F32, TQ)
        SCCC = buf(MS, F32, TQ)
        ebc2 = ln_bcast(TQ, [(4, 0, 512), (5, 512, 512)],
                        [(6, 0, 512), (7, 512, 512)],
                        SCAC, SCBC, SCCC, BCA2, BCB2, wait=[last_st2])
        XMY = [buf(WS + i * 4 * KB, F32, TQ) for i in range(2)]
        T1C = buf(PBo, F32, TQ)
        T2C = buf(WS2 + 4 * KB, F32, TQ)
        X2B = buf(MS + 2 * KB, BF16, TQ)
        X2SQ = buf(MS + 4 * KB, BF16, TQ)
        xmy_war = [None, None]
        last_st3 = None
        prev_shadow = [None]
        for ob in range(NDB):
            par = ob % 2
            ed = P.dma("sync", XMY[par][:], xt[ob * 128:(ob + 1) * 128, 0:TQ],
                       wait=[xmy_war[par]])
            e1 = P.op("vector", lambda E, ob=ob: E.tensor_tensor(
                T1C[:], AO[:, ob * TQ:(ob + 1) * TQ], BCA2[:], AluOpType.mult),
                wait=[ebc2], inc="ve")
            e2 = P.op("vector", lambda E: E.tensor_tensor(
                T2C[:], T1C[:], BCB2[:], AluOpType.subtract), inc="ve")
            e3 = P.op("vector", lambda E, ob=ob, p=par: E.tensor_tensor(
                X2[:, ob * TQ:(ob + 1) * TQ], T2C[:], XMY[p][:],
                AluOpType.add), wait=[ed], inc="ve")
            xmy_war[par] = e3
            e4 = P.op("vector", lambda E, ob=ob: E.tensor_copy(
                X2B[:], X2[:, ob * TQ:(ob + 1) * TQ]),
                wait=[prev_shadow[0]], inc="ve")
            e5 = P.op("scalar", lambda E: E.activation(X2SQ[:], X2B[:],
                                                       AF.Square),
                      wait=[e4], inc="se")
            for tc in range(2):
                P.op("tensor", lambda E, b=4 + tc, tc=tc, ob=ob:
                     E.matmul(ps[b][0:128, 0:512], C_ONESM[:],
                              X2B[:, tc * 512:(tc + 1) * 512],
                              start=(ob == 0), stop=(ob == NDB - 1)),
                     wait=[e4] + ([bank_free[4 + tc]] if ob == 0 else []))
                last_st3 = P.op("tensor", lambda E, b=6 + tc, tc=tc, ob=ob:
                                E.matmul(ps[b][0:128, 0:512], C_ONESM[:],
                                         X2SQ[:, tc * 512:(tc + 1) * 512],
                                         start=(ob == 0), stop=(ob == NDB - 1)),
                                wait=[e5] + ([bank_free[6 + tc]] if ob == 0 else []),
                                inc="pe")
            prev_shadow[0] = last_st3
            P.dma("sync", x2spill[ob * 128:(ob + 1) * 128, :],
                  X2[:, ob * TQ:(ob + 1) * TQ], wait=[e3])
        x2_done = P.ev("ve")
        x2spill_done = P.dma_fence("sync")
        if debug is True:
            P.dma("sync", dbg["d_x2"].rearrange("(db p) t -> p db t", p=128),
                  X2.rearrange("p (db t) -> p db t", db=NDB), wait=[x2_done])
        BCA3 = buf(BC, BF16, TQ)
        BCB3 = buf(BC + 4 * KB, BF16, TQ)
        ebc3 = ln_bcast(TQ, [(4, 0, 512), (5, 512, 512)],
                        [(6, 0, 512), (7, 512, 512)],
                        SCAC, SCBC, SCCC, BCA3, BCB3, wait=[last_st3])
        XN2 = buf(B0, BF16, NDB * TQ)
        dbg_bar = P.dma_fence("sync") if debug is True else []
        xn2_done = None
        for ob in range(NDB):
            e1 = P.op("vector", lambda E, ob=ob: E.tensor_tensor(
                T1C[:], X2[:, ob * TQ:(ob + 1) * TQ], BCA3[:], AluOpType.mult),
                wait=[ebc3] + dbg_bar, inc="ve")
            xn2_done = P.op("vector", lambda E, ob=ob: E.tensor_tensor(
                XN2[:, ob * TQ:(ob + 1) * TQ], T1C[:], BCB3[:],
                AluOpType.subtract), inc="ve")
        if debug in (True, "late"):
            P.dma("sync", dbg["d_xn2"], XN2[:, 0:NDB * TQ], wait=[xn2_done])
        if debug == "th":
            P.dma("sync", dbg["d_thv"][:, 0:6], buf(4, F32, 6),
                  wait=[th_done])

        if debug:
            for i in range(3):
                P.dma("sync", dbg["d_th"][0:1, i:i + 1], C_TH[2 * i][0:1, 0:1],
                      wait=[th_done])
                P.dma("sync", dbg["d_th"][0:1, 3 + i:4 + i], C_TOT[i][:],
                      wait=[th_done])

        HID = buf(B0 + 32 * KB, BF16, NHB * TQ)
        W1Q = [buf(B0 + 120 * KB + i * 8 * KB, BF16, NDB * 256)
               for i in range(2)]
        W2Q = [buf(B0 + 136 * KB + i * 8 * KB, BF16, NDB * 256)
               for i in range(2)]
        WF = [buf(WS + i * KB, F32, 256) for i in range(4)]
        SG = buf(MS, F32, 512)
        T2P = buf(MS + 2 * KB, BF16, 256)
        W3T = buf(PBo, BF16, 512)
        wf_war = [None] * 4
        sg_war = [None]
        sgn_war = [None]
        hid_done = None
        NPAN = 22
        qdone = {}

        def quant_panel(hp):
            par = hp % 2
            last = None
            for db in range(NDB):
                wpar = 2 * (db % 2)
                ed = P.dma("sync", WF[wpar][:],
                           w1[db * 128:(db + 1) * 128,
                              hp * 256:(hp + 1) * 256],
                           wait=[wf_war[wpar]])
                P.op("scalar", lambda E, p=wpar: E.activation(
                    SGN1[:, 0:256], WF[p][:], AF.Sign, bias=C_TH[1][:, 0:1]),
                    wait=[ed, th_done, sgn_war[0]], inc="se")
                es2 = P.op("scalar", lambda E, p=wpar: E.activation(
                    SGN2[:, 0:256], WF[p][:], AF.Sign, bias=C_TH[0][:, 0:1]),
                    inc="se")
                wf_war[wpar] = es2
                e1q = P.op("vector", lambda E, par=par, db=db: E.tensor_tensor(
                    W1Q[par][:, db * 256:(db + 1) * 256], SGN1[:, 0:256],
                    SGN2[:, 0:256], AluOpType.add), wait=[es2], inc="ve")
                sgn_war[0] = e1q
                wpar2 = 1 + 2 * (db % 2)
                ed2 = P.dma("sync", WF[wpar2][:],
                            w2[db * 128:(db + 1) * 128,
                               hp * 256:(hp + 1) * 256],
                            wait=[wf_war[wpar2]])
                P.op("vector", lambda E, p=wpar2: E.tensor_scalar(
                    T2P[:], WF[p][:], C_TH[3][:, 0:1], -1.0,
                    AluOpType.is_gt, AluOpType.add),
                    wait=[ed2, th_done], inc="ve")
                last = P.op("vector", lambda E, p=wpar2, par=par, db=db:
                            E.scalar_tensor_tensor(
                                W2Q[par][:, db * 256:(db + 1) * 256],
                                WF[p][:], C_TH[2][:, 0:1], T2P[:],
                                AluOpType.is_ge, AluOpType.add),
                            inc="ve")
                wf_war[wpar2] = last
            qdone[hp] = [last, P.ev("se")]

        quant_panel(0)
        for hp in range(NPAN):
            if hp + 1 < NPAN:
                quant_panel(hp + 1)
            par = hp % 2
            for hb in range(2):
                hglob = hp * 2 + hb
                for tc in range(2):
                    b1 = 2 * tc
                    b2 = b1 + 1
                    l1 = l2 = None
                    for db in range(NDB):
                        l1 = P.op("tensor",
                                  lambda E, b=b1, db=db, hb=hb, tc=tc, par=par:
                                  E.matmul(ps[b][0:128, 0:512],
                                           W1Q[par][:, db * 256 + hb * 128:
                                               db * 256 + hb * 128 + 128],
                                           XN2[:, db * TQ + tc * 512:
                                               db * TQ + tc * 512 + 512],
                                           start=(db == 0), stop=(db == NDB - 1)),
                                  wait=[qdone[hp], xn2_done, bank_free[b1]],
                                  inc="pe" if db == NDB - 1 else None)
                        l2 = P.op("tensor",
                                  lambda E, b=b2, db=db, hb=hb, tc=tc, par=par:
                                  E.matmul(ps[b][0:128, 0:512],
                                           W2Q[par][:, db * 256 + hb * 128:
                                               db * 256 + hb * 128 + 128],
                                           XN2[:, db * TQ + tc * 512:
                                               db * TQ + tc * 512 + 512],
                                           start=(db == 0), stop=(db == NDB - 1)),
                                  wait=[bank_free[b2]],
                                  inc="pe" if db == NDB - 1 else None)
                    eg1 = P.op("scalar", lambda E, b=b1: E.activation(
                        SG[:], ps[b][0:128, 0:512], AF.Sigmoid, scale=0.5),
                        wait=[l1, sg_war[0]], inc="se")
                    bank_free[b1] = eg1
                    eh = P.op("vector", lambda E, b=b2, hg=hglob, tc=tc:
                              E.tensor_tensor(
                                  HID[:, hg * TQ + tc * 512:
                                      hg * TQ + tc * 512 + 512],
                                  SG[:], ps[b][0:128, 0:512], AluOpType.mult),
                              wait=[eg1, l2], inc="ve")
                    bank_free[b2] = eh
                    sg_war[0] = eh
                    hid_done = eh
        if debug in (True, "late"):
            P.dma("sync", dbg["d_hid"], HID[:, 0:NHB * TQ], wait=[hid_done])

        FB = buf(B0, BF16, NDB * TQ)
        W3F = buf(B0 + 120 * KB, F32, NHB * 128)
        W3Q = buf(B0 + 142 * KB, BF16, NHB * 128)
        T3E = buf(MS + 2 * KB, BF16, 1408)
        FSQ = buf(MS, BF16, TQ)
        X2S = [buf(WS + i * 4 * KB, F32, TQ) for i in range(2)]
        SCR = [buf(WS2 + i * 4 * KB, F32, TQ) for i in range(2)]
        QTR = 11 * 128
        w3f_war = [None] * 4
        w3q_war = [None]
        last_st4 = None
        q3e = {}
        for ob in range(NDB):
            edq = []
            for qtr in range(4):
                srcq = AP(w3.tensor, ob * 128 + qtr * 11 * 128 * D,
                          [[D, 128], [128 * D, 11], [1, 128]])
                dstq = W3F[:, qtr * QTR:(qtr + 1) * QTR]
                edq.append(P.dma(
                    "sync", dstq.rearrange("p (hb c) -> p hb c", hb=11), srcq,
                    wait=[w3f_war[qtr], hid_done]))
            for qtr in range(4):
                P.op("vector", lambda E, q=qtr: E.tensor_scalar(
                    T3E[:], W3F[:, q * QTR:(q + 1) * QTR], C_TH[5][:, 0:1],
                    -1.0, AluOpType.is_gt, AluOpType.add),
                    wait=[edq[qtr], th3_done, w3q_war[0]], inc="ve")
                ev = P.op("vector", lambda E, q=qtr: E.scalar_tensor_tensor(
                    W3Q[:, q * QTR:(q + 1) * QTR],
                    W3F[:, q * QTR:(q + 1) * QTR], C_TH[4][:, 0:1], T3E[:],
                    AluOpType.is_ge, AluOpType.add), inc="ve")
                q3e[(ob, qtr)] = ev
                w3f_war[qtr] = ev
            for tc in range(2):
                bk = tc
                last = None
                for hb in range(NHB):
                    last = P.op("tensor", lambda E, b=bk, hb=hb, tc=tc, ob=ob:
                                E.matmul(ps[b][0:128, 0:512],
                                         W3Q[:, hb * 128:(hb + 1) * 128],
                                         HID[:, hb * TQ + tc * 512:
                                             hb * TQ + tc * 512 + 512],
                                         start=(hb == 0), stop=(hb == NHB - 1)),
                                wait=[q3e[(ob, min(hb // 11, 3))], hid_done,
                                      bank_free[bk]],
                                inc="pe" if hb == NHB - 1 else None)
                dst = FB[:, ob * TQ + tc * 512: ob * TQ + tc * 512 + 512]
                eev = P.op("scalar", lambda E, d=dst, b=bk: E.activation(
                    d, ps[b][0:128, 0:512], AF.Copy), wait=[last], inc="se")
                bank_free[bk] = eev
                if tc == 1:
                    w3q_war[0] = eev
                esq = P.op("scalar", lambda E, d=dst, tc=tc: E.activation(
                    FSQ[:, tc * 512:(tc + 1) * 512], d, AF.Square),
                    wait=[eev], inc="se")
                P.op("tensor", lambda E, d=dst, b=4 + tc, ob=ob:
                     E.matmul(ps[b][0:128, 0:512], C_ONESM[:], d,
                              start=(ob == 0), stop=(ob == NDB - 1)),
                     wait=[eev] + ([bank_free[4 + tc]] if ob == 0 else []))
                last_st4 = P.op("tensor", lambda E, b=6 + tc, tc=tc, ob=ob:
                                E.matmul(ps[b][0:128, 0:512], C_ONESM[:],
                                         FSQ[:, tc * 512:(tc + 1) * 512],
                                         start=(ob == 0), stop=(ob == NDB - 1)),
                                wait=[esq] + ([bank_free[6 + tc]] if ob == 0 else []),
                                inc="pe")
        if debug in (True, "late"):
            P.dma("sync", dbg["d_f"], FB[:, 0:NDB * TQ], wait=[P.ev("se")])
        BCA4 = buf(BC, BF16, TQ)
        BCB4 = buf(BC + 4 * KB, BF16, TQ)
        SCCE = buf(MS, F32, TQ)
        ebc4 = ln_bcast(TQ, [(4, 0, 512), (5, 512, 512)],
                        [(6, 0, 512), (7, 512, 512)],
                        SCAC, SCBC, SCCE, BCA4, BCB4, wait=[last_st4])
        x2s_war = [None, None]
        scr_war = [None, None]
        out_ev = None
        for ob in range(NDB):
            par = ob % 2
            ed = P.dma("sync", X2S[par][:],
                       x2spill[ob * 128:(ob + 1) * 128, :],
                       wait=[x2s_war[par], hid_done, x2spill_done])
            e1 = P.op("vector", lambda E, ob=ob: E.tensor_tensor(
                T1C[:], FB[:, ob * TQ:(ob + 1) * TQ], BCA4[:], AluOpType.mult),
                wait=[ebc4], inc="ve")
            e2 = P.op("vector", lambda E, p=par: E.tensor_tensor(
                SCR[p][:], T1C[:], BCB4[:], AluOpType.subtract),
                wait=[scr_war[par]], inc="ve")
            e3 = P.op("vector", lambda E, p=par: E.tensor_tensor(
                SCR[p][:], SCR[p][:], X2S[p][:], AluOpType.add),
                wait=[ed], inc="ve")
            x2s_war[par] = e3
            out_ev = P.dma("sync", out[ob * 128:(ob + 1) * 128, :],
                           SCR[par][:], wait=[e3])
            scr_war[par] = out_ev
        P.op("sync", None, wait=P.dma_fence("sync") + P.dma_fence("gpsimd"))

        blk = es.enter_context(nc.Block())
        P.build(blk)
    return nc


_NC_CACHE = {}


def get_nc(debug=False):
    if debug not in _NC_CACHE:
        _NC_CACHE[debug] = build_program(debug)
    return _NC_CACHE[debug]


def host_inputs(L0_attn_x, L0_attn_Wq, L0_attn_Wk, L0_attn_Wv, L0_attn_Wo,
                L0_ffn_w1, L0_ffn_w2, L0_ffn_w3):
    x = np.asarray(L0_attn_x)
    wqT = np.ascontiguousarray(np.asarray(L0_attn_Wq).T)
    wkT = np.ascontiguousarray(np.asarray(L0_attn_Wk).T)
    wvT = np.ascontiguousarray(np.asarray(L0_attn_Wv).T)
    woT = np.ascontiguousarray(np.asarray(L0_attn_Wo).T)
    w1T = np.ascontiguousarray(np.asarray(L0_ffn_w1).T)
    w2T = np.ascontiguousarray(np.asarray(L0_ffn_w2).T)
    w3T = np.ascontiguousarray(np.asarray(L0_ffn_w3).T)
    tri = np.triu(np.ones((128, 128), np.float32))
    msel_h = []
    for mine, peer in ((TILES_A, TILES_B), (TILES_B, TILES_A)):
        m = np.zeros((1, 8), np.float32)
        for k in range(8):
            s_real = sum(1 for i in mine if i < peer[k])
            m[0, k] = 0.0 if s_real > k else 1.0
        msel_h.append(m)
    perms = []
    for mine, peer in ((TILES_A, TILES_B), (TILES_B, TILES_A)):
        cols = []
        for i in mine + peer:
            cols.extend(range(i * 128, (i + 1) * 128))
        perms.append(np.array(cols))
    in_maps = []
    for core in range(8):
        b = core // 2
        half = core % 2
        xtp = np.ascontiguousarray(x[b].T[:, perms[half]])
        in_maps.append({
            "xt": xtp, "wq": wqT, "wk": wkT, "wv": wvT, "wo": woT,
            "w1": w1T, "w2": w2T, "w3": w3T, "tri": tri,
            "msel": msel_h[half],
        })
    return in_maps


def kernel(L0_attn_x, L0_attn_Wq, L0_attn_Wk, L0_attn_Wv, L0_attn_Wo,
           L0_ffn_w1, L0_ffn_w2, L0_ffn_w3):
    in_maps = host_inputs(L0_attn_x, L0_attn_Wq, L0_attn_Wk, L0_attn_Wv,
                          L0_attn_Wo, L0_ffn_w1, L0_ffn_w2, L0_ffn_w3)
    nc = get_nc(False)
    res = run_bass_kernel_spmd(nc, in_maps, core_ids=list(range(8)))
    y = np.empty((4, T, D), np.float32)
    for b in range(4):
        yT = np.empty((D, T), np.float32)
        for half, tiles in ((0, TILES_A), (1, TILES_B)):
            o = np.asarray(res.results[2 * b + half]["out"])
            for a, i in enumerate(tiles):
                yT[:, i * 128:(i + 1) * 128] = o[:, a * 128:(a + 1) * 128]
        y[b] = yT.T
    return y


# version 07
# speedup vs baseline: 2.3062x; 1.1148x over previous
import math
import contextlib
import numpy as np

import concourse.bass as bass
import concourse.mybir as mybir
from concourse.bass_utils import run_bass_kernel_spmd
from concourse.alu_op_type import AluOpType
from concourse.ap import AP

F32 = mybir.dt.float32
BF16 = mybir.dt.bfloat16
AF = mybir.ActivationFunctionType
AX = mybir.AxisListType

D = 2048
T = 2048
TQ = 1024
NDB = 16
NH = 16
HF = 5632
NHB = 44
EPS = 1e-5
QEPS = 1e-8
ISQ = 1.0 / math.sqrt(128.0)
KB = 1024

TILES_A = [0, 3, 4, 7, 8, 11, 12, 15]
TILES_B = [1, 2, 5, 6, 9, 10, 13, 14]

NDMASEM = 8
SEM_ENG = {"pe": "tensor", "se": "scalar", "ve": "vector", "ge": "gpsimd"}
for _q in ("a", "b"):
    for _i in range(NDMASEM):
        SEM_ENG[f"ld{_q}{_i}"] = None
ENGS = ("sync", "gpsimd", "tensor", "scalar", "vector")


class Plan:

    def __init__(self, nc, es):
        self.nc = nc
        self.q = {e: [] for e in ENGS}
        self.sems = {}
        self.cnt = {}
        self.seen = {e: {} for e in ENGS}
        self.dseq = {"a": 0, "b": 0}
        for s in SEM_ENG:
            self.sems[s] = es.enter_context(nc.semaphore(s))
            self.cnt[s] = 0

    def ev(self, sem):
        return (sem, self.cnt[sem])

    def op(self, eng, fn, wait=(), inc=None, inc_by=1):
        waits = []
        flat = []
        for w in wait:
            if isinstance(w, list):
                flat.extend(w)
            else:
                flat.append(w)
        for w in flat:
            if w is None:
                continue
            s, v = w
            if v <= 0:
                continue
            if SEM_ENG[s] == eng:
                continue
            if self.seen[eng].get(s, 0) >= v:
                continue
            self.seen[eng][s] = v
            waits.append((s, v))
        evo = None
        if inc is not None:
            self.cnt[inc] += inc_by
            evo = (inc, self.cnt[inc])
        self.q[eng].append((waits, fn, (inc, inc_by) if inc else None))
        return evo

    def dma(self, eng, out, in_, wait=()):
        qn = "b" if eng == "gpsimd" else "a"
        k = self.dseq[qn] % NDMASEM
        self.dseq[qn] += 1
        sem = f"ld{qn}{k}"
        waits = list(wait) + [(sem, self.cnt[sem])]
        return self.op(eng,
                       lambda E, o=out, i=in_: E.dma_start(out=o, in_=i),
                       waits, inc=sem, inc_by=16)

    def dma_fence(self, eng):
        qn = "b" if eng == "gpsimd" else "a"
        return [(f"ld{qn}{k}", self.cnt[f"ld{qn}{k}"]) for k in range(NDMASEM)]

    def barrier(self):
        return [(s, self.cnt[s]) for s in self.cnt]

    def build(self, blk):
        for eng in ENGS:
            qq = self.q[eng]
            sems = self.sems

            def runner(E, qq=qq):
                for waits, fn, inc in qq:
                    for (s, v) in waits:
                        E.wait_ge(sems[s], v)
                    if fn is None:
                        continue
                    ins = fn(E)
                    if inc is not None:
                        s, k = inc
                        ins.then_inc(sems[s], k)
            getattr(blk, eng)(runner)


def build_program(debug=False):
    nc = bass.Bass("TRN2", target_bir_lowering=False)

    xt = nc.dram_tensor("xt", [D, T], F32, kind="ExternalInput").ap()
    wq = nc.dram_tensor("wq", [D, D], F32, kind="ExternalInput").ap()
    wk = nc.dram_tensor("wk", [D, D], F32, kind="ExternalInput").ap()
    wv = nc.dram_tensor("wv", [D, D], F32, kind="ExternalInput").ap()
    wo = nc.dram_tensor("wo", [D, D], F32, kind="ExternalInput").ap()
    w1 = nc.dram_tensor("w1", [D, HF], F32, kind="ExternalInput").ap()
    w2 = nc.dram_tensor("w2", [D, HF], F32, kind="ExternalInput").ap()
    w3 = nc.dram_tensor("w3", [HF, D], F32, kind="ExternalInput").ap()
    tri = nc.dram_tensor("tri", [128, 128], F32, kind="ExternalInput").ap()
    msel = nc.dram_tensor("msel", [1, 8], F32, kind="ExternalInput").ap()
    out = nc.dram_tensor("out", [D, TQ], F32, kind="ExternalOutput").ap()
    x2spill = nc.dram_tensor("x2spill", [D, TQ], F32, kind="Internal").ap()
    dbg = {}
    if debug == "th":
        dbg["d_th"] = nc.dram_tensor("d_th", [1, 8], F32,
                                     kind="ExternalOutput").ap()
        dbg["d_thv"] = nc.dram_tensor("d_thv", [128, 8], F32,
                                      kind="ExternalOutput").ap()


    elif debug == "late":
        for name, shape, dt in [
            ("d_xn2", [128, NDB * TQ], BF16),
            ("d_hid", [128, NHB * TQ], BF16), ("d_f", [128, NDB * TQ], BF16),
            ("d_th", [1, 8], F32),
        ]:
            dbg[name] = nc.dram_tensor(name, shape, dt, kind="ExternalOutput").ap()
    elif debug:
        for name, shape, dt in [
            ("d_xn1", [D, T], BF16), ("d_k", [128, 4 * T], BF16),
            ("d_v", [128, 16 * 512], BF16), ("d_q", [128, 4 * TQ], BF16),
            ("d_z", [128, NH * TQ], BF16), ("d_ao", [128, NDB * TQ], BF16),
            ("d_x2", [D, TQ], F32), ("d_xn2", [128, NDB * TQ], BF16),
            ("d_hid", [128, NHB * TQ], BF16), ("d_f", [128, NDB * TQ], BF16),
            ("d_th", [1, 8], F32),
        ]:
            dbg[name] = nc.dram_tensor(name, shape, dt, kind="ExternalOutput").ap()

    es = contextlib.ExitStack()
    with es:
        arena = es.enter_context(
            nc.sbuf_tensor("arena", [128, 207 * KB + 256], mybir.dt.uint8))
        ar = arena[:]

        def buf(off_b, dtype, cols, p0=0, p1=128):
            nb = mybir.dt.size(dtype) * cols
            return ar[p0:p1, off_b:off_b + nb].bitcast(dtype)

        C_ONES = buf(0, F32, 1)
        C_TH = [buf(4 + 4 * i, F32, 1) for i in range(6)]
        C_GAC = [buf(28 + 4 * i, F32, 1) for i in range(3)]
        C_GRD = buf(40, F32, 1)
        C_GRD2 = buf(44, F32, 1)
        C_ONESR = buf(64, F32, 128, 0, 1)
        C_ONESRB = buf(576, BF16, 128, 0, 1)
        C_MSELR = buf(832, F32, 8, 0, 1)
        C_MSELB = buf(864, F32, 8)
        C_ONESB = buf(896, BF16, 1)
        C_ONESM = buf(1024, BF16, 128)
        C_EPS = buf(900, F32, 1)
        C_TOT = [buf(904 + 4 * i, F32, 1, 0, 1) for i in range(3)]
        C_EPS4 = buf(940, F32, 1)
        C_THR = [buf(916 + 4 * i, F32, 1, 0, 1) for i in range(3)]
        C_THNR = [buf(928 + 4 * i, F32, 1, 0, 1) for i in range(3)]
        TRI = buf(1536, F32, 128)
        RSUM = buf(2 * KB, F32, 1024, 0, 1)
        RREC = buf(6 * KB, F32, 1024, 0, 1)

        def mkrows(base_b, tmax):
            def rowf(slot, cols):
                return buf(base_b + slot * tmax * 4, F32, cols, 0, 1)
            return rowf

        MS = 10 * KB
        PBo = 16 * KB
        BC = 22 * KB
        WS = 30 * KB
        WS2 = 38 * KB
        B0 = 46 * KB
        GW = 198 * KB
        GDUMP = 202 * KB
        GACALL = 204 * KB

        nc.const_aps.register(F32, EPS, C_EPS[:, 0:1])
        nc.const_aps.register(F32, 4 * EPS, C_EPS4[:, 0:1])

        ps = [es.enter_context(nc.psum_tensor(f"ps{i}", [128, 512], F32))
              for i in range(8)]
        P = Plan(nc, es)
        bank_free = {b: None for b in range(8)}

        P.op("vector", lambda E: E.memset(C_ONES[:], 1.0))
        P.op("vector", lambda E: E.memset(C_ONESR[:], 1.0))
        P.op("vector", lambda E: E.memset(C_ONESRB[:], 1.0))
        P.op("vector", lambda E: E.memset(C_ONESB[:], 1.0))
        P.op("vector", lambda E: E.memset(C_ONESM[:], 1.0))
        P.op("vector", lambda E: E.memset(C_EPS[:], EPS))
        P.op("vector", lambda E: E.memset(C_EPS4[:], 4 * EPS))
        for g_ in C_GAC:
            P.op("vector", lambda E, g=g_: E.memset(g[:], 0.0))
        ev_init_v = P.op("vector", lambda E: E.engine_nop(), inc="ve")
        ev_tri = P.dma("sync", TRI[:], tri)
        ev_msel = P.dma("sync", C_MSELR[:], msel)
        em = P.op("tensor", lambda E: E.matmul(ps[7][0:128, 0:8], C_ONESR[:],
                                               C_MSELR[:], start=True, stop=True),
                  wait=[ev_msel, ev_init_v], inc="pe")
        ev_mselb = P.op("vector",
                        lambda E: E.tensor_copy(C_MSELB[:], ps[7][0:128, 0:8]),
                        wait=[em], inc="ve")
        bank_free[7] = ev_mselb
        init_bar = P.barrier()

        def ln_bcast(Tn, s1_banks, s2_banks, sa, sb_, sc, BCA, BCB,
                     wait=(), eps_val=EPS):
            for bk, o, w in s1_banks:
                e = P.op("vector", lambda E, b=bk, o=o, w=w: E.tensor_scalar(
                    sa[:, o:o + w], ps[b][0:128, 0:w], 1.0 / D, None,
                    AluOpType.mult), wait=wait, inc="ve")
                bank_free[bk] = e
            for bk, o, w in s2_banks:
                e = P.op("vector", lambda E, b=bk, o=o, w=w: E.tensor_scalar(
                    sb_[:, o:o + w], ps[b][0:128, 0:w], 1.0 / D, None,
                    AluOpType.mult), wait=wait, inc="ve")
                bank_free[bk] = e
            P.op("vector", lambda E, Tn=Tn: E.tensor_tensor(
                sc[:, 0:Tn], sa[:, 0:Tn], sa[:, 0:Tn], AluOpType.mult))
            ev = P.op("vector", lambda E, Tn=Tn: E.tensor_tensor(
                sb_[:, 0:Tn], sb_[:, 0:Tn], sc[:, 0:Tn], AluOpType.subtract),
                inc="ve")
            P.op("scalar", lambda E, Tn=Tn, e=eps_val: E.activation(
                sc[:, 0:Tn], sb_[:, 0:Tn], AF.Ln, bias=e), wait=[ev])
            er = P.op("scalar", lambda E, Tn=Tn: E.activation(
                sc[:, 0:Tn], sc[:, 0:Tn], AF.Exp, scale=-0.5), inc="se")
            ea = P.op("vector", lambda E, Tn=Tn: E.tensor_copy(
                BCA[:, 0:Tn], sc[:, 0:Tn]), wait=[er], inc="ve")
            eb = P.op("vector", lambda E, Tn=Tn: E.tensor_tensor(
                BCB[:, 0:Tn], sa[:, 0:Tn], sc[:, 0:Tn], AluOpType.mult),
                inc="ve")
            return eb

        GACC = buf(GACALL, F32, 528)
        gw_par = [0]
        gw_war = [None, None]
        gamma_jobs = []
        for hb in range(NHB):
            for c in range(4):
                gamma_jobs.append((2, w3[hb * 128:(hb + 1) * 128,
                                         c * 512:(c + 1) * 512]))
        for widx, w_ in ((0, w1), (1, w2)):
            for db in range(NDB):
                for hp in range(11):
                    gamma_jobs.append((widx, w_[db * 128:(db + 1) * 128,
                                                hp * 512:(hp + 1) * 512]))
        gamma_pos = [0]
        GDMP = buf(PBo + 4 * KB, BF16, 512)
        SGN1 = buf(GDUMP, BF16, 512)
        SGN2 = buf(GDUMP + 1 * KB, BF16, 512)
        T3P = buf(MS + 3 * KB, BF16, 512)
        W3T = buf(MS + 4 * KB, BF16, 512)

        def pump_gamma(n, eng=None):
            for j_ in range(n):
                i = gamma_pos[0]
                if i >= len(gamma_jobs):
                    return
                widx, src = gamma_jobs[i]
                gamma_pos[0] += 1
                par = gw_par[0] % 2
                gw_par[0] += 1
                gt = buf(GW + par * 2 * KB, F32, 512)
                ed = P.dma("sync", gt[:], src, wait=[gw_war[par]])
                e_ = eng if eng else ("vector" if i % 2 == 0 else "scalar")
                if e_ == "vector":
                    er = P.op("vector", lambda E, g=gt, i=i: E.tensor_reduce(
                        GACC[:, i:i + 1], g[:], AX.X, AluOpType.add,
                        apply_absolute_value=True), wait=[ed], inc="ve")
                else:
                    er = P.op("scalar", lambda E, g=gt, i=i: E.activation(
                        GDMP[:], g[:], AF.Abs, accum_out=GACC[:, i:i + 1]),
                        wait=[ed], inc="se")
                gw_war[par] = er

        XN1 = buf(B0, BF16, NDB * T)
        XB = [buf(B0 + 64 * KB + i * 4 * KB, BF16, T) for i in range(2)]
        XSQ = buf(MS, BF16, T)
        sq_done = [None, None]
        for db in range(NDB):
            par = db % 2
            ed = P.dma("gpsimd", XB[par][:], xt[db * 128:(db + 1) * 128, :],
                       wait=[sq_done[par]] + (init_bar if db < 2 else []))
            esq = P.op("scalar", lambda E, p=par: E.activation(
                XSQ[:], XB[p][:], AF.Square),
                wait=[ed, sq_done[1 - par]], inc="se")
            last = None
            for c in range(4):
                P.op("tensor", lambda E, p=par, c=c, db=db:
                     E.matmul(ps[c][0:128, 0:512], C_ONESM[:],
                              XB[p][:, c * 512:(c + 1) * 512],
                              start=(db == 0), stop=(db == NDB - 1)),
                     wait=[ed] + (init_bar if db == 0 else []))
                last = P.op("tensor", lambda E, c=c, db=db:
                            E.matmul(ps[4 + c][0:128, 0:512], C_ONESM[:],
                                     XSQ[:, c * 512:(c + 1) * 512],
                                     start=(db == 0), stop=(db == NDB - 1)),
                            wait=[esq], inc="pe" if c == 3 else None)
            sq_done[par] = last
            pump_gamma(6)
        stats_done = P.ev("pe")
        BCA1 = buf(BC, BF16, T)
        BCB1 = buf(BC + 4 * KB, BF16, T)
        SCA1 = buf(B0 + 88 * KB, F32, T)
        SCB1 = buf(B0 + 96 * KB, F32, T)
        SCC1 = buf(B0 + 104 * KB, F32, T)
        ebc1 = ln_bcast(T, [(c, c * 512, 512) for c in range(4)],
                        [(4 + c, c * 512, 512) for c in range(4)],
                        SCA1, SCB1, SCC1, BCA1, BCB1, wait=[stats_done])
        XF = [buf(B0 + 72 * KB + i * 8 * KB, F32, T) for i in range(2)]
        T1A = buf(WS, F32, T)
        xf_war = [None, None]
        t1_war = [None]
        xn1_done = None
        for db in range(NDB):
            par = db % 2
            ed = P.dma("sync", XF[par][:], xt[db * 128:(db + 1) * 128, :],
                       wait=[xf_war[par]])
            e1 = P.op("vector", lambda E, p=par: E.tensor_tensor(
                T1A[:], XF[p][:], BCA1[:], AluOpType.mult),
                wait=[ed, ebc1], inc="ve")
            xf_war[par] = e1
            xn1_done = P.op("vector", lambda E, db=db: E.tensor_tensor(
                XN1[:, db * T:(db + 1) * T], T1A[:], BCB1[:],
                AluOpType.subtract), inc="ve")
            pump_gamma(6, eng="scalar")
        if debug is True:
            P.dma("sync", dbg["d_xn1"].rearrange("(db p) t -> p db t", p=128),
                  XN1.rearrange("p (db t) -> p db t", db=NDB),
                  wait=[xn1_done])

        def theta_block(i, c0, n, gate):
            eg = P.op("tensor", lambda E, c0=c0, n=n: E.matmul(
                ps[7][0:1, 0:n], C_ONES[:], GACC[:, c0:c0 + n],
                start=True, stop=True),
                wait=list(gate) + [bank_free[7]], inc="pe")
            ecr = P.op("vector", lambda E, i=i, n=n: E.tensor_reduce(
                C_TOT[i][:], ps[7][0:1, 0:n], AX.X, AluOpType.add),
                wait=[eg], inc="ve")
            bank_free[7] = ecr
            Nw = D * HF
            eth = P.op("gpsimd", lambda E, i=i, N=Nw: E.tensor_scalar(
                C_THR[i][:], C_TOT[i][:], 0.5 / N, 0.5 * QEPS,
                AluOpType.mult, AluOpType.add), wait=[ecr], inc="ge")
            ethn = P.op("gpsimd", lambda E, i=i, N=Nw: E.tensor_scalar(
                C_THNR[i][:], C_TOT[i][:], -0.5 / N, -0.5 * QEPS,
                AluOpType.mult, AluOpType.add), inc="ge")
            eb1 = P.op("tensor", lambda E, i=i: E.matmul(
                ps[7][0:128, 0:1], C_ONESR[:], C_THR[i][:], start=True,
                stop=True), wait=[eth, bank_free[7]], inc="pe")
            ec1 = P.op("vector", lambda E, i=i: E.tensor_copy(
                C_TH[2 * i][:], ps[7][0:128, 0:1]), wait=[eb1], inc="ve")
            bank_free[7] = ec1
            eb2 = P.op("tensor", lambda E, i=i: E.matmul(
                ps[7][0:128, 0:1], C_ONESR[:], C_THNR[i][:], start=True,
                stop=True), wait=[ethn, ec1], inc="ve" if False else "pe")
            ec2 = P.op("vector", lambda E, i=i: E.tensor_copy(
                C_TH[2 * i + 1][:], ps[7][0:128, 0:1]), wait=[eb2], inc="ve")
            bank_free[7] = ec2
            return ec2

        assert gamma_pos[0] >= 176, gamma_pos[0]
        th3_done = theta_block(2, 0, 176, [P.ev("ve"), P.ev("se")])

        KG = buf(B0 + 64 * KB, BF16, 4 * T)
        VG = buf(B0 + 80 * KB, BF16, 16 * 512)
        QG = buf(B0 + 96 * KB, BF16, 4 * TQ)
        ZB = buf(B0 + 104 * KB, BF16, NH * TQ)
        WPQ = [buf(B0 + 136 * KB + i * 4 * KB, BF16, NDB * 128)
               for i in range(4)]
        WPV = [buf(B0 + 136 * KB + i * 8 * KB, BF16, NDB * 256)
               for i in range(2)]
        PT = [buf(PBo + i * 2 * KB, BF16, TQ) for i in range(2)]
        SMS = buf(MS, F32, TQ)
        qwar = {0: None, 1: None, 2: None, 3: None}

        def wpanel(dst, wsrc, c0, width, quarters, extra=()):
            src = AP(wsrc.tensor, c0, [[D, 128], [128 * D, NDB], [1, width]])
            waits = [qwar[q] for q in quarters] + list(extra)
            e = P.dma("gpsimd", dst.rearrange("p (db c) -> p db c", db=NDB),
                      src, wait=waits)
            return e

        for g in range(4):
            for ob in range(4):
                par = ob % 2
                ek = wpanel(WPQ[par], wk, (4 * g + ob) * 128, 128, (par,))
                for tc in range(4):
                    bk = tc % 2
                    last = None
                    for db in range(NDB):
                        last = P.op("tensor", lambda E, p=par, b=bk, db=db, tc=tc:
                                    E.matmul(ps[b][0:128, 0:512],
                                             WPQ[p][:, db * 128:(db + 1) * 128],
                                             XN1[:, db * T + tc * 512:
                                                 db * T + tc * 512 + 512],
                                             start=(db == 0), stop=(db == NDB - 1)),
                                    wait=[ek, xn1_done, bank_free[bk]],
                                    inc="pe" if db == NDB - 1 else None)
                    dst = KG[:, ob * T + tc * 512: ob * T + tc * 512 + 512]
                    if tc % 2 == 0:
                        ev = P.op("vector", lambda E, d=dst, b=bk:
                                  E.tensor_copy(d, ps[b][0:128, 0:512]),
                                  wait=[last], inc="ve")
                    else:
                        ev = P.op("scalar", lambda E, d=dst, b=bk:
                                  E.activation(d, ps[b][0:128, 0:512], AF.Copy),
                                  wait=[last], inc="se")
                    bank_free[bk] = ev
                    qwar[par] = ev
            for oc in range(2):
                par = oc % 2
                evw = wpanel(WPV[par], wv, g * 512 + oc * 256, 256,
                             (2 * par, 2 * par + 1))
                for tb in range(16):
                    bk = 2 + tb % 2
                    last = None
                    for db in range(NDB):
                        last = P.op("tensor", lambda E, p=par, b=bk, db=db, tb=tb:
                                    E.matmul(ps[b][0:128, 0:256],
                                             XN1[:, db * T + tb * 128:
                                                 db * T + tb * 128 + 128],
                                             WPV[p][:, db * 256:(db + 1) * 256],
                                             start=(db == 0), stop=(db == NDB - 1)),
                                    wait=[evw, bank_free[bk]],
                                    inc="pe" if db == NDB - 1 else None)
                    dst = VG[:, tb * 512 + oc * 256: tb * 512 + oc * 256 + 256]
                    if tb % 2 == 0:
                        ev = P.op("vector", lambda E, d=dst, b=bk:
                                  E.tensor_copy(d, ps[b][0:128, 0:256]),
                                  wait=[last], inc="ve")
                    else:
                        ev = P.op("scalar", lambda E, d=dst, b=bk:
                                  E.activation(d, ps[b][0:128, 0:256], AF.Copy),
                                  wait=[last], inc="se")
                    bank_free[bk] = ev
                    qwar[2 * par] = ev
                    qwar[2 * par + 1] = ev
            for ob in range(4):
                par = 2 + ob % 2
                eq = wpanel(WPQ[par], wq, (4 * g + ob) * 128, 128, (par,))
                for tc in range(2):
                    bk = 4 + tc
                    last = None
                    for db in range(NDB):
                        last = P.op("tensor", lambda E, p=par, b=bk, db=db, tc=tc:
                                    E.matmul(ps[b][0:128, 0:512],
                                             WPQ[p][:, db * 128:(db + 1) * 128],
                                             XN1[:, db * T + tc * 512:
                                                 db * T + tc * 512 + 512],
                                             start=(db == 0), stop=(db == NDB - 1)),
                                    wait=[eq, bank_free[bk]],
                                    inc="pe" if db == NDB - 1 else None)
                    dst = QG[:, ob * TQ + tc * 512: ob * TQ + tc * 512 + 512]
                    if tc == 0:
                        ev = P.op("vector", lambda E, d=dst, b=bk:
                                  E.tensor_copy(d, ps[b][0:128, 0:512]),
                                  wait=[last], inc="ve")
                    else:
                        ev = P.op("scalar", lambda E, d=dst, b=bk:
                                  E.activation(d, ps[b][0:128, 0:512], AF.Copy),
                                  wait=[last], inc="se")
                    bank_free[bk] = ev
                    qwar[par] = ev
            pump_gamma(16)
            projd = [P.ev("ve"), P.ev("se")]
            if debug is True and g == 0:
                P.dma("sync", dbg["d_k"], KG[:, 0:4 * T], wait=projd)
                P.dma("sync", dbg["d_v"], VG[:, 0:16 * 512], wait=projd)
                P.dma("sync", dbg["d_q"], QG[:, 0:4 * TQ], wait=projd)
                projd = projd + P.dma_fence("sync")
            for h in range(4):
                H = 4 * g + h
                last_z = {0: None, 1: None}
                last_sum = {2: None, 3: None}
                evp = {}
                emk = {}

                def emit_s(p, h=h):
                    s = p if p < 8 else p - 8
                    col0 = 128 * s
                    pieces = ([(col0, 512), (512, 1024)] if col0 < 512
                              else [(col0, 1024)])
                    par = p % 2
                    evp[p] = []
                    for pi, (c0, c1) in enumerate(pieces):
                        sb = 4 + 2 * par + pi
                        wd = c1 - c0
                        emm = P.op("tensor", lambda E, b=sb, h=h, p=p, c0=c0, wd=wd:
                                   E.matmul(ps[b][0:128, 0:wd],
                                            KG[:, h * T + p * 128:
                                               h * T + p * 128 + 128],
                                            QG[:, h * TQ + c0: h * TQ + c0 + wd],
                                            start=True, stop=True),
                                   wait=[bank_free[sb]] + projd, inc="pe")
                        ee = P.op("scalar", lambda E, b=sb, pr=par, c0=c0, wd=wd:
                                  E.activation(PT[pr][:, c0:c0 + wd],
                                               ps[b][0:128, 0:wd],
                                               AF.Exp, scale=ISQ),
                                  wait=[emm], inc="se")
                        bank_free[sb] = ee
                        evp[p].append(ee)
                    if p < 8:
                        emk[p] = P.op("vector", lambda E, pr=par, c0=col0:
                                      E.tensor_tensor(PT[pr][:, c0:c0 + 128],
                                                      PT[pr][:, c0:c0 + 128],
                                                      TRI[:], AluOpType.mult),
                                      wait=[evp[p][0], ev_tri], inc="ve")
                    else:
                        emk[p] = P.op("vector",
                                      lambda E, pr=par, c0=col0, k=p - 8:
                                      E.tensor_scalar(PT[pr][:, c0:c0 + 128],
                                                      PT[pr][:, c0:c0 + 128],
                                                      C_MSELB[:, k:k + 1], None,
                                                      AluOpType.mult),
                                      wait=[evp[p][0], ev_mselb], inc="ve")

                emit_s(0)
                for p in range(16):
                    if p + 1 < 16:
                        emit_s(p + 1)
                    pump_gamma(1)
                    s = p if p < 8 else p - 8
                    col0 = 128 * s
                    pieces = ([(col0, 512), (512, 1024)] if col0 < 512
                              else [(col0, 1024)])
                    par = p % 2
                    for pi, (c0, c1) in enumerate(pieces):
                        wd = c1 - c0
                        zb_ = c0 // 512
                        zo = c0 - 512 * zb_
                        wz = [evp[p][pi], emk[p]]
                        if p == 0:
                            wz += [bank_free[zb_], bank_free[2 + zb_]]
                        ez = P.op("tensor",
                                  lambda E, b=zb_, zo=zo, wd=wd, p=p, pr=par,
                                  c0=c0, h=h:
                                  E.matmul(ps[b][0:128, zo:zo + wd],
                                           VG[:, p * 512 + h * 128:
                                              p * 512 + h * 128 + 128],
                                           PT[pr][:, c0:c0 + wd],
                                           start=(p == 0), stop=(p == 15)),
                                  wait=wz, inc="pe")
                        last_z[zb_] = ez
                        es_ = P.op("tensor",
                                   lambda E, b=2 + zb_, zo=zo, wd=wd, p=p,
                                   pr=par, c0=c0:
                                   E.matmul(ps[b][0:128, zo:zo + wd],
                                            C_ONESM[:],
                                            PT[pr][:, c0:c0 + wd],
                                            start=(p == 0), stop=(p == 15)),
                                   wait=wz, inc="pe")
                        last_sum[2 + zb_] = es_
                e1s = P.op("vector", lambda E: E.tensor_copy(
                    SMS[:, 0:512], ps[2][0:128, 0:512]),
                    wait=[last_sum[2], last_z[0], last_z[1]], inc="ve")
                e2s = P.op("vector", lambda E: E.tensor_copy(
                    SMS[:, 512:1024], ps[3][0:128, 0:512]),
                    wait=[last_sum[3]], inc="ve")
                bank_free[2] = e2s
                bank_free[3] = e2s
                err = P.op("vector", lambda E: E.reciprocal(SMS[:], SMS[:]),
                           inc="ve")
                for c in range(2):
                    ezv = P.op("vector", lambda E, b=c, c=c, H=H:
                               E.tensor_tensor(
                                   ZB[:, H * TQ + c * 512: H * TQ + c * 512 + 512],
                                   ps[b][0:128, 0:512],
                                   SMS[:, c * 512:(c + 1) * 512],
                                   AluOpType.mult),
                               wait=[last_z[c], err], inc="ve")
                    bank_free[c] = ezv
                pump_gamma(2)
        zb_done = P.ev("ve")
        while gamma_pos[0] < len(gamma_jobs):
            pump_gamma(8)
        gamma_done = [P.ev("ve"), P.ev("se")]
        theta_block(0, 176, 176, gamma_done)
        e_th2 = theta_block(1, 352, 176, gamma_done)
        th_done = e_th2
        if debug is True:
            P.dma("sync", dbg["d_z"], ZB[:, 0:NH * TQ], wait=[zb_done])

        HID = buf(B0 + 32 * KB, BF16, NHB * TQ)
        W1Q = [buf(B0 + 120 * KB + i * 8 * KB, BF16, NDB * 256)
               for i in range(2)]
        W2Q = [buf(B0 + 136 * KB + i * 8 * KB, BF16, NDB * 256)
               for i in range(2)]
        WF = [buf(WS + i * KB, F32, 256) for i in range(4)]
        SG = buf(MS, F32, 512)
        T2P = buf(MS + 2 * KB, BF16, 256)
        W3T = buf(PBo, BF16, 512)
        wf_war = [None] * 4
        sg_war = [None]
        sgn_war = [None]
        hid_done = None
        NPAN = 22
        qdone = {}

        def quant_db(hp, db):
            par = hp % 2
            last = None
            if True:
                wpar = 2 * (db % 2)
                ed = P.dma("sync", WF[wpar][:],
                           w1[db * 128:(db + 1) * 128,
                              hp * 256:(hp + 1) * 256],
                           wait=[wf_war[wpar]])
                P.op("scalar", lambda E, p=wpar: E.activation(
                    SGN1[:, 0:256], WF[p][:], AF.Sign, bias=C_TH[1][:, 0:1]),
                    wait=[ed, th_done, sgn_war[0]], inc="se")
                es2 = P.op("scalar", lambda E, p=wpar: E.activation(
                    SGN2[:, 0:256], WF[p][:], AF.Sign, bias=C_TH[0][:, 0:1]),
                    inc="se")
                wf_war[wpar] = es2
                e1q = P.op("vector", lambda E, par=par, db=db: E.tensor_tensor(
                    W1Q[par][:, db * 256:(db + 1) * 256], SGN1[:, 0:256],
                    SGN2[:, 0:256], AluOpType.add), wait=[es2], inc="ve")
                sgn_war[0] = e1q
                wpar2 = 1 + 2 * (db % 2)
                ed2 = P.dma("sync", WF[wpar2][:],
                            w2[db * 128:(db + 1) * 128,
                               hp * 256:(hp + 1) * 256],
                            wait=[wf_war[wpar2]])
                P.op("vector", lambda E, p=wpar2: E.tensor_scalar(
                    T2P[:], WF[p][:], C_TH[3][:, 0:1], -1.0,
                    AluOpType.is_gt, AluOpType.add),
                    wait=[ed2, th_done], inc="ve")
                last = P.op("vector", lambda E, p=wpar2, par=par, db=db:
                            E.scalar_tensor_tensor(
                                W2Q[par][:, db * 256:(db + 1) * 256],
                                WF[p][:], C_TH[2][:, 0:1], T2P[:],
                                AluOpType.is_ge, AluOpType.add),
                            inc="ve")
                wf_war[wpar2] = last
            qdone[hp] = [last, P.ev("se")]

        def quant_panel(hp):
            for db in range(NDB):
                quant_db(hp, db)

        AO = buf(B0, BF16, NDB * TQ)
        X2 = buf(B0 + 32 * KB, F32, NDB * TQ)
        WOP = [buf(B0 + 136 * KB + i * 4 * KB, BF16, NDB * 128)
               for i in range(2)]
        AOSQ = buf(MS, BF16, TQ)
        last_st2 = None
        for ob in range(NDB):
            par = ob % 2
            ew = wpanel(WOP[par], wo, ob * 128, 128, (par,))
            for tc in range(2):
                bk = tc
                last = None
                for db in range(NDB):
                    last = P.op("tensor", lambda E, p=par, b=bk, db=db, tc=tc:
                                E.matmul(ps[b][0:128, 0:512],
                                         WOP[p][:, db * 128:(db + 1) * 128],
                                         ZB[:, db * TQ + tc * 512:
                                            db * TQ + tc * 512 + 512],
                                         start=(db == 0), stop=(db == NDB - 1)),
                                wait=[ew, zb_done, bank_free[bk]],
                                inc="pe" if db == NDB - 1 else None)
                dst = AO[:, ob * TQ + tc * 512: ob * TQ + tc * 512 + 512]
                eev = P.op("vector", lambda E, d=dst, b=bk: E.tensor_copy(
                    d, ps[b][0:128, 0:512]), wait=[last], inc="ve")
                bank_free[bk] = eev
                qwar[par] = eev
                esq = P.op("scalar", lambda E, d=dst, tc=tc: E.activation(
                    AOSQ[:, tc * 512:(tc + 1) * 512], d, AF.Square),
                    wait=[eev], inc="se")
                P.op("tensor", lambda E, d=dst, b=4 + tc, ob=ob:
                     E.matmul(ps[b][0:128, 0:512], C_ONESM[:], d,
                              start=(ob == 0), stop=(ob == NDB - 1)),
                     wait=[eev] + ([bank_free[4 + tc]] if ob == 0 else []))
                last_st2 = P.op("tensor", lambda E, b=6 + tc, tc=tc, ob=ob:
                                E.matmul(ps[b][0:128, 0:512], C_ONESM[:],
                                         AOSQ[:, tc * 512:(tc + 1) * 512],
                                         start=(ob == 0), stop=(ob == NDB - 1)),
                                wait=[esq] + ([bank_free[6 + tc]] if ob == 0 else []),
                                inc="pe")
        if debug is True:
            edbg = P.dma("sync", dbg["d_ao"], AO[:, 0:NDB * TQ],
                         wait=[P.ev("se")])
        BCA2 = buf(BC, BF16, TQ)
        BCB2 = buf(BC + 4 * KB, BF16, TQ)
        SCAC = buf(WS2, F32, TQ)
        SCBC = buf(WS2 + 4 * KB, F32, TQ)
        SCCC = buf(MS, F32, TQ)
        ebc2 = ln_bcast(TQ, [(4, 0, 512), (5, 512, 512)],
                        [(6, 0, 512), (7, 512, 512)],
                        SCAC, SCBC, SCCC, BCA2, BCB2, wait=[last_st2])
        XMY = [buf(WS + i * 4 * KB, F32, TQ) for i in range(2)]
        T1C = buf(PBo, F32, TQ)
        T2C = buf(WS2 + 4 * KB, F32, TQ)
        X2B = buf(MS + 2 * KB, BF16, TQ)
        X2SQ = buf(MS + 4 * KB, BF16, TQ)
        xmy_war = [None, None]
        last_st3 = None
        prev_shadow = [None]
        for ob in range(NDB):
            par = ob % 2
            ed = P.dma("sync", XMY[par][:], xt[ob * 128:(ob + 1) * 128, 0:TQ],
                       wait=[xmy_war[par]])
            e1 = P.op("vector", lambda E, ob=ob: E.tensor_tensor(
                T1C[:], AO[:, ob * TQ:(ob + 1) * TQ], BCA2[:], AluOpType.mult),
                wait=[ebc2], inc="ve")
            e2 = P.op("vector", lambda E: E.tensor_tensor(
                T2C[:], T1C[:], BCB2[:], AluOpType.subtract), inc="ve")
            e3 = P.op("vector", lambda E, ob=ob, p=par: E.tensor_tensor(
                X2[:, ob * TQ:(ob + 1) * TQ], T2C[:], XMY[p][:],
                AluOpType.add), wait=[ed], inc="ve")
            xmy_war[par] = e3
            e4 = P.op("vector", lambda E, ob=ob: E.tensor_copy(
                X2B[:], X2[:, ob * TQ:(ob + 1) * TQ]),
                wait=[prev_shadow[0]], inc="ve")
            e5 = P.op("scalar", lambda E: E.activation(X2SQ[:], X2B[:],
                                                       AF.Square),
                      wait=[e4], inc="se")
            for tc in range(2):
                P.op("tensor", lambda E, b=4 + tc, tc=tc, ob=ob:
                     E.matmul(ps[b][0:128, 0:512], C_ONESM[:],
                              X2B[:, tc * 512:(tc + 1) * 512],
                              start=(ob == 0), stop=(ob == NDB - 1)),
                     wait=[e4] + ([bank_free[4 + tc]] if ob == 0 else []))
                last_st3 = P.op("tensor", lambda E, b=6 + tc, tc=tc, ob=ob:
                                E.matmul(ps[b][0:128, 0:512], C_ONESM[:],
                                         X2SQ[:, tc * 512:(tc + 1) * 512],
                                         start=(ob == 0), stop=(ob == NDB - 1)),
                                wait=[e5] + ([bank_free[6 + tc]] if ob == 0 else []),
                                inc="pe")
            prev_shadow[0] = last_st3
            P.dma("sync", x2spill[ob * 128:(ob + 1) * 128, :],
                  X2[:, ob * TQ:(ob + 1) * TQ], wait=[e3])
        x2_done = P.ev("ve")
        x2spill_done = P.dma_fence("sync")
        if debug is True:
            P.dma("sync", dbg["d_x2"].rearrange("(db p) t -> p db t", p=128),
                  X2.rearrange("p (db t) -> p db t", db=NDB), wait=[x2_done])
        BCA3 = buf(BC, BF16, TQ)
        BCB3 = buf(BC + 4 * KB, BF16, TQ)
        ebc3 = ln_bcast(TQ, [(4, 0, 512), (5, 512, 512)],
                        [(6, 0, 512), (7, 512, 512)],
                        SCAC, SCBC, SCCC, BCA3, BCB3, wait=[last_st3])
        XN2 = buf(B0, BF16, NDB * TQ)
        dbg_bar = P.dma_fence("sync") if debug is True else []
        xn2_done = None
        for ob in range(NDB):
            e1 = P.op("vector", lambda E, ob=ob: E.tensor_tensor(
                T1C[:], X2[:, ob * TQ:(ob + 1) * TQ], BCA3[:], AluOpType.mult),
                wait=[ebc3] + dbg_bar, inc="ve")
            xn2_done = P.op("vector", lambda E, ob=ob: E.tensor_tensor(
                XN2[:, ob * TQ:(ob + 1) * TQ], T1C[:], BCB3[:],
                AluOpType.subtract), inc="ve")
            quant_db(0, ob)
        if debug in (True, "late"):
            P.dma("sync", dbg["d_xn2"], XN2[:, 0:NDB * TQ], wait=[xn2_done])
        if debug == "th":
            P.dma("sync", dbg["d_thv"][:, 0:6], buf(4, F32, 6),
                  wait=[th_done])

        if debug:
            for i in range(3):
                P.dma("sync", dbg["d_th"][0:1, i:i + 1], C_TH[2 * i][0:1, 0:1],
                      wait=[th_done])
                P.dma("sync", dbg["d_th"][0:1, 3 + i:4 + i], C_TOT[i][:],
                      wait=[th_done])

        for hp in range(NPAN):
            if hp + 1 < NPAN:
                quant_panel(hp + 1)
            par = hp % 2
            for hb in range(2):
                hglob = hp * 2 + hb
                for tc in range(2):
                    b1 = 2 * tc
                    b2 = b1 + 1
                    l1 = l2 = None
                    for db in range(NDB):
                        l1 = P.op("tensor",
                                  lambda E, b=b1, db=db, hb=hb, tc=tc, par=par:
                                  E.matmul(ps[b][0:128, 0:512],
                                           W1Q[par][:, db * 256 + hb * 128:
                                               db * 256 + hb * 128 + 128],
                                           XN2[:, db * TQ + tc * 512:
                                               db * TQ + tc * 512 + 512],
                                           start=(db == 0), stop=(db == NDB - 1)),
                                  wait=[qdone[hp], xn2_done, bank_free[b1]],
                                  inc="pe" if db == NDB - 1 else None)
                        l2 = P.op("tensor",
                                  lambda E, b=b2, db=db, hb=hb, tc=tc, par=par:
                                  E.matmul(ps[b][0:128, 0:512],
                                           W2Q[par][:, db * 256 + hb * 128:
                                               db * 256 + hb * 128 + 128],
                                           XN2[:, db * TQ + tc * 512:
                                               db * TQ + tc * 512 + 512],
                                           start=(db == 0), stop=(db == NDB - 1)),
                                  wait=[bank_free[b2]],
                                  inc="pe" if db == NDB - 1 else None)
                    eg1 = P.op("scalar", lambda E, b=b1: E.activation(
                        SG[:], ps[b][0:128, 0:512], AF.Sigmoid, scale=0.5),
                        wait=[l1, sg_war[0]], inc="se")
                    bank_free[b1] = eg1
                    eh = P.op("vector", lambda E, b=b2, hg=hglob, tc=tc:
                              E.tensor_tensor(
                                  HID[:, hg * TQ + tc * 512:
                                      hg * TQ + tc * 512 + 512],
                                  SG[:], ps[b][0:128, 0:512], AluOpType.mult),
                              wait=[eg1, l2], inc="ve")
                    bank_free[b2] = eh
                    sg_war[0] = eh
                    hid_done = eh
        if debug in (True, "late"):
            P.dma("sync", dbg["d_hid"], HID[:, 0:NHB * TQ], wait=[hid_done])

        FB = buf(B0, BF16, NDB * TQ)
        W3F = buf(B0 + 120 * KB, F32, NHB * 128)
        W3Q = buf(B0 + 142 * KB, BF16, NHB * 128)
        T3E = buf(MS + 2 * KB, BF16, 1408)
        FSQ = buf(MS, BF16, TQ)
        X2S = [buf(WS + i * 4 * KB, F32, TQ) for i in range(2)]
        SCR = [buf(WS2 + i * 4 * KB, F32, TQ) for i in range(2)]
        QTR = 11 * 128
        w3f_war = [None] * 4
        w3q_war = [None]
        last_st4 = None
        q3e = {}
        for ob in range(NDB):
            edq = []
            for qtr in range(4):
                srcq = AP(w3.tensor, ob * 128 + qtr * 11 * 128 * D,
                          [[D, 128], [128 * D, 11], [1, 128]])
                dstq = W3F[:, qtr * QTR:(qtr + 1) * QTR]
                edq.append(P.dma(
                    "sync", dstq.rearrange("p (hb c) -> p hb c", hb=11), srcq,
                    wait=[w3f_war[qtr], hid_done]))
            for qtr in range(4):
                P.op("vector", lambda E, q=qtr: E.tensor_scalar(
                    T3E[:], W3F[:, q * QTR:(q + 1) * QTR], C_TH[5][:, 0:1],
                    -1.0, AluOpType.is_gt, AluOpType.add),
                    wait=[edq[qtr], th3_done, w3q_war[0]], inc="ve")
                ev = P.op("vector", lambda E, q=qtr: E.scalar_tensor_tensor(
                    W3Q[:, q * QTR:(q + 1) * QTR],
                    W3F[:, q * QTR:(q + 1) * QTR], C_TH[4][:, 0:1], T3E[:],
                    AluOpType.is_ge, AluOpType.add), inc="ve")
                q3e[(ob, qtr)] = ev
                w3f_war[qtr] = ev
            for tc in range(2):
                bk = tc
                last = None
                for hb in range(NHB):
                    last = P.op("tensor", lambda E, b=bk, hb=hb, tc=tc, ob=ob:
                                E.matmul(ps[b][0:128, 0:512],
                                         W3Q[:, hb * 128:(hb + 1) * 128],
                                         HID[:, hb * TQ + tc * 512:
                                             hb * TQ + tc * 512 + 512],
                                         start=(hb == 0), stop=(hb == NHB - 1)),
                                wait=[q3e[(ob, min(hb // 11, 3))], hid_done,
                                      bank_free[bk]],
                                inc="pe" if hb == NHB - 1 else None)
                dst = FB[:, ob * TQ + tc * 512: ob * TQ + tc * 512 + 512]
                eev = P.op("scalar", lambda E, d=dst, b=bk: E.activation(
                    d, ps[b][0:128, 0:512], AF.Copy), wait=[last], inc="se")
                bank_free[bk] = eev
                if tc == 1:
                    w3q_war[0] = eev
                esq = P.op("scalar", lambda E, d=dst, tc=tc: E.activation(
                    FSQ[:, tc * 512:(tc + 1) * 512], d, AF.Square),
                    wait=[eev], inc="se")
                P.op("tensor", lambda E, d=dst, b=4 + tc, ob=ob:
                     E.matmul(ps[b][0:128, 0:512], C_ONESM[:], d,
                              start=(ob == 0), stop=(ob == NDB - 1)),
                     wait=[eev] + ([bank_free[4 + tc]] if ob == 0 else []))
                last_st4 = P.op("tensor", lambda E, b=6 + tc, tc=tc, ob=ob:
                                E.matmul(ps[b][0:128, 0:512], C_ONESM[:],
                                         FSQ[:, tc * 512:(tc + 1) * 512],
                                         start=(ob == 0), stop=(ob == NDB - 1)),
                                wait=[esq] + ([bank_free[6 + tc]] if ob == 0 else []),
                                inc="pe")
        if debug in (True, "late"):
            P.dma("sync", dbg["d_f"], FB[:, 0:NDB * TQ], wait=[P.ev("se")])
        BCA4 = buf(BC, BF16, TQ)
        BCB4 = buf(BC + 4 * KB, BF16, TQ)
        SCCE = buf(MS, F32, TQ)
        ebc4 = ln_bcast(TQ, [(4, 0, 512), (5, 512, 512)],
                        [(6, 0, 512), (7, 512, 512)],
                        SCAC, SCBC, SCCE, BCA4, BCB4, wait=[last_st4])
        x2s_war = [None, None]
        scr_war = [None, None]
        out_ev = None
        for ob in range(NDB):
            par = ob % 2
            ed = P.dma("sync", X2S[par][:],
                       x2spill[ob * 128:(ob + 1) * 128, :],
                       wait=[x2s_war[par], hid_done, x2spill_done])
            e1 = P.op("vector", lambda E, ob=ob: E.tensor_tensor(
                T1C[:], FB[:, ob * TQ:(ob + 1) * TQ], BCA4[:], AluOpType.mult),
                wait=[ebc4], inc="ve")
            e2 = P.op("vector", lambda E, p=par: E.tensor_tensor(
                SCR[p][:], T1C[:], BCB4[:], AluOpType.subtract),
                wait=[scr_war[par]], inc="ve")
            e3 = P.op("vector", lambda E, p=par: E.tensor_tensor(
                SCR[p][:], SCR[p][:], X2S[p][:], AluOpType.add),
                wait=[ed], inc="ve")
            x2s_war[par] = e3
            out_ev = P.dma("sync", out[ob * 128:(ob + 1) * 128, :],
                           SCR[par][:], wait=[e3])
            scr_war[par] = out_ev
        P.op("sync", None, wait=P.dma_fence("sync") + P.dma_fence("gpsimd"))

        blk = es.enter_context(nc.Block())
        P.build(blk)
    return nc


_NC_CACHE = {}


def get_nc(debug=False):
    if debug not in _NC_CACHE:
        _NC_CACHE[debug] = build_program(debug)
    return _NC_CACHE[debug]


def host_inputs(L0_attn_x, L0_attn_Wq, L0_attn_Wk, L0_attn_Wv, L0_attn_Wo,
                L0_ffn_w1, L0_ffn_w2, L0_ffn_w3):
    x = np.asarray(L0_attn_x)
    wqT = np.ascontiguousarray(np.asarray(L0_attn_Wq).T)
    wkT = np.ascontiguousarray(np.asarray(L0_attn_Wk).T)
    wvT = np.ascontiguousarray(np.asarray(L0_attn_Wv).T)
    woT = np.ascontiguousarray(np.asarray(L0_attn_Wo).T)
    w1T = np.ascontiguousarray(np.asarray(L0_ffn_w1).T)
    w2T = np.ascontiguousarray(np.asarray(L0_ffn_w2).T)
    w3T = np.ascontiguousarray(np.asarray(L0_ffn_w3).T)
    tri = np.triu(np.ones((128, 128), np.float32))
    msel_h = []
    for mine, peer in ((TILES_A, TILES_B), (TILES_B, TILES_A)):
        m = np.zeros((1, 8), np.float32)
        for k in range(8):
            s_real = sum(1 for i in mine if i < peer[k])
            m[0, k] = 0.0 if s_real > k else 1.0
        msel_h.append(m)
    perms = []
    for mine, peer in ((TILES_A, TILES_B), (TILES_B, TILES_A)):
        cols = []
        for i in mine + peer:
            cols.extend(range(i * 128, (i + 1) * 128))
        perms.append(np.array(cols))
    in_maps = []
    for core in range(8):
        b = core // 2
        half = core % 2
        xtp = np.ascontiguousarray(x[b].T[:, perms[half]])
        in_maps.append({
            "xt": xtp, "wq": wqT, "wk": wkT, "wv": wvT, "wo": woT,
            "w1": w1T, "w2": w2T, "w3": w3T, "tri": tri,
            "msel": msel_h[half],
        })
    return in_maps


def kernel(L0_attn_x, L0_attn_Wq, L0_attn_Wk, L0_attn_Wv, L0_attn_Wo,
           L0_ffn_w1, L0_ffn_w2, L0_ffn_w3):
    in_maps = host_inputs(L0_attn_x, L0_attn_Wq, L0_attn_Wk, L0_attn_Wv,
                          L0_attn_Wo, L0_ffn_w1, L0_ffn_w2, L0_ffn_w3)
    nc = get_nc(False)
    res = run_bass_kernel_spmd(nc, in_maps, core_ids=list(range(8)))
    y = np.empty((4, T, D), np.float32)
    for b in range(4):
        yT = np.empty((D, T), np.float32)
        for half, tiles in ((0, TILES_A), (1, TILES_B)):
            o = np.asarray(res.results[2 * b + half]["out"])
            for a, i in enumerate(tiles):
                yT[:, i * 128:(i + 1) * 128] = o[:, a * 128:(a + 1) * 128]
        y[b] = yT.T
    return y
